# version 08
# baseline (speedup 1.0000x reference)
import math
import numpy as np
import ml_dtypes
import concourse.bass as bass
import concourse.mybir as mybir
from concourse.bass_utils import run_bass_kernel_spmd

F32 = mybir.dt.float32
BF16 = mybir.dt.bfloat16
I32 = mybir.dt.int32
AF = mybir.ActivationFunctionType
ALU = mybir.AluOpType
AX = mybir.AxisListType.X

T = 4096
DM = 1024
NCH = 8
NEG = -1.0e30
S127 = float(np.float32(1.0 / 127.0))

G0 = [0, 3, 4, 7]
G1 = [1, 2, 5, 6]
BND = [8, 16, 24, 32]
ENGS = ("sync", "gpsimd", "tensor", "vector", "scalar")


def _slopes16():
    start = 2.0 ** (-2.0 ** (-(math.log2(16) - 3)))
    return [start * (start ** i) for i in range(16)]


def host_constants():
    sl = np.array(_slopes16(), dtype=np.float64)
    k = np.arange(T, dtype=np.float64)
    alibi = sl[:, None] * (k[None, :] - 2047.0)
    a = alibi.reshape(16, 32, 128).transpose(2, 0, 1).reshape(128, 512)
    a_hi = a.astype(np.float32)
    a_lo = (a - a_hi.astype(np.float64)).astype(np.float32)
    q = np.arange(T, dtype=np.float64)
    aq8 = (8.0 * sl[:, None] * (q[None, :] - 2047.0)).astype(np.float32)
    masks = np.zeros((6, 128, 512), dtype=np.float64)
    kk = np.arange(128)[:, None]
    qq = np.arange(512)[None, :]
    for m in range(4):
        masks[1 + m] = np.where(qq >= 128 * m + kk, 0.0, NEG)
    masks[5] = NEG
    sel2 = np.zeros((2, 128), dtype=np.float32)
    sel2[0, 0:64] = 1.0
    sel2[1, 64:128] = 1.0
    onesh = np.zeros((128, 2), dtype=np.float32)
    onesh[0:64, 0] = 1.0
    onesh[64:128, 1] = 1.0
    return dict(c_alibi_hi=a_hi, c_alibi_lo=a_lo, c_aq8=aq8,
                c_masks=masks.astype(ml_dtypes.bfloat16),
                c_sel2=sel2, c_ident=np.eye(128, dtype=ml_dtypes.bfloat16),
                c_identf=np.eye(128, dtype=np.float32), c_onesh=onesh)


def l1_mask_idx(role, s, j):
    g = (G0 if role == 0 else G1)[s]
    prefix = 4 * (g + 1)
    kt = BND[s] - 8 + j
    if kt < prefix - 4:
        return 0
    if kt < prefix:
        return 1 + (kt - (prefix - 4))
    return 5


class Ctr:
    def __init__(self, sem):
        self.sem = sem
        self.v = 0


def build_program():
    nc = bass.Bass("TRN2", target_bir_lowering=False, debug=False)
    P = {e: [] for e in ENGS}

    def em(eng, fn):
        P[eng].append(fn)

    din = {}

    dinh = {}

    def DI(name, shape, dt=F32):
        h = nc.dram_tensor(name, shape, dt, kind="ExternalInput")
        dinh[name] = h
        din[name] = h.ap()

    DI("x_fm", [NCH, 128, T])
    DI("Wq", [NCH, 128, DM]); DI("Wk", [NCH, 128, DM]); DI("Wv", [NCH, 128, DM])
    DI("Wo", [NCH, 128, DM]); DI("W_T", [NCH, 128, DM])
    DI("bW", [128, 8]); DI("RA", [128, 1088]); DI("Bstk", [128, DM])
    DI("w_attnT", [NCH, 128, 3072]); DI("battn", [128, 24])
    DI("la_attnT", [128, 256]); DI("lb_attnT", [32, 3072])
    DI("w_projT", [NCH, 128, DM]); DI("bproj", [128, 8])
    DI("la_projT", [128, 256]); DI("lb_projT", [32, DM])
    DI("wgT", [NCH, 128, 3072], I32); DI("wuT", [NCH, 128, 3072], I32)
    DI("wdT", [24, 128, DM], I32)
    DI("c_alibi_hi", [128, 512]); DI("c_alibi_lo", [128, 512])
    DI("c_aq8", [16, T]); DI("c_masks", [6, 128, 512], BF16)
    DI("c_sel2", [2, 128]); DI("c_ident", [128, 128], BF16)
    DI("c_identf", [128, 128])
    DI("c_onesh", [128, 2])
    DI("l1masks", [8, 128, 4096], BF16)

    out = nc.dram_tensor("out", [NCH, 128, 2048], F32, kind="ExternalOutput").ap()

    ds = {}

    dsh = {}

    def DS(name, shape, dt):
        h = nc.dram_tensor(name, shape, dt)
        dsh[name] = h
        ds[name] = h.ap()

    DS("w0h", [5, NCH, 128, DM], BF16); DS("w0l", [5, NCH, 128, DM], BF16)
    DS("w1h", [NCH, 128, 4096], BF16); DS("w1l", [NCH, 128, 4096], BF16)
    DS("wgu", [12, NCH, 128, 512], BF16)
    DS("wd", [24, 128, DM], BF16)
    for L in (0, 1):
        DS(f"k{L}h", [NCH, 128, T], BF16); DS(f"k{L}l", [NCH, 128, T], BF16)
        DS(f"v{L}h", [NCH, T, 128], BF16); DS(f"v{L}l", [NCH, T, 128], BF16)
    DS("x1s", [8, 128, 4096], F32)
    DS("hsh", [24, 128, 512], BF16); DS("hsl", [24, 128, 512], BF16)

    sb = {}

    def SB(name, shape, dt):
        sb[name] = nc.alloc_sbuf_tensor("sb_" + name, shape, dt)

    SB("AR1", [128, 4096], F32)
    SB("AR2h", [128, 4096], BF16); SB("AR2l", [128, 4096], BF16)
    SB("AR3h", [128, 4096], BF16); SB("AR3l", [128, 4096], BF16)
    SB("AR4h", [128, 4096], BF16); SB("AR4l", [128, 4096], BF16)
    SB("gA", [128, 4096], F32)
    SB("wsh", [128, 4096], BF16); SB("wsl", [128, 4096], BF16)
    SB("pf", [128, 2048], F32)
    SB("ph", [128, 3072], BF16); SB("pl", [128, 3072], BF16)
    SB("kbh", [128, 1024], BF16); SB("kbl", [128, 1024], BF16)
    SB("vbh", [128, 1024], BF16); SB("vbl", [128, 1024], BF16)
    SB("evh", [128, 512], BF16); SB("evl", [128, 512], BF16)
    SB("btile", [128, 1024], F32)
    SB("rtile", [128, 512], F32)
    SB("otmp", [128, 512], F32)
    SB("tsq", [128, 512], F32)
    SB("sqb", [128, 512], BF16)
    SB("ttile", [128, 512], F32)
    SB("hrow", [128, 512], F32)
    SB("c_alibi", [128, 512], F32)
    SB("c_masksb", [128, 2048], BF16)
    SB("c_onesh", [128, 2], BF16)
    SB("c_ones_col", [128, 1], BF16)
    SB("c_ones_f32", [128, 1], F32)
    SB("c_zero", [128, 1], F32)
    SB("bvb", [64, 512], BF16)
    SB("c_o1b", [64, 128], BF16)
    SB("c_identf", [128, 128], F32)
    SB("rah", [128, 1088], BF16); SB("ral", [128, 1088], BF16)
    SB("bsh", [128, DM], BF16); SB("bsl", [128, DM], BF16)
    SB("lath", [128, 256], BF16); SB("latl", [128, 256], BF16)
    SB("laph", [128, 256], BF16); SB("lapl", [128, 256], BF16)
    SB("bW", [128, 8], F32); SB("battn", [128, 24], F32); SB("bproj", [128, 8], F32)
    SB("ghf", [128, 128], F32)
    SB("ghh", [128, 128], BF16); SB("ghl", [128, 128], BF16)
    SB("gth", [128, 512], BF16); SB("gtl", [128, 512], BF16)
    SB("gb0", [128, 8], F32); SB("gb1", [128, 8], F32); SB("gb2", [128, 8], F32)
    SB("gb3", [128, 8], F32)
    SB("r8", [128, 1], F32); SB("negm", [128, 1], F32)
    SB("STA", [128, 6464], F32)
    SB("LORA", [128, 10240], BF16)
    class View:
        def __init__(self, t, ro, co, nr, ncl):
            self.t, self.ro, self.co, self.nr, self.ncl = t, ro, co, nr, ncl

        def __getitem__(self, idx):
            if not isinstance(idx, tuple):
                idx = (idx, slice(None))
            rs, cs = idx
            r0 = self.ro + (rs.start or 0)
            r1 = self.ro + (self.nr if rs.stop is None else rs.stop)
            c0 = self.co + (cs.start or 0)
            c1 = self.co + (self.ncl if cs.stop is None else cs.stop)
            return self.t[r0:r1, c0:c1]

    def VW(name, base, ro, co, nr, ncl):
        sb[name] = View(sb[base], ro, co, nr, ncl)

    VW("c_aq8", "STA", 0, 0, 16, 512)
    VW("nq2", "STA", 0, 512, 16, 1024)
    VW("bvec", "STA", 0, 1536, 16, 1024)
    VW("nkst2", "STA", 0, 2560, 16, 2)
    VW("nk16", "STA", 0, 2564, 16, 1)
    VW("nkmax", "STA", 0, 2568, 16, 1)
    VW("nktmp", "STA", 0, 2572, 16, 1)
    VW("lnrow", "STA", 0, 2576, 1, 2048)
    VW("bvrow", "STA", 0, 4624, 1, 1024)
    VW("bvrow2", "STA", 0, 4624, 33, 512)
    VW("drows", "STA", 0, 5648, 2, 512)
    VW("c_sel2", "STA", 0, 6160, 2, 128)
    VW("c_ones_row", "STA", 0, 6288, 1, 128)
    VW("lbth", "LORA", 0, 0, 32, 3072)
    VW("lbtl", "LORA", 0, 3072, 32, 3072)
    VW("uh", "LORA", 0, 6144, 32, 1024)
    VW("ul", "LORA", 0, 7168, 32, 1024)
    VW("lbph", "LORA", 0, 8192, 32, 1024)
    VW("lbpl", "LORA", 0, 9216, 32, 1024)
    VW("wbuf", "pf", 0, 0, 128, 2048)
    VW("maskb", "wsl", 0, 0, 128, 4096)
    VW("b1sb", "rtile", 0, 0, 128, 512)
    VW("b2sb", "otmp", 0, 0, 128, 512)
    VW("cw", "gA", 0, 0, 128, 3072)
    VW("urow", "btile", 0, 0, 32, 1024)

    pb = [nc.alloc_psum_tensor(f"pb{i}", [128, 512], F32) for i in range(8)]

    import contextlib
    es = contextlib.ExitStack()
    C = {}
    for name in ("LD", "SP", "MM", "VE", "SC", "OT", "WS", "KV0", "KV1", "CT", "MK",
                 "WSA", "WSB", "SPA", "SPB", "GP"):
        C[name] = Ctr(es.enter_context(nc.semaphore(name)))
    nc._keep = es
    LD, SP, MM, VE, SC, OT = (C[k] for k in ("LD", "SP", "MM", "VE", "SC", "OT"))
    WS, KV0, KV1, CT, MK = (C[k] for k in ("WS", "KV0", "KV1", "CT", "MK"))
    WSA, WSB = C["WSA"], C["WSB"]
    SPA, SPB = C["SPA"], C["SPB"]

    def dma(q, dst, src, ctr, role=None):
        ctr.v += 16
        if role is None:
            em(q, lambda e, d=dst, s=src, sm=ctr.sem: e.dma_start(out=d, in_=s).then_inc(sm, 16))
        else:
            em(q, lambda e, d=dst, s=src, sm=ctr.sem, r=role: e.dma_start(
                out=d, in_=s, cond=(e.partition_id() % 2 == r)).then_inc(sm, 16))
        return ctr.v

    _role_sv = {}

    def dma_ri(q, dst, src_fn, i0, i1, ctr):
        ctr.v += 16
        def go(e, d=dst, f=src_fn, a=i0, b=i1, sm=ctr.sem):
            key = id(e)
            if key not in _role_sv:
                _role_sv[key] = e.partition_id() % 2
            ik = (key, a, b)
            if ik not in _role_sv:
                _role_sv[ik] = a + _role_sv[key] * (b - a)
            e.dma_start(out=d, in_=f(_role_sv[ik])).then_inc(sm, 16)
        em(q, go)
        return ctr.v

    def wt(eng, ctr, val):
        if val > 0:
            em(eng, lambda e, sm=ctr.sem, v=val: e.wait_ge(sm, v))

    def _attach_inc(eng, ctr):
        ctr.v += 1
        last = P[eng].pop()
        P[eng].append(lambda e, f=last, sm=ctr.sem: f(e).then_inc(sm, 1))
        return ctr.v

    def vinc():
        return _attach_inc("vector", VE)

    def vlock():
        cv = _attach_inc("vector", VE)
        wt("vector", VE, cv)
        return cv

    def slock():
        cs = _attach_inc("scalar", SC)
        wt("scalar", SC, cs)
        return cs

    def sinc():
        return _attach_inc("scalar", SC)

    def minc():
        return _attach_inc("tensor", MM)

    def ginc():
        return _attach_inc("gpsimd", C["GP"])

    st = {"wsh_mm": 0, "wsh_sp": 0, "kv_mm": [0, 0], "pf_mm": [0, 0, 0], "pS_sc": [0, 0, 0],
          "kst_ve": 0, "prep_spa": 0, "prep_spb": 0,
          "b12_ve": 0, "prep_par": 0, "prep_lastv": [0, 0], "prep_lastw": [0, 0],
          "mask_used_ve": 0, "bvb_mm": 0, "bvb_ve": 0,
          "drows_ve": 0, "nq2_ld": 0, "nkst_ld": 0, "bvec_sc": 0,
          "kvspill_sp": 0, "x1spill_sp": 0, "gA_free_ve": 0, "ar2_free_mm": 0,
          "ar3_free_mm": 0, "ar4_free_mm": 0, "gt_free_mm": 0, "u_free_mm": 0,
          "ev_sp": 0, "maskb_ve": 0, "hrow_mm": 0}
    CTRS = {"LD": LD, "SP": SP, "MM": MM, "VE": VE, "SC": SC, "OT": OT}
    bank_free = [("VE", 0)] * 8

    def wt_bank(i):
        kind, v = bank_free[i]
        wt("tensor", CTRS[kind], v)

    dma("sync", sb["cw"][:, 0:512], din["c_alibi_lo"], LD)
    dma("sync", sb["c_alibi"][:], din["c_alibi_hi"], LD)
    for mi in range(4):
        dma("sync", sb["c_masksb"][:, mi * 512:mi * 512 + 512], din["c_masks"][mi + 1], LD)
    dma("sync", sb["c_sel2"][:], din["c_sel2"], LD)
    dma("sync", sb["c_identf"][:], din["c_identf"], LD)
    dma("sync", sb["bW"][:], din["bW"], LD)
    dma("sync", sb["battn"][:], din["battn"], LD)
    c0 = dma("sync", sb["bproj"][:], din["bproj"], LD)

    wt("vector", LD, c0)
    for nm, v in (("c_ones_row", 1.0), ("c_ones_col", 1.0), ("c_ones_f32", 1.0),
                  ("c_o1b", 1.0), ("c_zero", 0.0), ("nkst2", 0.0), ("nk16", 0.0)):
        em("vector", lambda e, n=nm, vv=v: e.memset(sb[n][:], vv))
    em("vector", lambda e: e.tensor_tensor(out=sb["c_alibi"][:], in0=sb["c_alibi"][:],
                                           in1=sb["cw"][:, 0:512], op=ALU.add))
    st_cw = {"ve": vinc()}

    def split_const(hname, lname, src, npart, width):
        wt("sync", VE, st_cw["ve"])
        c = dma("sync", sb["cw"][0:npart, 0:width], src, LD)
        wt("vector", LD, c)
        em("vector", lambda e, h=hname, n=npart, w=width: e.tensor_copy(
            out=sb[h][0:n, 0:w], in_=sb["cw"][0:n, 0:w]))
        vlock()
        em("vector", lambda e, h=hname, l=lname, n=npart, w=width: e.tensor_tensor(
            out=sb[l][0:n, 0:w], in0=sb["cw"][0:n, 0:w], in1=sb[h][0:n, 0:w], op=ALU.subtract))
        st_cw["ve"] = vinc()
        return st_cw["ve"]

    split_const("rah", "ral", din["RA"], 128, 1088)
    split_const("bsh", "bsl", din["Bstk"], 128, DM)
    split_const("lath", "latl", din["la_attnT"], 128, 256)
    split_const("lbth", "lbtl", din["lb_attnT"], 32, 3072)
    split_const("laph", "lapl", din["la_projT"], 128, 256)
    split_const("lbph", "lbpl", din["lb_projT"], 32, DM)
    wt("sync", VE, st_cw["ve"])
    c = dma("sync", sb["cw"][:, 0:2], din["c_onesh"], LD)
    wt("vector", LD, c)
    em("vector", lambda e: e.tensor_copy(out=sb["c_onesh"][:], in_=sb["cw"][:, 0:2]))
    st_cw["ve"] = vinc()
    st["gA_free_ve"] = st_cw["ve"]
    st["prep_lastv"][0] = st_cw["ve"]
    st["prep_lastv"][1] = st_cw["ve"]

    def prep_split(dsth, dstl, src, ncols, is_int=False, dst_writes=None):
        for i in range((ncols + 1023) // 1024):
            w0 = i * 1024
            wid = min(1024, ncols - w0)
            par = st["prep_par"]
            st["prep_par"] = 1 - par
            KVp = KV0 if par == 0 else KV1
            wt("sync", VE, st["prep_lastv"][par])
            dstap = sb["wbuf"][:, par * 1024:par * 1024 + wid]
            if is_int:
                dstap = dstap.bitcast(I32)
            cl = dma("sync", dstap, src[:, w0:w0 + wid], KVp)
            wt("vector", KVp, cl)
            wt("vector", SPA if par == 0 else SPB, st["prep_lastw"][par])
            if is_int:
                em("vector", lambda e, p=par, w=wid: e.tensor_copy(
                    out=sb["wsh"][:, p * 1024:p * 1024 + w],
                    in_=sb["wbuf"][:, p * 1024:p * 1024 + w].bitcast(I32)))
            else:
                em("vector", lambda e, p=par, w=wid: e.tensor_copy(
                    out=sb["wsh"][:, p * 1024:p * 1024 + w],
                    in_=sb["wbuf"][:, p * 1024:p * 1024 + w]))
                vlock()
                em("vector", lambda e, p=par, w=wid: e.tensor_tensor(
                    out=sb["wsl"][:, p * 1024:p * 1024 + w],
                    in0=sb["wbuf"][:, p * 1024:p * 1024 + w],
                    in1=sb["wsh"][:, p * 1024:p * 1024 + w], op=ALU.subtract))
            cv = vinc()
            st["prep_lastv"][par] = cv
            wt("gpsimd", VE, cv)
            SPp = SPA if par == 0 else SPB
            cs = 0
            if dst_writes is None:
                dma("gpsimd", dsth[:, w0:w0 + wid], sb["wsh"][:, par * 1024:par * 1024 + wid], SPp)
                cs = dma("gpsimd", dstl[:, w0:w0 + wid], sb["wsl"][:, par * 1024:par * 1024 + wid], SPp)
            else:
                for (dap, c0_, cw_) in dst_writes:
                    lo = max(c0_, w0)
                    hi = min(c0_ + cw_, w0 + wid)
                    if lo < hi:
                        cs = dma("gpsimd", dap[:, lo - c0_:hi - c0_],
                                 sb["wsh"][:, par * 1024 + (lo - w0):par * 1024 + (hi - w0)], SPp)
            st["prep_lastw"][par] = cs

    for j, wn in enumerate(("Wq", "Wk", "Wv", "Wo", "W_T")):
        for c in range(NCH):
            prep_split(ds["w0h"][j][c], ds["w0l"][j][c], din[wn][c], DM)
    for c in range(NCH):
        prep_split(ds["w1h"][c][:, 0:3072], ds["w1l"][c][:, 0:3072], din["w_attnT"][c], 3072)
        prep_split(ds["w1h"][c][:, 3072:4096], ds["w1l"][c][:, 3072:4096], din["w_projT"][c], DM)
    for c in range(NCH):
        prep_split(None, None, din["wgT"][c], 3072, is_int=True,
                   dst_writes=[(ds["wgu"][hcp][c][:, 0:256], hcp * 256, 256) for hcp in range(12)])
        prep_split(None, None, din["wuT"][c], 3072, is_int=True,
                   dst_writes=[(ds["wgu"][hcp][c][:, 256:512], hcp * 256, 256) for hcp in range(12)])
    for hc in range(24):
        prep_split(None, None, din["wdT"][hc], DM, is_int=True,
                   dst_writes=[(ds["wd"][hc], 0, DM)])
    st["wsh_sp"] = SP.v
    st["prep_spa"] = SPA.v
    st["prep_spb"] = SPB.v

    def ln(Ttok, xof, xnof):
        for tt in range(Ttok // 512):
            lastmm = 0
            for c in range(NCH):
                xo = xof(c) + tt * 512
                wt("vector", MM, lastmm)
                em("vector", lambda e, xo=xo: e.tensor_tensor(
                    out=sb["tsq"][:], in0=sb["AR1"][:, xo:xo + 512],
                    in1=sb["AR1"][:, xo:xo + 512], op=ALU.mult))
                cv = vinc()
                wt("tensor", VE, cv)
                if c == 0:
                    wt_bank(0)
                    wt_bank(1)
                em("tensor", lambda e, xo=xo, f=(c == 0), l=(c == NCH - 1): e.matmul(
                    pb[0][0:1, :], sb["c_ones_f32"][:], sb["AR1"][:, xo:xo + 512],
                    start=f, stop=l))
                em("tensor", lambda e, f=(c == 0), l=(c == NCH - 1): e.matmul(
                    pb[1][0:1, :], sb["c_ones_f32"][:], sb["tsq"][:],
                    start=f, stop=l))
                lastmm = minc()
            wt("vector", MM, lastmm)
            em("vector", lambda e: e.tensor_scalar(out=sb["lnrow"][0:1, 0:512], in0=pb[0][0:1, :],
                                                   scalar1=1.0 / 1024.0, scalar2=None, op0=ALU.mult))
            em("vector", lambda e: e.tensor_scalar(out=sb["lnrow"][0:1, 512:1024],
                                                   in0=pb[1][0:1, :],
                                                   scalar1=1.0 / 1024.0, scalar2=None, op0=ALU.mult))
            vlock()
            em("vector", lambda e: e.tensor_tensor(out=sb["lnrow"][0:1, 1536:2048],
                                                   in0=sb["lnrow"][0:1, 0:512],
                                                   in1=sb["lnrow"][0:1, 0:512], op=ALU.mult))
            vlock()
            em("vector", lambda e: e.tensor_tensor(out=sb["lnrow"][0:1, 512:1024],
                                                   in0=sb["lnrow"][0:1, 512:1024],
                                                   in1=sb["lnrow"][0:1, 1536:2048], op=ALU.subtract))
            vlock()
            em("vector", lambda e: e.tensor_scalar(out=sb["lnrow"][0:1, 512:1024],
                                                   in0=sb["lnrow"][0:1, 512:1024],
                                                   scalar1=1e-5, scalar2=None, op0=ALU.add))
            cv = vinc()
            bank_free[0] = ("VE", cv)
            bank_free[1] = ("VE", cv)
            wt("scalar", VE, cv)
            em("scalar", lambda e: e.activation(out=sb["lnrow"][0:1, 1024:1536],
                                                in_=sb["lnrow"][0:1, 512:1024],
                                                func=AF.Sqrt, bias=0.0, scale=1.0))
            cs = sinc()
            wt("vector", SC, cs)
            em("vector", lambda e: e.reciprocal(out=sb["lnrow"][0:1, 1024:1536],
                                                in_=sb["lnrow"][0:1, 1024:1536]))
            vlock()
            em("vector", lambda e: e.tensor_tensor(out=sb["lnrow"][0:1, 1536:2048],
                                                   in0=sb["lnrow"][0:1, 0:512],
                                                   in1=sb["lnrow"][0:1, 1024:1536], op=ALU.mult))
            cv = vinc()
            wt("tensor", VE, cv)
            wt("tensor", VE, st["b12_ve"])
            wt_bank(2)
            wt_bank(3)
            em("tensor", lambda e: e.matmul(pb[2][:], sb["c_ones_row"][:],
                                            sb["lnrow"][0:1, 1024:1536], start=True, stop=True))
            em("tensor", lambda e: e.matmul(pb[3][:], sb["c_ones_row"][:],
                                            sb["lnrow"][0:1, 1536:2048], start=True, stop=True))
            cm = minc()
            wt("scalar", MM, cm)
            em("scalar", lambda e: e.copy(out=sb["b1sb"][:], in_=pb[2][:]))
            em("scalar", lambda e: e.copy(out=sb["b2sb"][:], in_=pb[3][:]))
            cs = sinc()
            bank_free[2] = ("SC", cs)
            bank_free[3] = ("SC", cs)
            wt("vector", SC, cs)
            wt("vector", MM, MM.v)
            for cp in range(NCH // 2):
                for half, tn in ((0, "tsq"), (1, "ttile")):
                    c = 2 * cp + half
                    xo = xof(c) + tt * 512
                    em("vector", lambda e, xo=xo, tn=tn: e.tensor_tensor(
                        out=sb[tn][:], in0=sb["AR1"][:, xo:xo + 512], in1=sb["b1sb"][:],
                        op=ALU.mult))
                vlock()
                for half, tn in ((0, "tsq"), (1, "ttile")):
                    em("vector", lambda e, tn=tn: e.tensor_tensor(
                        out=sb[tn][:], in0=sb[tn][:], in1=sb["b2sb"][:], op=ALU.subtract))
                vlock()
                for half, tn in ((0, "tsq"), (1, "ttile")):
                    c = 2 * cp + half
                    no = xnof(c) + tt * 512
                    em("vector", lambda e, no=no, tn=tn: e.tensor_copy(
                        out=sb["AR2h"][:, no:no + 512], in_=sb[tn][:]))
                vlock()
                for half, tn in ((0, "tsq"), (1, "ttile")):
                    c = 2 * cp + half
                    no = xnof(c) + tt * 512
                    em("vector", lambda e, no=no, tn=tn: e.tensor_tensor(
                        out=sb["AR2l"][:, no:no + 512], in0=sb[tn][:],
                        in1=sb["AR2h"][:, no:no + 512], op=ALU.subtract))
                vlock()
            st["b12_ve"] = vinc()

    def wload(srcs_h, srcs_l, gw):
        wt("sync", MM, st["wsh_mm"])
        wt("sync", MM, st.get("wshp_mm", [0, 0])[0])
        wt("sync", MM, st.get("wshp_mm", [0, 0])[1])
        wt("sync", SP, st["wsh_sp"])
        wt("sync", SPA, st.pop("prep_spa", 0))
        wt("sync", SPB, st.pop("prep_spb", 0))
        wt("sync", VE, st.get("wsl_ve", 0))
        st["wsl_ve"] = 0
        st["wsh_sp"] = 0
        cl = 0
        for dc in range(NCH):
            cl = dma("sync", sb["wsh"][:, dc * 512:dc * 512 + gw], srcs_h[dc], WS)
            if srcs_l is not None:
                cl = dma("sync", sb["wsl"][:, dc * 512:dc * 512 + gw], srcs_l[dc], WS)
        return cl

    def fm_proj(Ttok, wh_fn, wl_fn, oc_list, evac, rhs=("AR2h", "AR2l"),
                rhsof=None, lora2=None):
        ntt = Ttok // 512
        rh, rl = rhs
        groups = [oc_list[k:k + 2] for k in range(0, len(oc_list), 2)]
        wt("sync", SP, st["wsh_sp"])
        wt("sync", SPA, st.pop("prep_spa", 0))
        wt("sync", SPB, st.pop("prep_spb", 0))
        wt("sync", VE, st.get("wsl_ve", 0))
        st["wsl_ve"] = 0
        st["wsh_sp"] = 0
        st.setdefault("wshp_mm", [0, 0])

        def loadg(g):
            par = g % 2
            ws = WSA if par == 0 else WSB
            gw = len(groups[g]) * 128
            c0 = 2 * g * 128
            wt("sync", MM, st["wshp_mm"][par])
            wt("sync", MM, st["wsh_mm"])
            cl = 0
            for dc in range(NCH):
                cl = dma("sync", sb["wsh"][:, par * 2048 + dc * 256:par * 2048 + dc * 256 + gw],
                         wh_fn(dc, c0, gw), ws)
                cl = dma("sync", sb["wsl"][:, par * 2048 + dc * 256:par * 2048 + dc * 256 + gw],
                         wl_fn(dc, c0, gw), ws)
            return cl

        cls = [None] * len(groups)
        cls[0] = loadg(0)
        for g, ocs in enumerate(groups):
            par = g % 2
            if g + 1 < len(groups):
                cls[g + 1] = loadg(g + 1)
            wt("tensor", WSA if par == 0 else WSB, cls[g])
            for i, oc in enumerate(ocs):
                for tt in range(ntt):
                    bk = par * 2 + i * ntt + tt
                    wt_bank(bk)
                    pairs = []
                    for dc in range(NCH):
                        wof = par * 2048 + dc * 256 + i * 128
                        xno = rhsof(dc) + tt * 512
                        pairs.append((("wsh", wof, 128), (rh, xno)))
                        pairs.append((("wsh", wof, 128), (rl, xno)))
                        pairs.append((("wsl", wof, 128), (rh, xno)))
                    if lora2 is not None:
                        pairs.extend(lora2(i, oc, tt))
                    n = len(pairs)
                    for j, ((an, ao, aw), (bn, bo)) in enumerate(pairs):
                        em("tensor", lambda e, an=an, ao=ao, aw=aw, bn=bn, bo=bo, bk=bk,
                           f=(j == 0), l=(j == n - 1): e.matmul(
                            pb[bk][:], sb[an][:, ao:ao + aw], sb[bn][:, bo:bo + 512],
                            start=f, stop=l))
                    cm = minc()
                    st["wshp_mm"][par] = cm
                    evac(bk, oc, tt, cm)

    def evac_split(bk, cm, dsth, dstl, ho, bias_ap):
        wt("vector", MM, cm)
        if bias_ap is None:
            em("vector", lambda e, bk=bk, ho=ho, d=dsth: e.tensor_copy(
                out=sb[d][:, ho:ho + 512], in_=pb[bk][:]))
            vlock()
            em("vector", lambda e, bk=bk, ho=ho, d=dsth, dl=dstl: e.tensor_tensor(
                out=sb[dl][:, ho:ho + 512], in0=pb[bk][:], in1=sb[d][:, ho:ho + 512],
                op=ALU.subtract))
        else:
            em("vector", lambda e, bk=bk, ho=ho, d=dsth, ba=bias_ap: e.tensor_scalar(
                out=sb[d][:, ho:ho + 512], in0=pb[bk][:], scalar1=ba, scalar2=None, op0=ALU.add))
            vlock()
            em("vector", lambda e, bk=bk, ho=ho, d=dsth, dl=dstl, ba=bias_ap: e.scalar_tensor_tensor(
                out=sb[dl][:, ho:ho + 512], in0=pb[bk][:], scalar=ba,
                in1=sb[d][:, ho:ho + 512], op0=ALU.add, op1=ALU.subtract))
        return vinc()

    def stats_sq_mm(bk, hname, ho):
        em("vector", lambda e, h=hname, ho=ho: e.tensor_tensor(
            out=sb["sqb"][:], in0=sb[h][:, ho:ho + 512], in1=sb[h][:, ho:ho + 512], op=ALU.mult))
        cv = vinc()
        wt("tensor", VE, cv)
        em("tensor", lambda e, bk=bk: e.matmul(pb[bk][0:2, :], sb["c_onesh"][:], sb["sqb"][:],
                                               start=True, stop=True))
        return minc()

    def stats_q(bk, c, tt, hname, ho):
        cm2 = stats_sq_mm(bk, hname, ho)
        wt("vector", MM, cm2)
        wt("vector", CT, st.get("otmp_ld", 0))
        em("vector", lambda e, bk=bk: e.tensor_copy(out=sb["otmp"][0:2, :], in_=pb[bk][0:2, :]))
        cv = vinc()
        bank_free[bk] = ("VE", cv)
        wt("sync", VE, cv)
        wt("sync", SC, st["bvec_sc"])
        st["bvec_sc"] = 0
        cd = dma("sync", sb["nq2"][2 * c:2 * c + 2, tt * 512:tt * 512 + 512], sb["otmp"][0:2, :], CT)
        st["nq2_ld"] = max(st["nq2_ld"], cd)
        st["otmp_ld"] = cd

    def stats_k(bk, c, tt, hname, ho):
        cm2 = stats_sq_mm(bk, hname, ho)
        wt("vector", MM, cm2)
        wt("vector", CT, CT.v)
        em("vector", lambda e, bk=bk: e.reduce_max(out=sb["nktmp"][0:2, 0:1],
                                                   in_=pb[bk][0:2, :], axis=AX))
        cv2 = vinc()
        bank_free[bk] = ("VE", cv2)
        wt("sync", VE, cv2)
        cd = dma("sync", sb["nkst2"][2 * c:2 * c + 2, tt:tt + 1], sb["nktmp"][0:2, 0:1], CT)
        st["nkst_ld"] = cd

    def finish_kstats():
        wt("vector", CT, CT.v)
        em("vector", lambda e: e.reduce_max(out=sb["nktmp"][:], in_=sb["nkst2"][:], axis=AX))
        vlock()
        em("vector", lambda e: e.tensor_tensor(out=sb["nk16"][:], in0=sb["nk16"][:],
                                               in1=sb["nktmp"][:], op=ALU.max))
        em("vector", lambda e: e.memset(sb["nkst2"][:], 0.0))
        st["kst_ve"] = vinc()
        return st["kst_ve"]

    def build_bvec(Ttok, q0, use_alibi):
        if use_alibi:
            wt("sync", VE, st.get("kst_ve", 0))
            caq = dma("sync", sb["c_aq8"][0:16, 0:Ttok], din["c_aq8"][0:16, q0:q0 + Ttok], CT)
            wt("vector", CT, caq)
            wt("scalar", CT, caq)
        wt("scalar", CT, CT.v)
        em("scalar", lambda e, w=Ttok: e.activation(out=sb["bvec"][0:16, 0:w],
                                                    in_=sb["nq2"][0:16, 0:w], func=AF.Sqrt,
                                                    bias=0.0, scale=1.0))
        em("scalar", lambda e: e.activation(out=sb["nkmax"][:], in_=sb["nk16"][:],
                                            func=AF.Sqrt, bias=0.0, scale=1.0))
        cs = sinc()
        st["bvec_sc"] = cs
        wt("vector", SC, cs)
        em("vector", lambda e, w=Ttok: e.tensor_scalar(
            out=sb["bvec"][0:16, 0:w], in0=sb["bvec"][0:16, 0:w],
            scalar1=sb["nkmax"][0:16, 0:1], scalar2=1.05, op0=ALU.mult, op1=ALU.mult))
        vlock()
        if use_alibi:
            em("vector", lambda e, w=Ttok: e.tensor_tensor(
                out=sb["bvec"][0:16, 0:w], in0=sb["bvec"][0:16, 0:w],
                in1=sb["c_aq8"][0:16, 0:w], op=ALU.add))
            vlock()
        em("vector", lambda e, w=Ttok: e.tensor_scalar(
            out=sb["bvec"][0:16, 0:w], in0=sb["bvec"][0:16, 0:w],
            scalar1=8.0, scalar2=-1.0, op0=ALU.add, op1=ALU.mult))
        return vinc()

    def attention(L, hp, qts, bvec_ve, o_bias_col):
        wt("gpsimd", SP, st["kvspill_sp"])
        st["kvspill_sp"] = 0
        for bk in range(7):
            wt_bank(bk)
        for i, qd in enumerate(qts):
            for a in range(2):
                idx = i * 2 + a
                h = 2 * hp + a
                wt("sync", VE, bvec_ve)
                wt("sync", VE, st.get("o_ve", 0))
                wt("sync", VE, st["bvb_ve"])
                cd = dma("sync", sb["bvrow2"][idx * 32:idx * 32 + 1, 0:512],
                         sb["bvec"][h:h + 1, qd["bvoff"]:qd["bvoff"] + 512], CT)
                wt("vector", CT, cd)
                wt("vector", MM, st["bvb_mm"])
                em("vector", lambda e, idx=idx: e.tensor_copy(
                    out=sb["bvb"][idx * 32:idx * 32 + 1, 0:512],
                    in_=sb["bvrow2"][idx * 32:idx * 32 + 1, 0:512]))
        st["bvb_ve"] = vinc()
        PMAP = ((0, 1), (2, 3), (5, 7))

        def emit_av(un):
            wt("tensor", VE, un["cvL"])
            p, vo, obk, i = un["p"], un["vo"], un["obk"], un["i"]
            for pi, (vn, pn) in enumerate((("vbh", "ph"), ("vbh", "pl"), ("vbl", "ph"))):
                for a in range(2):
                    em("tensor", lambda e, vn=vn, pn=pn, a=a, vo=vo, p=p, obk=obk,
                       f=(un["fflag"] and pi == 0), l=(un["lflag"] and pi == 2): e.matmul(
                        pb[obk][a * 64:a * 64 + 64, :],
                        sb[vn][:, vo + a * 64:vo + a * 64 + 64],
                        sb[pn][:, p * 1024 + a * 512:p * 1024 + a * 512 + 512],
                        start=f, stop=l, tile_position=(0, a * 64),
                        skip_group_check=True))
            for pi, pn in enumerate(("ph", "pl")):
                for a in range(2):
                    dr = i * 64 + a * 32
                    em("tensor", lambda e, pn=pn, a=a, p=p, dr=dr,
                       f=(un["fflag"] and pi == 0), l=(un["lflag"] and pi == 1): e.matmul(
                        pb[6][dr:dr + 1, :], sb["c_ones_col"][:],
                        sb[pn][:, p * 1024 + a * 512:p * 1024 + a * 512 + 512],
                        start=f, stop=l, tile_position=(0, dr),
                        skip_group_check=True))
            cmA = minc()
            st["pf_mm"][p] = cmA
            st["kv_mm"][un["kp"]] = cmA

        wt("tensor", VE, st["bvb_ve"])
        ktmax = max(qd["kt_lim"] for qd in qts)
        u = 0
        ckv = 0
        for ktg in range((ktmax + 3) // 4):
            kp = ktg % 2
            KVp = KV0 if kp == 0 else KV1
            wt("gpsimd", MM, st["kv_mm"][kp])
            dma("gpsimd", sb["kbh"][:, kp * 512:kp * 512 + 512],
                ds[f"k{L}h"][hp][:, ktg * 512:ktg * 512 + 512], KVp)
            dma("gpsimd", sb["kbl"][:, kp * 512:kp * 512 + 512],
                ds[f"k{L}l"][hp][:, ktg * 512:ktg * 512 + 512], KVp)
            for j4 in range(4):
                dma("gpsimd", sb["vbh"][:, kp * 512 + j4 * 128:kp * 512 + j4 * 128 + 128],
                    ds[f"v{L}h"][hp][ktg * 512 + j4 * 128:ktg * 512 + j4 * 128 + 128, :], KVp)
                ckv = dma("gpsimd", sb["vbl"][:, kp * 512 + j4 * 128:kp * 512 + j4 * 128 + 128],
                          ds[f"v{L}l"][hp][ktg * 512 + j4 * 128:ktg * 512 + j4 * 128 + 128, :], KVp)
            ckvp = kp
            for j in range(4):
                kt = ktg * 4 + j
                if kt >= ktmax:
                    break
                for i, qd in enumerate(qts):
                    if kt >= qd["kt_lim"]:
                        continue
                    p = u % 3
                    wt("tensor", KV0 if ckvp == 0 else KV1, ckv)
                    wt("tensor", SC, st["pS_sc"][p])
                    ko = kp * 512 + j * 128
                    kvp_now = kp
                    for pi, (kn, qn) in enumerate((("kbh", qd["qh"]), ("kbh", qd["ql"]),
                                                   ("kbl", qd["qh"]))):
                        for a in range(2):
                            em("tensor", lambda e, kn=kn, qn=qn, a=a, ko=ko, qc=qd["qcol"],
                               p=p, f=(pi == 0): e.matmul(
                                pb[PMAP[p][a]][:], sb[kn][a * 64:a * 64 + 64, ko:ko + 128],
                                sb[qn][a * 64:a * 64 + 64, qc:qc + 512],
                                start=f, stop=False, tile_position=(a * 64, 0),
                                skip_group_check=True))
                    for a in range(2):
                        em("tensor", lambda e, a=a, p=p, idx=(i * 2 + a): e.matmul(
                            pb[PMAP[p][a]][:], sb["c_o1b"][idx * 32:idx * 32 + 1, :],
                            sb["bvb"][idx * 32:idx * 32 + 1, 0:512],
                            start=False, stop=True, tile_position=(idx * 32, 0),
                            skip_group_check=True))
                    cmS = minc()
                    st["bvb_mm"] = cmS
                    mk = qd["mask_fn"](kt)
                    if mk is not None:
                        wt("vector", MM, cmS)
                        for a in range(2):
                            em("vector", lambda e, a=a, p=p, mn=mk[0], mo=mk[1]: e.tensor_tensor(
                                out=pb[PMAP[p][a]][:], in0=pb[PMAP[p][a]][:],
                                in1=sb[mn][:, mo:mo + 512], op=ALU.add))
                        cvA = vinc()
                        st["mask_used_ve"] = cvA
                        wt("scalar", VE, cvA)
                    else:
                        wt("scalar", MM, cmS)
                    p2 = u % 2
                    wt("scalar", VE, st.setdefault("pf_ve", [0, 0])[p2])
                    for a in range(2):
                        if L == 0:
                            bcol = (2 * hp + a) * 32 + kt
                            bap = sb["c_alibi"][:, bcol:bcol + 1]
                        else:
                            bap = sb["c_zero"][:, 0:1]
                        pof = p2 * 1024 + a * 512
                        em("scalar", lambda e, p=p, a=a, bap=bap, pof=pof: e.activation(
                            out=sb["pf"][:, pof:pof + 512], in_=pb[PMAP[p][a]][:],
                            func=AF.Exp, bias=bap, scale=0.125))
                    csE = sinc()
                    st["pS_sc"][p] = csE
                    wt("gpsimd", SC, csE)
                    wt("gpsimd", MM, st["pf_mm"][p])
                    for a in range(2):
                        pof = p2 * 1024 + a * 512
                        po = p * 1024 + a * 512
                        em("gpsimd", lambda e, po=po, pof=pof: e.tensor_copy(
                            out=sb["ph"][:, po:po + 512], in_=sb["pf"][:, pof:pof + 512]))
                    cgE = ginc()
                    wt("vector", C["GP"], cgE)
                    for a in range(2):
                        pof = p2 * 1024 + a * 512
                        po = p * 1024 + a * 512
                        em("vector", lambda e, po=po, pof=pof: e.tensor_tensor(
                            out=sb["pl"][:, po:po + 512], in0=sb["pf"][:, pof:pof + 512],
                            in1=sb["ph"][:, po:po + 512], op=ALU.subtract))
                    cvL = vinc()
                    st["pf_ve"][p2] = cvL
                    fflag = not qd["_started"]
                    qd["_started"] = True
                    lflag = (kt == qd["kt_lim"] - 1)
                    unit = dict(p=p, vo=kp * 512 + j * 128, obk=4 + i, i=i,
                                fflag=fflag, lflag=lflag, cvL=cvL, kp=kvp_now)
                    pq = st.setdefault("att_pend", [])
                    pq.append(unit)
                    if len(pq) > 2:
                        emit_av(pq.pop(0))
                    u += 1
        for un in st.get("att_pend", []):
            emit_av(un)
        st["att_pend"] = []
        for i, qd in enumerate(qts):
            obk = 4 + i
            wt("vector", MM, max(st["pf_mm"]))
            wt("vector", CT, st.get("otmp_ld", 0))
            em("vector", lambda e, i=i: e.tensor_copy(out=sb["otmp"][i * 64:i * 64 + 1, :],
                                                      in_=pb[6][i * 64:i * 64 + 1, :]))
            em("vector", lambda e, i=i: e.tensor_copy(out=sb["otmp"][i * 64 + 32:i * 64 + 33, :],
                                                      in_=pb[6][i * 64 + 32:i * 64 + 33, :]))
            cvd = vinc()
            wt("sync", VE, cvd)
            wt("sync", VE, st["drows_ve"])
            dma("sync", sb["drows"][0:1, :], sb["otmp"][i * 64:i * 64 + 1, :], CT)
            cd2 = dma("sync", sb["drows"][1:2, :], sb["otmp"][i * 64 + 32:i * 64 + 33, :], CT)
            st["otmp_ld"] = cd2
            wt("vector", CT, cd2)
            em("vector", lambda e: e.reciprocal(out=sb["drows"][:], in_=sb["drows"][:]))
            cv = vinc()
            st["drows_ve"] = cv
            wt("tensor", VE, cv)
            wt_bank(7)
            em("tensor", lambda e: e.matmul(pb[7][:], sb["c_sel2"][:], sb["drows"][:],
                                            start=True, stop=True))
            cm = minc()
            wt("scalar", MM, cm)
            em("scalar", lambda e: e.copy(out=sb["rtile"][:], in_=pb[7][:]))
            cs = sinc()
            bank_free[7] = ("SC", cs)
            wt("vector", SC, cs)
            wt("vector", MM, st["ar4_free_mm"])
            st["ar4_free_mm"] = 0
            em("vector", lambda e, obk=obk: e.tensor_tensor(
                out=sb["otmp"][:], in0=pb[obk][:], in1=sb["rtile"][:], op=ALU.mult))
            vlock()
            oc = qd["ocol"]
            if o_bias_col is not None:
                em("vector", lambda e, oc=oc, bc=o_bias_col: e.tensor_scalar(
                    out=sb["AR4h"][:, oc:oc + 512], in0=sb["otmp"][:],
                    scalar1=sb["battn"][:, bc:bc + 1], scalar2=None, op0=ALU.add))
                vlock()
                em("vector", lambda e, oc=oc, bc=o_bias_col: e.scalar_tensor_tensor(
                    out=sb["AR4l"][:, oc:oc + 512], in0=sb["otmp"][:],
                    scalar=sb["battn"][:, bc:bc + 1], in1=sb["AR4h"][:, oc:oc + 512],
                    op0=ALU.add, op1=ALU.subtract))
            else:
                em("vector", lambda e, oc=oc: e.tensor_copy(
                    out=sb["AR4h"][:, oc:oc + 512], in_=sb["otmp"][:]))
                vlock()
                em("vector", lambda e, oc=oc: e.tensor_tensor(
                    out=sb["AR4l"][:, oc:oc + 512], in0=sb["otmp"][:],
                    in1=sb["AR4h"][:, oc:oc + 512], op=ALU.subtract))
            cv = vinc()
            bank_free[obk] = ("VE", cv)
            bank_free[6] = ("VE", cv)
            st["o_ve"] = cv
            qd["_started"] = False

    def lora_u(Ttok, la_h, la_l, rhs, rhsof):
        rh, rl = rhs
        wt("tensor", VE, st["b12_ve"])
        for tt in range(Ttok // 512):
            bk = 7
            wt_bank(bk)
            pairs = []
            for dc in range(NCH):
                pairs.append(((la_h, dc * 32), (rh, rhsof(dc) + tt * 512)))
                pairs.append(((la_h, dc * 32), (rl, rhsof(dc) + tt * 512)))
                pairs.append(((la_l, dc * 32), (rh, rhsof(dc) + tt * 512)))
            n = len(pairs)
            for j, ((an, ao), (bn, bo)) in enumerate(pairs):
                em("tensor", lambda e, an=an, ao=ao, bn=bn, bo=bo, bk=bk,
                   f=(j == 0), l=(j == n - 1): e.matmul(
                    pb[bk][0:32, :], sb[an][:, ao:ao + 32], sb[bn][:, bo:bo + 512],
                    start=f, stop=l))
            cm = minc()
            wt("scalar", MM, cm)
            wt("scalar", MM, st["u_free_mm"])
            st["u_free_mm"] = 0
            em("scalar", lambda e, tt=tt, bk=bk: e.mul(sb["urow"][0:32, tt * 512:tt * 512 + 512],
                                                       pb[bk][0:32, :], 0.5))
            cs = sinc()
            bank_free[bk] = ("SC", cs)
            wt("vector", SC, cs)
            em("vector", lambda e, tt=tt: e.tensor_copy(
                out=sb["uh"][0:32, tt * 512:tt * 512 + 512],
                in_=sb["urow"][0:32, tt * 512:tt * 512 + 512]))
            vlock()
            em("vector", lambda e, tt=tt: e.tensor_tensor(
                out=sb["ul"][0:32, tt * 512:tt * 512 + 512],
                in0=sb["urow"][0:32, tt * 512:tt * 512 + 512],
                in1=sb["uh"][0:32, tt * 512:tt * 512 + 512], op=ALU.subtract))
        return vinc()

    for blk in range(8):
        t0 = blk * 512
        wt("sync", SP, st["x1spill_sp"])
        st["x1spill_sp"] = 0
        wt("sync", VE, st["b12_ve"])
        if blk == 0:
            wt("sync", VE, st["prep_lastv"][0])
        cx = 0
        for c in range(NCH):
            cx = dma("sync", sb["AR1"][:, c * 512:c * 512 + 512],
                     din["x_fm"][c][:, t0:t0 + 512], LD)
        wt("vector", LD, cx)
        ln(512, lambda c: c * 512, lambda c: c * 512)

        def ev_q(bk, oc, tt, cm):
            cv = evac_split(bk, cm, "AR3h", "AR3l", oc * 512, None)
            bank_free[bk] = ("VE", cv)
            stats_q(bk, oc, 0, "AR3h", oc * 512)

        wt("vector", MM, st["ar3_free_mm"])
        st["ar3_free_mm"] = 0
        wt("tensor", VE, st["b12_ve"])
        fm_proj(512, lambda dc, c0, gw: ds["w0h"][0][dc][:, c0:c0 + gw],
                lambda dc, c0, gw: ds["w0l"][0][dc][:, c0:c0 + gw],
                list(range(8)), ev_q, rhsof=lambda dc: dc * 512)

        def ev_k(bk, oc, tt, cm):
            wt("vector", SP, st["ev_sp"])
            cv = evac_split(bk, cm, "evh", "evl", 0, None)
            wt("gpsimd", VE, cv)
            dma("gpsimd", ds["k0h"][oc][:, t0:t0 + 512], sb["evh"][:], SP)
            cs = dma("gpsimd", ds["k0l"][oc][:, t0:t0 + 512], sb["evl"][:], SP)
            st["ev_sp"] = cs
            st["kvspill_sp"] = cs
            bank_free[bk] = ("VE", cv)
            stats_k(bk, oc, 0, "evh", 0)

        fm_proj(512, lambda dc, c0, gw: ds["w0h"][1][dc][:, c0:c0 + gw],
                lambda dc, c0, gw: ds["w0l"][1][dc][:, c0:c0 + gw],
                list(range(8)), ev_k, rhsof=lambda dc: dc * 512)
        finish_kstats()

        for half in range(2):
            cl = wload([ds["w0h"][2][dc][:, half * 512:half * 512 + 512] for dc in range(NCH)],
                       [ds["w0l"][2][dc][:, half * 512:half * 512 + 512] for dc in range(NCH)], 512)
            wt("tensor", WS, cl)
            for tk in range(4):
                bk = tk % 2
                wt_bank(bk)
                pairs = []
                for dc in range(NCH):
                    xno = dc * 512 + tk * 128
                    pairs.append((("AR2h", xno), ("wsh", dc * 512)))
                    pairs.append((("AR2l", xno), ("wsh", dc * 512)))
                    pairs.append((("AR2h", xno), ("wsl", dc * 512)))
                n = len(pairs)
                for j, ((an, ao), (bn, bo)) in enumerate(pairs):
                    em("tensor", lambda e, an=an, ao=ao, bn=bn, bo=bo, bk=bk,
                       f=(j == 0), l=(j == n - 1): e.matmul(
                        pb[bk][:], sb[an][:, ao:ao + 128], sb[bn][:, bo:bo + 512],
                        start=f, stop=l))
                cm = minc()
                st["wsh_mm"] = cm
                wt("vector", MM, cm)
                wt("vector", SP, st["ev_sp"])
                em("vector", lambda e, bk=bk: e.tensor_copy(out=sb["evh"][:], in_=pb[bk][:]))
                vlock()
                em("vector", lambda e, bk=bk: e.tensor_tensor(
                    out=sb["evl"][:], in0=pb[bk][:], in1=sb["evh"][:], op=ALU.subtract))
                cv = vinc()
                bank_free[bk] = ("VE", cv)
                wt("gpsimd", VE, cv)
                cs = 0
                for c4 in range(4):
                    cpr = half * 4 + c4
                    dma("gpsimd", ds["v0h"][cpr][t0 + tk * 128:t0 + tk * 128 + 128, :],
                        sb["evh"][:, c4 * 128:c4 * 128 + 128], SP)
                    cs = dma("gpsimd", ds["v0l"][cpr][t0 + tk * 128:t0 + tk * 128 + 128, :],
                             sb["evl"][:, c4 * 128:c4 * 128 + 128], SP)
                st["ev_sp"] = cs
                st["kvspill_sp"] = cs

        for tk in range(4):
            bk = 3
            wt_bank(bk)
            pairs = []
            for dc in range(NCH):
                xno = dc * 512 + tk * 128
                pairs.append((("AR2h", xno), ("rah", dc * 136)))
                pairs.append((("AR2l", xno), ("rah", dc * 136)))
                pairs.append((("AR2h", xno), ("ral", dc * 136)))
            n = len(pairs)
            for j, ((an, ao), (bn, bo)) in enumerate(pairs):
                em("tensor", lambda e, an=an, ao=ao, bn=bn, bo=bo, bk=bk,
                   f=(j == 0), l=(j == n - 1): e.matmul(
                    pb[bk][:, 0:136], sb[an][:, ao:ao + 128], sb[bn][:, bo:bo + 136],
                    start=f, stop=l))
            cm = minc()
            wt("vector", MM, cm)
            em("vector", lambda e, bk=bk: e.reduce_max(out=sb["negm"][:], in_=pb[bk][:, 0:8], axis=AX))
            vlock()
            em("vector", lambda e: e.tensor_scalar(out=sb["negm"][:], in0=sb["negm"][:],
                                                   scalar1=-1.0, scalar2=None, op0=ALU.mult))
            cv = vinc()
            wt("scalar", VE, cv)
            em("scalar", lambda e, bk=bk: e.activation(out=sb["gb0"][:], in_=pb[bk][:, 0:8],
                                                       func=AF.Exp, bias=sb["negm"][:, 0:1],
                                                       scale=1.0, accum_out=sb["r8"][:]))
            cs = sinc()
            wt("vector", SC, cs)
            em("vector", lambda e: e.reciprocal(out=sb["r8"][:], in_=sb["r8"][:]))
            cv = vinc()
            wt("vector", VE, cv)
            em("vector", lambda e: e.tensor_scalar(out=sb["gb1"][:], in0=sb["gb0"][:],
                                                   scalar1=sb["r8"][:, 0:1], scalar2=None,
                                                   op0=ALU.mult))
            vlock()
            em("vector", lambda e: e.max(out=sb["gb2"][:], in_=sb["gb1"][:]))
            cv = vinc()
            wt("vector", VE, cv)
            em("vector", lambda e: e.tensor_scalar(out=sb["gb3"][:], in0=sb["gb1"][:],
                                                   scalar1=sb["gb2"][:, 1:2], scalar2=None,
                                                   op0=ALU.is_ge))
            vlock()
            em("vector", lambda e: e.tensor_tensor(out=sb["gb1"][:], in0=sb["gb1"][:],
                                                   in1=sb["gb3"][:], op=ALU.mult))
            vlock()
            em("vector", lambda e: e.reduce_sum(out=sb["r8"][:], in_=sb["gb1"][:], axis=AX))
            vlock()
            em("vector", lambda e: e.tensor_scalar(out=sb["r8"][:], in0=sb["r8"][:],
                                                   scalar1=1e-6, scalar2=None, op0=ALU.add))
            vlock()
            em("vector", lambda e: e.reciprocal(out=sb["r8"][:], in_=sb["r8"][:]))
            cv = vinc()
            wt("vector", VE, cv)
            em("vector", lambda e: e.tensor_scalar(out=sb["gb1"][:], in0=sb["gb1"][:],
                                                   scalar1=sb["r8"][:, 0:1], scalar2=None,
                                                   op0=ALU.mult))
            cv = vinc()
            wt("vector", VE, cv)
            for ee in range(8):
                em("vector", lambda e, bk=bk, ee=ee: e.tensor_scalar(
                    out=sb["ghf"][:, ee * 16:ee * 16 + 16],
                    in0=pb[bk][:, 8 + ee * 16:8 + ee * 16 + 16],
                    scalar1=sb["gb1"][:, ee:ee + 1], scalar2=None, op0=ALU.mult))
            cv = vinc()
            bank_free[bk] = ("VE", cv)
            wt("tensor", VE, cv)
            wt("tensor", MM, st["gt_free_mm"])
            st["gt_free_mm"] = 0
            wt_bank(7)
            em("tensor", lambda e: e.transpose(pb[7][:, 0:128], sb["ghf"][:], sb["c_identf"][:]))
            cm = minc()
            wt("vector", MM, cm)
            em("vector", lambda e, tk=tk: e.tensor_copy(
                out=sb["gth"][:, tk * 128:tk * 128 + 128], in_=pb[7][:, 0:128]))
            vlock()
            em("vector", lambda e, tk=tk: e.tensor_tensor(
                out=sb["gtl"][:, tk * 128:tk * 128 + 128], in0=pb[7][:, 0:128],
                in1=sb["gth"][:, tk * 128:tk * 128 + 128], op=ALU.subtract))
            cv = vinc()
            bank_free[7] = ("VE", cv)
            st["gt_ve"] = cv

        def lora0(i, oc, tt):
            return [(("bsh", oc * 128, 128), ("gth", 0)),
                    (("bsh", oc * 128, 128), ("gtl", 0)),
                    (("bsl", oc * 128, 128), ("gth", 0))]

        def ev_ffn0(bk, oc, tt, cm):
            wt("scalar", MM, cm)
            wt("scalar", VE, st["gA_free_ve"])
            st["gA_free_ve"] = 0
            em("scalar", lambda e, bk=bk, oc=oc: e.activation(
                out=sb["gA"][:, oc * 512:oc * 512 + 512],
                in_=pb[bk][:], func=AF.Silu, bias=sb["bW"][:, oc:oc + 1], scale=1.0))
            cs = sinc()
            bank_free[bk] = ("SC", cs)
            st["gt_free_mm"] = cm

        wt("tensor", VE, st.get("gt_ve", 0))
        fm_proj(512, lambda dc, c0, gw: ds["w0h"][4][dc][:, c0:c0 + gw],
                lambda dc, c0, gw: ds["w0l"][4][dc][:, c0:c0 + gw],
                list(range(8)), ev_ffn0, rhsof=lambda dc: dc * 512, lora2=lora0)

        cbv = build_bvec(512, t0, True)
        st["ar2_free_mm"] = MM.v
        for hp in range(NCH):
            def mk0(kt, g=blk):
                base = 4 * g
                if kt >= base:
                    return ("c_masksb", (kt - base) * 512)
                return None
            qts = [dict(qcol=hp * 512, ocol=hp * 512, bvoff=0, kt_lim=4 * blk + 4,
                        mask_fn=mk0, qh="AR3h", ql="AR3l", _started=False)]
            attention(0, hp, qts, cbv, None)
        st["ar3_free_mm"] = MM.v

        def ev_o(bk, oc, tt, cm):
            wt("vector", MM, cm)
            wt("vector", VE, st.get("res_ve", 0))
            em("vector", lambda e, bk=bk, oc=oc: e.scalar_tensor_tensor(
                out=sb["hrow"][:], in0=pb[bk][:], scalar=0.0,
                in1=sb["gA"][:, oc * 512:oc * 512 + 512],
                op0=ALU.add, op1=ALU.mult))
            vlock()
            em("vector", lambda e, oc=oc: e.tensor_tensor(
                out=sb["AR1"][:, oc * 512:oc * 512 + 512],
                in0=sb["AR1"][:, oc * 512:oc * 512 + 512],
                in1=sb["hrow"][:], op=ALU.add))
            cv = vinc()
            bank_free[bk] = ("VE", cv)
            st["gA_free_ve"] = cv
            st["res_ve"] = cv
            st["b12_ve"] = max(st["b12_ve"], cv)
            wt("gpsimd", VE, cv)
            cs = dma("gpsimd", ds["x1s"][blk][:, oc * 512:oc * 512 + 512],
                     sb["AR1"][:, oc * 512:oc * 512 + 512], SP)
            st["x1spill_sp"] = cs

        wt("tensor", VE, st.get("o_ve", 0))
        fm_proj(512, lambda dc, c0, gw: ds["w0h"][3][dc][:, c0:c0 + gw],
                lambda dc, c0, gw: ds["w0l"][3][dc][:, c0:c0 + gw],
                list(range(8)), ev_o, rhs=("AR4h", "AR4l"), rhsof=lambda dc: dc * 512)
        st["ar4_free_mm"] = MM.v

    em("vector", lambda e: e.memset(sb["nk16"][:], 0.0))
    em("vector", lambda e: e.memset(sb["nkst2"][:], 0.0))
    vinc()
    for blk in range(8):
        t0 = blk * 512
        wt("sync", SP, st["x1spill_sp"])
        wt("sync", VE, st["b12_ve"])
        cx = dma("sync", sb["AR1"][:, 0:4096], ds["x1s"][blk], LD)
        wt("vector", LD, cx)
        ln(512, lambda c: c * 512, lambda c: c * 512)
        cu = lora_u(512, "lath", "latl", ("AR2h", "AR2l"), lambda dc: dc * 512)
        wt("tensor", VE, cu)

        def mk_lora_fm(colbase):
            def f(i, oc, tt):
                col = colbase + oc * 128
                return [(("lbth", col, 128), ("uh", tt * 512)),
                        (("lbth", col, 128), ("ul", tt * 512)),
                        (("lbtl", col, 128), ("uh", tt * 512))]
            return f

        def ev_k1(bk, oc, tt, cm):
            wt("vector", SP, st["ev_sp"])
            cv = evac_split(bk, cm, "evh", "evl", 0, sb["battn"][:, 8 + oc:8 + oc + 1])
            wt("gpsimd", VE, cv)
            dma("gpsimd", ds["k1h"][oc][:, t0:t0 + 512], sb["evh"][:], SP)
            cs = dma("gpsimd", ds["k1l"][oc][:, t0:t0 + 512], sb["evl"][:], SP)
            st["ev_sp"] = cs
            st["kvspill_sp"] = cs
            bank_free[bk] = ("VE", cv)
            stats_k(bk, oc, 0, "evh", 0)

        fm_proj(512, lambda dc, c0, gw: ds["w1h"][dc][:, 1024 + c0:1024 + c0 + gw],
                lambda dc, c0, gw: ds["w1l"][dc][:, 1024 + c0:1024 + c0 + gw],
                list(range(8)), ev_k1, rhsof=lambda dc: dc * 512,
                lora2=mk_lora_fm(1024))
        st["u_free_mm"] = MM.v

        for half in range(2):
            cl = wload([ds["w1h"][dc][:, 2048 + half * 512:2048 + half * 512 + 512]
                        for dc in range(NCH)],
                       [ds["w1l"][dc][:, 2048 + half * 512:2048 + half * 512 + 512]
                        for dc in range(NCH)], 512)
            wt("tensor", WS, cl)
            for tk in range(4):
                bk = tk % 2
                wt_bank(bk)
                pairs = []
                for dc in range(NCH):
                    xno = dc * 512 + tk * 128
                    pairs.append((("AR2h", xno, 128), ("wsh", dc * 512)))
                    pairs.append((("AR2l", xno, 128), ("wsl", dc * 512)))
                    pairs.append((("AR2h", xno, 128), ("wsl", dc * 512)))
                pairs = []
                for dc in range(NCH):
                    xno = dc * 512 + tk * 128
                    pairs.append((("AR2h", xno, 128), ("wsh", dc * 512)))
                    pairs.append((("AR2l", xno, 128), ("wsh", dc * 512)))
                    pairs.append((("AR2h", xno, 128), ("wsl", dc * 512)))
                pairs.append((("uh", tk * 128, 128), ("lbthv", half * 512)))
                pairs.append((("ul", tk * 128, 128), ("lbthv", half * 512)))
                pairs.append((("uh", tk * 128, 128), ("lbtlv", half * 512)))
                n = len(pairs)
                for j, (aa, bb) in enumerate(pairs):
                    an, ao = aa[0], aa[1]
                    bn, bo = bb
                    if bn == "lbthv":
                        bn, bo = "lbth", 2048 + bo
                    if bn == "lbtlv":
                        bn, bo = "lbtl", 2048 + bo
                    em("tensor", lambda e, an=an, ao=ao, bn=bn, bo=bo, bk=bk,
                       f=(j == 0), l=(j == n - 1): e.matmul(
                        pb[bk][:], sb[an][:, ao:ao + 128], sb[bn][:, bo:bo + 512],
                        start=f, stop=l))
                cm = minc()
                st["wsh_mm"] = cm
                st["u_free_mm"] = cm
                wt("vector", MM, cm)
                wt("vector", SP, st["ev_sp"])
                em("vector", lambda e, bk=bk: e.tensor_copy(out=sb["evh"][:], in_=pb[bk][:]))
                vlock()
                em("vector", lambda e, bk=bk: e.tensor_tensor(
                    out=sb["evl"][:], in0=pb[bk][:], in1=sb["evh"][:], op=ALU.subtract))
                cv = vinc()
                bank_free[bk] = ("VE", cv)
                wt("gpsimd", VE, cv)
                cs = 0
                for c4 in range(4):
                    cpr = half * 4 + c4
                    dma("gpsimd", ds["v1h"][cpr][t0 + tk * 128:t0 + tk * 128 + 128, :],
                        sb["evh"][:, c4 * 128:c4 * 128 + 128], SP)
                    cs = dma("gpsimd", ds["v1l"][cpr][t0 + tk * 128:t0 + tk * 128 + 128, :],
                             sb["evl"][:, c4 * 128:c4 * 128 + 128], SP)
                st["ev_sp"] = cs
                st["kvspill_sp"] = cs
        st["ar2_free_mm"] = MM.v
        finish_kstats()

    for s in range(4):
        wt("sync", SP, st["x1spill_sp"])
        st["x1spill_sp"] = 0
        wt("sync", VE, st["b12_ve"])
        cx = dma_ri("sync", sb["AR1"][:, 0:4096],
                    (lambda idx: dsh["x1s"][idx]), G0[s], G1[s], LD)
        wt("vector", LD, cx)
        ln(512, lambda c: c * 512, lambda c: c * 512)
        cu = lora_u(512, "lath", "latl", ("AR2h", "AR2l"), lambda dc: dc * 512)
        wt("tensor", VE, cu)

        def ev_q1(bk, oc, tt, cm):
            cv = evac_split(bk, cm, "AR3h", "AR3l", oc * 512, sb["battn"][:, oc:oc + 1])
            bank_free[bk] = ("VE", cv)
            stats_q(bk, oc, 0, "AR3h", oc * 512)

        wt("vector", MM, st["ar3_free_mm"])
        st["ar3_free_mm"] = 0

        def l1q_lora(i, oc, tt):
            return [(("lbth", oc * 128, 128), ("uh", 0)),
                    (("lbth", oc * 128, 128), ("ul", 0)),
                    (("lbtl", oc * 128, 128), ("uh", 0))]

        fm_proj(512, lambda dc, c0, gw: ds["w1h"][dc][:, c0:c0 + gw],
                lambda dc, c0, gw: ds["w1l"][dc][:, c0:c0 + gw],
                list(range(8)), ev_q1, rhsof=lambda dc: dc * 512, lora2=l1q_lora)
        st["u_free_mm"] = MM.v

        for hcp in range(12):
            cl = wload([ds["wgu"][hcp][dc] for dc in range(NCH)], None, 512)
            wt("tensor", WS, cl)
            for jj in range(2):
                hc = 2 * hcp + jj
                bkg = jj * 2
                bku = jj * 2 + 1
                wt_bank(bkg)
                wt_bank(bku)
                mms = []
                for tgt, bk, co in ((0, bkg, jj * 128), (1, bku, 256 + jj * 128)):
                    pairs = []
                    for dc in range(NCH):
                        pairs.append((("wsh", dc * 512 + co), ("AR2h", dc * 512)))
                        pairs.append((("wsh", dc * 512 + co), ("AR2l", dc * 512)))
                    n = len(pairs)
                    for j2, ((an, ao), (bn, bo)) in enumerate(pairs):
                        em("tensor", lambda e, an=an, ao=ao, bn=bn, bo=bo, bk=bk,
                           f=(j2 == 0), l=(j2 == n - 1): e.matmul(
                            pb[bk][:], sb[an][:, ao:ao + 128], sb[bn][:, bo:bo + 512],
                            start=f, stop=l))
                    mms.append(minc())
                st["wsh_mm"] = mms[-1]
                wt("scalar", MM, mms[0])
                wt("scalar", VE, max(st.get("hrow_ve", 0), st.get("res_ve", 0)))
                wt("scalar", OT, st.get("out_ot", 0))
                em("scalar", lambda e, bkg=bkg: e.activation(
                    out=sb["hrow"][:], in_=pb[bkg][:], func=AF.Silu, bias=0.0, scale=S127))
                cs = sinc()
                bank_free[bkg] = ("SC", cs)
                wt("vector", SC, cs)
                wt("vector", MM, mms[1])
                wt("vector", SP, st["ev_sp"])
                em("vector", lambda e, bku=bku: e.tensor_tensor(
                    out=sb["hrow"][:], in0=sb["hrow"][:], in1=pb[bku][:], op=ALU.mult))
                vlock()
                em("vector", lambda e: e.tensor_copy(out=sb["evh"][:], in_=sb["hrow"][:]))
                vlock()
                em("vector", lambda e: e.tensor_tensor(
                    out=sb["evl"][:], in0=sb["hrow"][:], in1=sb["evh"][:], op=ALU.subtract))
                cv = vinc()
                bank_free[bku] = ("VE", cv)
                st["hrow_ve"] = cv
                wt("gpsimd", VE, cv)
                dma("gpsimd", ds["hsh"][hc], sb["evh"][:], SP)
                cs2 = dma("gpsimd", ds["hsl"][hc], sb["evl"][:], SP)
                st["ev_sp"] = cs2
                st["hspill_sp"] = cs2

        for ocg in range(2):
            for hg in range(3):
                cl = wload([ds["wd"][hg * 8 + ii][:, ocg * 512:ocg * 512 + 512] for ii in range(NCH)],
                           None, 512)
                wt("tensor", WS, cl)
                for ii in range(8):
                    hc = hg * 8 + ii
                    hb = hc % 2
                    KVp = KV0 if hb == 0 else KV1
                    wt("sync", MM, st["kv_mm"][hb])
                    if hc == 0 and ocg == 0:
                        wt("sync", SP, st.get("hspill_sp", 0))
                    dma("sync", sb["kbh"][:, hb * 512:hb * 512 + 512], ds["hsh"][hc], KVp)
                    chl = dma("sync", sb["kbl"][:, hb * 512:hb * 512 + 512], ds["hsl"][hc], KVp)
                    wt("tensor", KVp, chl)
                    for ocf in range(4):
                        bk = 4 + ocf
                        if hc == 0:
                            wt_bank(bk)
                        for pi, hn in enumerate(("kbh", "kbl")):
                            em("tensor", lambda e, hn=hn, hb=hb, ii=ii, ocf=ocf, bk=bk,
                               f=(hc == 0 and pi == 0), l=(hc == 23 and pi == 1): e.matmul(
                                pb[bk][:], sb["wsh"][:, ii * 512 + ocf * 128:ii * 512 + ocf * 128 + 128],
                                sb[hn][:, hb * 512:hb * 512 + 512], start=f, stop=l))
                    cmu = minc()
                    st["kv_mm"][hb] = cmu
                cm = cmu
                st["wsh_mm"] = cm
            for ocf in range(4):
                oc = ocg * 4 + ocf
                bk = 4 + ocf
                wt("scalar", MM, cm)
                wt("scalar", VE, st["gA_free_ve"])
                st["gA_free_ve"] = 0
                em("scalar", lambda e, bk=bk, oc=oc: e.activation(
                    out=sb["gA"][:, oc * 512:oc * 512 + 512], in_=pb[bk][:],
                    func=AF.Silu, bias=0.0, scale=S127 * S127))
                cs = sinc()
                bank_free[bk] = ("SC", cs)

        wt("sync", MM, st["wsh_mm"])
        st["wsh_mm"] = 0
        cm0 = dma_ri("sync", sb["maskb"][:, 0:4096],
                     (lambda idx: dinh["l1masks"][idx]), s, 4 + s, MK)
        st["maskb_ld"] = cm0
        cbv = build_bvec(512, 0, False)
        st["ar2_free_mm"] = MM.v
        wt("vector", MK, st["maskb_ld"])

        for hp in range(NCH):
            def mk1(kt, s=s):
                w0 = BND[s] - 8
                if kt >= w0:
                    return ("maskb", (kt - w0) * 512)
                return None
            qts = [dict(qcol=hp * 512, ocol=hp * 512, bvoff=0, kt_lim=BND[s], mask_fn=mk1,
                        qh="AR3h", ql="AR3l", _started=False)]
            attention(1, hp, qts, cbv, 16 + hp)
        st["ar3_free_mm"] = MM.v
        st["wsl_ve"] = st["mask_used_ve"]

        wt("tensor", VE, st.get("o_ve", 0))
        cu = lora_u(512, "laph", "lapl", ("AR4h", "AR4l"), lambda dc: dc * 512)
        wt("tensor", VE, cu)

        def l1p_lora(i, oc, tt):
            return [(("lbph", oc * 128, 128), ("uh", 0)),
                    (("lbph", oc * 128, 128), ("ul", 0)),
                    (("lbpl", oc * 128, 128), ("uh", 0))]

        def ev_p1(bk, oc, tt, cm):
            wt("vector", MM, cm)
            wt("vector", VE, st.get("res_ve", 0))
            em("vector", lambda e, bk=bk, oc=oc: e.scalar_tensor_tensor(
                out=sb["hrow"][:], in0=pb[bk][:], scalar=sb["bproj"][:, oc:oc + 1],
                in1=sb["gA"][:, oc * 512:oc * 512 + 512], op0=ALU.add, op1=ALU.mult))
            vlock()
            em("vector", lambda e, oc=oc: e.tensor_tensor(
                out=sb["hrow"][:], in0=sb["AR1"][:, oc * 512:oc * 512 + 512],
                in1=sb["hrow"][:], op=ALU.add))
            cv = vinc()
            bank_free[bk] = ("VE", cv)
            st["gA_free_ve"] = cv
            st["res_ve"] = cv
            st["b12_ve"] = max(st["b12_ve"], cv)
            wt("gpsimd", VE, cv)
            dma("gpsimd", out[oc][:, s * 512:s * 512 + 512], sb["hrow"][:], OT)

        def ev_p1_guarded(bk, oc, tt, cm):
            wt("vector", OT, st.get("out_ot", 0))
            ev_p1(bk, oc, tt, cm)
            st["out_ot"] = OT.v

        fm_proj(512, lambda dc, c0, gw: ds["w1h"][dc][:, 3072 + c0:3072 + c0 + gw],
                lambda dc, c0, gw: ds["w1l"][dc][:, 3072 + c0:3072 + c0 + gw],
                list(range(8)), ev_p1_guarded, rhs=("AR4h", "AR4l"),
                rhsof=lambda dc: dc * 512, lora2=l1p_lora)
        st["ar4_free_mm"] = MM.v
        st["u_free_mm"] = MM.v

    wt("gpsimd", OT, OT.v)

    with nc.Block() as block:
        @block.sync
        def _(eng):
            for f in P["sync"]:
                f(eng)

        @block.gpsimd
        def _(eng):
            for f in P["gpsimd"]:
                f(eng)

        @block.tensor
        def _(eng):
            for f in P["tensor"]:
                f(eng)

        @block.vector
        def _(eng):
            for f in P["vector"]:
                f(eng)

        @block.scalar
        def _(eng):
            for f in P["scalar"]:
                f(eng)

    return nc


_NC_CACHE = []


def kernel(**inp):
    x = np.ascontiguousarray(np.asarray(inp["L0_attn_x"], dtype=np.float32))
    f32 = lambda a: np.ascontiguousarray(np.asarray(a, dtype=np.float32))
    i32 = lambda a: np.ascontiguousarray(np.asarray(a, dtype=np.int32))

    A = f32(inp["L0_ffn_A"])
    Bm = f32(inp["L0_ffn_B"])
    R = f32(inp["L0_ffn_R"])
    RA = np.concatenate([R.T, A.transpose(2, 0, 1).reshape(DM, 128)], axis=1)
    RA = np.ascontiguousarray(RA.reshape(NCH, 128, 136).transpose(1, 0, 2).reshape(128, 1088))

    base = dict(
        Wq=f32(inp["L0_attn_Wq"]).reshape(NCH, 128, DM),
        Wk=f32(inp["L0_attn_Wk"]).reshape(NCH, 128, DM),
        Wv=f32(inp["L0_attn_Wv"]).reshape(NCH, 128, DM),
        Wo=f32(inp["L0_attn_Wo"]).reshape(NCH, 128, DM),
        W_T=np.ascontiguousarray(f32(inp["L0_ffn_W"]).T).reshape(NCH, 128, DM),
        bW=np.ascontiguousarray(f32(inp["L0_ffn_b"]).reshape(NCH, 128).T),
        RA=RA,
        Bstk=np.ascontiguousarray(Bm.transpose(0, 2, 1).reshape(128, DM)),
        w_attnT=np.ascontiguousarray(f32(inp["L1_attn_w_attn"]).T).reshape(NCH, 128, 3072),
        battn=np.ascontiguousarray(f32(inp["L1_attn_b_attn"]).reshape(24, 128).T),
        la_attnT=np.ascontiguousarray(
            f32(inp["L1_attn_la_attn"]).T.reshape(NCH, 128, 32).transpose(1, 0, 2).reshape(128, 256)),
        lb_attnT=np.ascontiguousarray(f32(inp["L1_attn_lb_attn"]).T),
        w_projT=np.ascontiguousarray(f32(inp["L1_attn_w_proj"]).T).reshape(NCH, 128, DM),
        bproj=np.ascontiguousarray(f32(inp["L1_attn_b_proj"]).reshape(NCH, 128).T),
        la_projT=np.ascontiguousarray(
            f32(inp["L1_attn_la_proj"]).T.reshape(NCH, 128, 32).transpose(1, 0, 2).reshape(128, 256)),
        lb_projT=np.ascontiguousarray(f32(inp["L1_attn_lb_proj"]).T),
        wgT=np.ascontiguousarray(i32(inp["L1_ffn_w_gate"]).T).reshape(NCH, 128, 3072),
        wuT=np.ascontiguousarray(i32(inp["L1_ffn_w_up"]).T).reshape(NCH, 128, 3072),
        wdT=np.ascontiguousarray(i32(inp["L1_ffn_w_down"]).T).reshape(24, 128, DM),
    )
    cst = host_constants()
    l1m = np.zeros((8, 128, 4096), dtype=ml_dtypes.bfloat16)
    for role in range(2):
        for s in range(4):
            for j in range(8):
                l1m[role * 4 + s][:, j * 512:(j + 1) * 512] = cst["c_masks"][l1_mask_idx(role, s, j)]
    base["l1masks"] = l1m
    base.update({k: np.ascontiguousarray(v) for k, v in cst.items()})

    in_maps = []
    for core in range(8):
        seq = core // 2
        m = dict(base)
        m["x_fm"] = np.ascontiguousarray(x[seq].T).reshape(NCH, 128, T)
        in_maps.append(m)

    nc = build_program()
    res = run_bass_kernel_spmd(nc, in_maps, core_ids=list(range(8)))

    outp = np.empty((4, T, DM), dtype=np.float32)
    for core in range(8):
        seq, role = core // 2, core % 2
        G = G0 if role == 0 else G1
        fm = np.asarray(res.results[core]["out"]).reshape(DM, 2048)
        for s, g in enumerate(G):
            outp[seq, g * 512:(g + 1) * 512, :] = fm[:, s * 512:(s + 1) * 512].T
    return outp


# version 09
# speedup vs baseline: 1.3286x; 1.3286x over previous
import math
import numpy as np
import ml_dtypes
import concourse.bass as bass
import concourse.mybir as mybir
from concourse.bass_utils import run_bass_kernel_spmd

F32 = mybir.dt.float32
BF16 = mybir.dt.bfloat16
I32 = mybir.dt.int32
AF = mybir.ActivationFunctionType
ALU = mybir.AluOpType
AX = mybir.AxisListType.X

T = 4096
DM = 1024
NCH = 8
NEG = -1.0e30
S127 = float(np.float32(1.0 / 127.0))

G0 = [0, 3, 4, 7]
G1 = [1, 2, 5, 6]
BND = [8, 16, 24, 32]
ENGS = ("sync", "gpsimd", "tensor", "vector", "scalar")


def _slopes16():
    start = 2.0 ** (-2.0 ** (-(math.log2(16) - 3)))
    return [start * (start ** i) for i in range(16)]


def host_constants():
    sl = np.array(_slopes16(), dtype=np.float64)
    k = np.arange(T, dtype=np.float64)
    alibi = sl[:, None] * (k[None, :] - 2047.0)
    a = alibi.reshape(16, 32, 128).transpose(2, 0, 1).reshape(128, 512)
    a_hi = a.astype(np.float32)
    a_lo = (a - a_hi.astype(np.float64)).astype(np.float32)
    q = np.arange(T, dtype=np.float64)
    aq8 = (8.0 * sl[:, None] * (q[None, :] - 2047.0)).astype(np.float32)
    masks = np.zeros((6, 128, 512), dtype=np.float64)
    kk = np.arange(128)[:, None]
    qq = np.arange(512)[None, :]
    for m in range(4):
        masks[1 + m] = np.where(qq >= 128 * m + kk, 0.0, NEG)
    masks[5] = NEG
    sel2 = np.zeros((2, 128), dtype=np.float32)
    sel2[0, 0:64] = 1.0
    sel2[1, 64:128] = 1.0
    onesh = np.zeros((128, 2), dtype=np.float32)
    onesh[0:64, 0] = 1.0
    onesh[64:128, 1] = 1.0
    return dict(c_alibi_hi=a_hi, c_alibi_lo=a_lo, c_aq8=aq8,
                c_masks=masks.astype(ml_dtypes.bfloat16),
                c_sel2=sel2, c_ident=np.eye(128, dtype=ml_dtypes.bfloat16),
                c_identf=np.eye(128, dtype=np.float32), c_onesh=onesh)


def l1_mask_idx(role, s, j):
    g = (G0 if role == 0 else G1)[s]
    prefix = 4 * (g + 1)
    kt = BND[s] - 8 + j
    if kt < prefix - 4:
        return 0
    if kt < prefix:
        return 1 + (kt - (prefix - 4))
    return 5


class Ctr:
    def __init__(self, sem):
        self.sem = sem
        self.v = 0


def build_program():
    nc = bass.Bass("TRN2", target_bir_lowering=False, debug=False)
    P = {e: [] for e in ENGS}

    def em(eng, fn):
        P[eng].append(fn)

    din = {}

    dinh = {}

    def DI(name, shape, dt=F32):
        h = nc.dram_tensor(name, shape, dt, kind="ExternalInput")
        dinh[name] = h
        din[name] = h.ap()

    DI("x_fm", [NCH, 128, T])
    DI("Wq", [NCH, 128, DM]); DI("Wk", [NCH, 128, DM]); DI("Wv", [NCH, 128, DM])
    DI("Wo", [NCH, 128, DM]); DI("W_T", [NCH, 128, DM])
    DI("bW", [128, 8]); DI("RA", [128, 1088]); DI("Bstk", [128, DM])
    DI("w_attnT", [NCH, 128, 3072]); DI("battn", [128, 24])
    DI("la_attnT", [128, 256]); DI("lb_attnT", [32, 3072])
    DI("w_projT", [NCH, 128, DM]); DI("bproj", [128, 8])
    DI("la_projT", [128, 256]); DI("lb_projT", [32, DM])
    DI("wgT", [NCH, 128, 3072], I32); DI("wuT", [NCH, 128, 3072], I32)
    DI("wdT", [24, 128, DM], I32)
    DI("c_alibi_hi", [128, 512]); DI("c_alibi_lo", [128, 512])
    DI("c_aq8", [16, T]); DI("c_masks", [6, 128, 512], BF16)
    DI("c_sel2", [2, 128]); DI("c_ident", [128, 128], BF16)
    DI("c_identf", [128, 128])
    DI("c_onesh", [128, 2])
    DI("l1masks", [8, 128, 4096], BF16)

    out = nc.dram_tensor("out", [NCH, 128, 2048], F32, kind="ExternalOutput").ap()

    ds = {}

    dsh = {}

    def DS(name, shape, dt):
        h = nc.dram_tensor(name, shape, dt)
        dsh[name] = h
        ds[name] = h.ap()

    DS("w0h", [5, NCH, 128, DM], BF16); DS("w0l", [5, NCH, 128, DM], BF16)
    DS("w1h", [NCH, 128, 4096], BF16); DS("w1l", [NCH, 128, 4096], BF16)
    DS("wgu", [12, NCH, 128, 512], BF16)
    DS("wd", [24, 128, DM], BF16)
    for L in (0, 1):
        DS(f"k{L}h", [NCH, 128, T], BF16); DS(f"k{L}l", [NCH, 128, T], BF16)
        DS(f"v{L}h", [NCH, T, 128], BF16); DS(f"v{L}l", [NCH, T, 128], BF16)
    DS("x1s", [8, 128, 4096], F32)
    DS("hsh", [24, 128, 512], BF16); DS("hsl", [24, 128, 512], BF16)

    sb = {}

    def SB(name, shape, dt):
        sb[name] = nc.alloc_sbuf_tensor("sb_" + name, shape, dt)

    SB("AR1", [128, 4096], F32)
    SB("AR2h", [128, 4096], BF16); SB("AR2l", [128, 4096], BF16)
    SB("AR3h", [128, 4096], BF16); SB("AR3l", [128, 4096], BF16)
    SB("AR4h", [128, 4096], BF16); SB("AR4l", [128, 4096], BF16)
    SB("gA", [128, 4096], F32)
    SB("wsh", [128, 4096], BF16); SB("wsl", [128, 4096], BF16)
    SB("pf", [128, 2048], F32)
    SB("ph", [128, 3072], BF16); SB("pl", [128, 3072], BF16)
    SB("kbh", [128, 1024], BF16); SB("kbl", [128, 1024], BF16)
    SB("vbh", [128, 1024], BF16); SB("vbl", [128, 1024], BF16)
    SB("evh", [128, 512], BF16); SB("evl", [128, 512], BF16)
    SB("btile", [128, 1024], F32)
    SB("rtile", [128, 512], F32)
    SB("otmp", [128, 512], F32)
    SB("tsq", [128, 512], F32)
    SB("sqb", [128, 512], BF16)
    SB("ttile", [128, 512], F32)
    SB("hrow", [128, 512], F32)
    SB("c_alibi", [128, 512], F32)
    SB("c_masksb", [128, 2048], BF16)
    SB("c_onesh", [128, 2], BF16)
    SB("c_ones_col", [128, 1], BF16)
    SB("c_ones_f32", [128, 1], F32)
    SB("c_zero", [128, 1], F32)
    SB("bvb", [64, 512], BF16)
    SB("c_o1b", [64, 128], BF16)
    SB("c_identf", [128, 128], F32)
    SB("rah", [128, 1088], BF16); SB("ral", [128, 1088], BF16)
    SB("bsh", [128, DM], BF16); SB("bsl", [128, DM], BF16)
    SB("lath", [128, 256], BF16); SB("latl", [128, 256], BF16)
    SB("laph", [128, 256], BF16); SB("lapl", [128, 256], BF16)
    SB("bW", [128, 8], F32); SB("battn", [128, 24], F32); SB("bproj", [128, 8], F32)
    SB("ghf", [128, 128], F32)
    SB("ghh", [128, 128], BF16); SB("ghl", [128, 128], BF16)
    SB("gth", [128, 512], BF16); SB("gtl", [128, 512], BF16)
    SB("gb0", [128, 8], F32); SB("gb1", [128, 8], F32); SB("gb2", [128, 8], F32)
    SB("gb3", [128, 8], F32)
    SB("r8", [128, 1], F32); SB("negm", [128, 1], F32)
    SB("STA", [128, 6464], F32)
    SB("LORA", [128, 10240], BF16)
    class View:
        def __init__(self, t, ro, co, nr, ncl):
            self.t, self.ro, self.co, self.nr, self.ncl = t, ro, co, nr, ncl

        def __getitem__(self, idx):
            if not isinstance(idx, tuple):
                idx = (idx, slice(None))
            rs, cs = idx
            r0 = self.ro + (rs.start or 0)
            r1 = self.ro + (self.nr if rs.stop is None else rs.stop)
            c0 = self.co + (cs.start or 0)
            c1 = self.co + (self.ncl if cs.stop is None else cs.stop)
            return self.t[r0:r1, c0:c1]

    def VW(name, base, ro, co, nr, ncl):
        sb[name] = View(sb[base], ro, co, nr, ncl)

    VW("c_aq8", "STA", 0, 0, 16, 512)
    VW("nq2", "STA", 0, 512, 16, 1024)
    VW("bvec", "STA", 0, 1536, 16, 1024)
    VW("nkst2", "STA", 0, 2560, 16, 2)
    VW("nk16", "STA", 0, 2564, 16, 1)
    VW("nkmax", "STA", 0, 2568, 16, 1)
    VW("nktmp", "STA", 0, 2572, 16, 1)
    VW("lnrow", "STA", 0, 2576, 1, 2048)
    VW("bvrow", "STA", 0, 4624, 1, 1024)
    VW("bvrow2", "STA", 0, 4624, 33, 512)
    VW("drows", "STA", 0, 5648, 2, 512)
    VW("c_sel2", "STA", 0, 6160, 2, 128)
    VW("c_ones_row", "STA", 0, 6288, 1, 128)
    VW("lbth", "LORA", 0, 0, 32, 3072)
    VW("lbtl", "LORA", 0, 3072, 32, 3072)
    VW("uh", "LORA", 0, 6144, 32, 1024)
    VW("ul", "LORA", 0, 7168, 32, 1024)
    VW("lbph", "LORA", 0, 8192, 32, 1024)
    VW("lbpl", "LORA", 0, 9216, 32, 1024)
    VW("wbuf", "pf", 0, 0, 128, 2048)
    VW("maskb", "wsl", 0, 0, 128, 4096)
    VW("b1sb", "rtile", 0, 0, 128, 512)
    VW("b2sb", "otmp", 0, 0, 128, 512)
    VW("cw", "gA", 0, 0, 128, 3072)
    VW("urow", "btile", 0, 0, 32, 1024)

    pb = [nc.alloc_psum_tensor(f"pb{i}", [128, 512], F32) for i in range(8)]

    import contextlib
    es = contextlib.ExitStack()
    C = {}
    for name in ("LD", "SP", "MM", "VE", "SC", "OT", "WS", "KV0", "KV1", "CT", "MK",
                 "WSA", "WSB", "SPA", "SPB", "GP"):
        C[name] = Ctr(es.enter_context(nc.semaphore(name)))
    nc._keep = es
    LD, SP, MM, VE, SC, OT = (C[k] for k in ("LD", "SP", "MM", "VE", "SC", "OT"))
    WS, KV0, KV1, CT, MK = (C[k] for k in ("WS", "KV0", "KV1", "CT", "MK"))
    WSA, WSB = C["WSA"], C["WSB"]
    SPA, SPB = C["SPA"], C["SPB"]

    def dma(q, dst, src, ctr, role=None):
        ctr.v += 16
        if role is None:
            em(q, lambda e, d=dst, s=src, sm=ctr.sem: e.dma_start(out=d, in_=s).then_inc(sm, 16))
        else:
            em(q, lambda e, d=dst, s=src, sm=ctr.sem, r=role: e.dma_start(
                out=d, in_=s, cond=(e.partition_id() % 2 == r)).then_inc(sm, 16))
        return ctr.v

    _role_sv = {}

    def dma_ri(q, dst, src_fn, i0, i1, ctr):
        ctr.v += 16
        def go(e, d=dst, f=src_fn, a=i0, b=i1, sm=ctr.sem):
            key = id(e)
            if key not in _role_sv:
                _role_sv[key] = e.partition_id() % 2
            ik = (key, a, b)
            if ik not in _role_sv:
                _role_sv[ik] = a + _role_sv[key] * (b - a)
            e.dma_start(out=d, in_=f(_role_sv[ik])).then_inc(sm, 16)
        em(q, go)
        return ctr.v

    def wt(eng, ctr, val):
        if val > 0:
            em(eng, lambda e, sm=ctr.sem, v=val: e.wait_ge(sm, v))

    def _attach_inc(eng, ctr):
        ctr.v += 1
        last = P[eng].pop()
        P[eng].append(lambda e, f=last, sm=ctr.sem: f(e).then_inc(sm, 1))
        return ctr.v

    def vinc():
        return _attach_inc("vector", VE)

    def vlock():
        cv = _attach_inc("vector", VE)
        wt("vector", VE, cv)
        return cv

    def slock():
        cs = _attach_inc("scalar", SC)
        wt("scalar", SC, cs)
        return cs

    def sinc():
        return _attach_inc("scalar", SC)

    def minc():
        return _attach_inc("tensor", MM)

    def ginc():
        return _attach_inc("gpsimd", C["GP"])

    st = {"wsh_mm": 0, "wsh_sp": 0, "kv_mm": [0, 0], "pf_mm": [0, 0, 0], "pS_sc": [0, 0, 0],
          "kst_ve": 0, "prep_spa": 0, "prep_spb": 0,
          "b12_ve": 0, "prep_par": 0, "prep_lastv": [0, 0], "prep_lastw": [0, 0],
          "mask_used_ve": 0, "bvb_mm": 0, "bvb_ve": 0,
          "drows_ve": 0, "nq2_ld": 0, "nkst_ld": 0, "bvec_sc": 0,
          "kvspill_sp": 0, "x1spill_sp": 0, "gA_free_ve": 0, "ar2_free_mm": 0,
          "ar3_free_mm": 0, "ar4_free_mm": 0, "gt_free_mm": 0, "u_free_mm": 0,
          "ev_sp": 0, "maskb_ve": 0, "hrow_mm": 0}
    CTRS = {"LD": LD, "SP": SP, "MM": MM, "VE": VE, "SC": SC, "OT": OT}
    bank_free = [("VE", 0)] * 8

    def wt_bank(i):
        kind, v = bank_free[i]
        wt("tensor", CTRS[kind], v)

    dma("sync", sb["cw"][:, 0:512], din["c_alibi_lo"], LD)
    dma("sync", sb["c_alibi"][:], din["c_alibi_hi"], LD)
    for mi in range(4):
        dma("sync", sb["c_masksb"][:, mi * 512:mi * 512 + 512], din["c_masks"][mi + 1], LD)
    dma("sync", sb["c_sel2"][:], din["c_sel2"], LD)
    dma("sync", sb["c_identf"][:], din["c_identf"], LD)
    dma("sync", sb["bW"][:], din["bW"], LD)
    dma("sync", sb["battn"][:], din["battn"], LD)
    c0 = dma("sync", sb["bproj"][:], din["bproj"], LD)

    wt("vector", LD, c0)
    for nm, v in (("c_ones_row", 1.0), ("c_ones_col", 1.0), ("c_ones_f32", 1.0),
                  ("c_o1b", 1.0), ("c_zero", 0.0), ("nkst2", 0.0), ("nk16", 0.0)):
        em("vector", lambda e, n=nm, vv=v: e.memset(sb[n][:], vv))
    em("vector", lambda e: e.tensor_tensor(out=sb["c_alibi"][:], in0=sb["c_alibi"][:],
                                           in1=sb["cw"][:, 0:512], op=ALU.add))
    st_cw = {"ve": vinc()}

    def split_const(hname, lname, src, npart, width):
        wt("sync", VE, st_cw["ve"])
        c = dma("sync", sb["cw"][0:npart, 0:width], src, LD)
        wt("vector", LD, c)
        em("vector", lambda e, h=hname, n=npart, w=width: e.tensor_copy(
            out=sb[h][0:n, 0:w], in_=sb["cw"][0:n, 0:w]))
        vlock()
        em("vector", lambda e, h=hname, l=lname, n=npart, w=width: e.tensor_tensor(
            out=sb[l][0:n, 0:w], in0=sb["cw"][0:n, 0:w], in1=sb[h][0:n, 0:w], op=ALU.subtract))
        st_cw["ve"] = vinc()
        return st_cw["ve"]

    split_const("rah", "ral", din["RA"], 128, 1088)
    split_const("bsh", "bsl", din["Bstk"], 128, DM)
    split_const("lath", "latl", din["la_attnT"], 128, 256)
    split_const("lbth", "lbtl", din["lb_attnT"], 32, 3072)
    split_const("laph", "lapl", din["la_projT"], 128, 256)
    split_const("lbph", "lbpl", din["lb_projT"], 32, DM)
    wt("sync", VE, st_cw["ve"])
    c = dma("sync", sb["cw"][:, 0:2], din["c_onesh"], LD)
    wt("vector", LD, c)
    em("vector", lambda e: e.tensor_copy(out=sb["c_onesh"][:], in_=sb["cw"][:, 0:2]))
    st_cw["ve"] = vinc()
    st["gA_free_ve"] = st_cw["ve"]
    st["prep_lastv"][0] = st_cw["ve"]
    st["prep_lastv"][1] = st_cw["ve"]

    def prep_split(dsth, dstl, src, ncols, is_int=False, dst_writes=None):
        for i in range((ncols + 1023) // 1024):
            w0 = i * 1024
            wid = min(1024, ncols - w0)
            par = st["prep_par"]
            st["prep_par"] = 1 - par
            KVp = KV0 if par == 0 else KV1
            wt("sync", VE, st["prep_lastv"][par])
            dstap = sb["wbuf"][:, par * 1024:par * 1024 + wid]
            if is_int:
                dstap = dstap.bitcast(I32)
            cl = dma("sync", dstap, src[:, w0:w0 + wid], KVp)
            wt("vector", KVp, cl)
            wt("vector", SPA if par == 0 else SPB, st["prep_lastw"][par])
            if is_int:
                em("vector", lambda e, p=par, w=wid: e.tensor_copy(
                    out=sb["wsh"][:, p * 1024:p * 1024 + w],
                    in_=sb["wbuf"][:, p * 1024:p * 1024 + w].bitcast(I32)))
            else:
                em("vector", lambda e, p=par, w=wid: e.tensor_copy(
                    out=sb["wsh"][:, p * 1024:p * 1024 + w],
                    in_=sb["wbuf"][:, p * 1024:p * 1024 + w]))
                vlock()
                em("vector", lambda e, p=par, w=wid: e.tensor_tensor(
                    out=sb["wsl"][:, p * 1024:p * 1024 + w],
                    in0=sb["wbuf"][:, p * 1024:p * 1024 + w],
                    in1=sb["wsh"][:, p * 1024:p * 1024 + w], op=ALU.subtract))
            cv = vinc()
            st["prep_lastv"][par] = cv
            wt("gpsimd", VE, cv)
            SPp = SPA if par == 0 else SPB
            cs = 0
            if dst_writes is None:
                dma("gpsimd", dsth[:, w0:w0 + wid], sb["wsh"][:, par * 1024:par * 1024 + wid], SPp)
                cs = dma("gpsimd", dstl[:, w0:w0 + wid], sb["wsl"][:, par * 1024:par * 1024 + wid], SPp)
            else:
                for (dap, c0_, cw_) in dst_writes:
                    lo = max(c0_, w0)
                    hi = min(c0_ + cw_, w0 + wid)
                    if lo < hi:
                        cs = dma("gpsimd", dap[:, lo - c0_:hi - c0_],
                                 sb["wsh"][:, par * 1024 + (lo - w0):par * 1024 + (hi - w0)], SPp)
            st["prep_lastw"][par] = cs

    for j, wn in enumerate(("Wq", "Wk", "Wv", "Wo", "W_T")):
        for c in range(NCH):
            prep_split(ds["w0h"][j][c], ds["w0l"][j][c], din[wn][c], DM)
    for c in range(NCH):
        prep_split(ds["w1h"][c][:, 0:3072], ds["w1l"][c][:, 0:3072], din["w_attnT"][c], 3072)
        prep_split(ds["w1h"][c][:, 3072:4096], ds["w1l"][c][:, 3072:4096], din["w_projT"][c], DM)
    for c in range(NCH):
        prep_split(None, None, din["wgT"][c], 3072, is_int=True,
                   dst_writes=[(ds["wgu"][hcp][c][:, 0:256], hcp * 256, 256) for hcp in range(12)])
        prep_split(None, None, din["wuT"][c], 3072, is_int=True,
                   dst_writes=[(ds["wgu"][hcp][c][:, 256:512], hcp * 256, 256) for hcp in range(12)])
    for hc in range(24):
        prep_split(None, None, din["wdT"][hc], DM, is_int=True,
                   dst_writes=[(ds["wd"][hc], 0, DM)])
    st["wsh_sp"] = SP.v
    st["prep_spa"] = SPA.v
    st["prep_spb"] = SPB.v

    def ln(Ttok, xof, xnof):
        for tt in range(Ttok // 512):
            lastmm = 0
            for c in range(NCH):
                xo = xof(c) + tt * 512
                wt("vector", MM, lastmm)
                em("vector", lambda e, xo=xo: e.tensor_tensor(
                    out=sb["tsq"][:], in0=sb["AR1"][:, xo:xo + 512],
                    in1=sb["AR1"][:, xo:xo + 512], op=ALU.mult))
                cv = vinc()
                wt("tensor", VE, cv)
                if c == 0:
                    wt_bank(0)
                    wt_bank(1)
                em("tensor", lambda e, xo=xo, f=(c == 0), l=(c == NCH - 1): e.matmul(
                    pb[0][0:1, :], sb["c_ones_f32"][:], sb["AR1"][:, xo:xo + 512],
                    start=f, stop=l))
                em("tensor", lambda e, f=(c == 0), l=(c == NCH - 1): e.matmul(
                    pb[1][0:1, :], sb["c_ones_f32"][:], sb["tsq"][:],
                    start=f, stop=l))
                lastmm = minc()
            wt("vector", MM, lastmm)
            em("vector", lambda e: e.tensor_scalar(out=sb["lnrow"][0:1, 0:512], in0=pb[0][0:1, :],
                                                   scalar1=1.0 / 1024.0, scalar2=None, op0=ALU.mult))
            em("vector", lambda e: e.tensor_scalar(out=sb["lnrow"][0:1, 512:1024],
                                                   in0=pb[1][0:1, :],
                                                   scalar1=1.0 / 1024.0, scalar2=None, op0=ALU.mult))
            vlock()
            em("vector", lambda e: e.tensor_tensor(out=sb["lnrow"][0:1, 1536:2048],
                                                   in0=sb["lnrow"][0:1, 0:512],
                                                   in1=sb["lnrow"][0:1, 0:512], op=ALU.mult))
            vlock()
            em("vector", lambda e: e.tensor_tensor(out=sb["lnrow"][0:1, 512:1024],
                                                   in0=sb["lnrow"][0:1, 512:1024],
                                                   in1=sb["lnrow"][0:1, 1536:2048], op=ALU.subtract))
            vlock()
            em("vector", lambda e: e.tensor_scalar(out=sb["lnrow"][0:1, 512:1024],
                                                   in0=sb["lnrow"][0:1, 512:1024],
                                                   scalar1=1e-5, scalar2=None, op0=ALU.add))
            cv = vinc()
            bank_free[0] = ("VE", cv)
            bank_free[1] = ("VE", cv)
            wt("scalar", VE, cv)
            em("scalar", lambda e: e.activation(out=sb["lnrow"][0:1, 1024:1536],
                                                in_=sb["lnrow"][0:1, 512:1024],
                                                func=AF.Sqrt, bias=0.0, scale=1.0))
            cs = sinc()
            wt("vector", SC, cs)
            em("vector", lambda e: e.reciprocal(out=sb["lnrow"][0:1, 1024:1536],
                                                in_=sb["lnrow"][0:1, 1024:1536]))
            vlock()
            em("vector", lambda e: e.tensor_tensor(out=sb["lnrow"][0:1, 1536:2048],
                                                   in0=sb["lnrow"][0:1, 0:512],
                                                   in1=sb["lnrow"][0:1, 1024:1536], op=ALU.mult))
            cv = vinc()
            wt("tensor", VE, cv)
            wt("tensor", VE, st["b12_ve"])
            wt_bank(2)
            wt_bank(3)
            em("tensor", lambda e: e.matmul(pb[2][:], sb["c_ones_row"][:],
                                            sb["lnrow"][0:1, 1024:1536], start=True, stop=True))
            em("tensor", lambda e: e.matmul(pb[3][:], sb["c_ones_row"][:],
                                            sb["lnrow"][0:1, 1536:2048], start=True, stop=True))
            cm = minc()
            wt("scalar", MM, cm)
            em("scalar", lambda e: e.copy(out=sb["b1sb"][:], in_=pb[2][:]))
            em("scalar", lambda e: e.copy(out=sb["b2sb"][:], in_=pb[3][:]))
            cs = sinc()
            bank_free[2] = ("SC", cs)
            bank_free[3] = ("SC", cs)
            wt("vector", SC, cs)
            wt("vector", MM, MM.v)
            for cp in range(NCH // 2):
                for half, tn in ((0, "tsq"), (1, "ttile")):
                    c = 2 * cp + half
                    xo = xof(c) + tt * 512
                    em("vector", lambda e, xo=xo, tn=tn: e.tensor_tensor(
                        out=sb[tn][:], in0=sb["AR1"][:, xo:xo + 512], in1=sb["b1sb"][:],
                        op=ALU.mult))
                vlock()
                for half, tn in ((0, "tsq"), (1, "ttile")):
                    em("vector", lambda e, tn=tn: e.tensor_tensor(
                        out=sb[tn][:], in0=sb[tn][:], in1=sb["b2sb"][:], op=ALU.subtract))
                vlock()
                for half, tn in ((0, "tsq"), (1, "ttile")):
                    c = 2 * cp + half
                    no = xnof(c) + tt * 512
                    em("vector", lambda e, no=no, tn=tn: e.tensor_copy(
                        out=sb["AR2h"][:, no:no + 512], in_=sb[tn][:]))
                vlock()
                for half, tn in ((0, "tsq"), (1, "ttile")):
                    c = 2 * cp + half
                    no = xnof(c) + tt * 512
                    em("vector", lambda e, no=no, tn=tn: e.tensor_tensor(
                        out=sb["AR2l"][:, no:no + 512], in0=sb[tn][:],
                        in1=sb["AR2h"][:, no:no + 512], op=ALU.subtract))
                vlock()
            st["b12_ve"] = vinc()

    def wload(srcs_h, srcs_l, gw):
        wt("sync", MM, st["wsh_mm"])
        wt("sync", MM, st.get("wshp_mm", [0, 0])[0])
        wt("sync", MM, st.get("wshp_mm", [0, 0])[1])
        wt("sync", SP, st["wsh_sp"])
        wt("sync", SPA, st.pop("prep_spa", 0))
        wt("sync", SPB, st.pop("prep_spb", 0))
        wt("sync", VE, st.get("wsl_ve", 0))
        st["wsl_ve"] = 0
        st["wsh_sp"] = 0
        cl = 0
        for dc in range(NCH):
            cl = dma("sync", sb["wsh"][:, dc * 512:dc * 512 + gw], srcs_h[dc], WS)
            if srcs_l is not None:
                cl = dma("sync", sb["wsl"][:, dc * 512:dc * 512 + gw], srcs_l[dc], WS)
        return cl

    def fm_proj(Ttok, wh_fn, wl_fn, oc_list, evac, rhs=("AR2h", "AR2l"),
                rhsof=None, lora2=None):
        ntt = Ttok // 512
        rh, rl = rhs
        groups = [oc_list[k:k + 2] for k in range(0, len(oc_list), 2)]
        wt("sync", SP, st["wsh_sp"])
        wt("sync", SPA, st.pop("prep_spa", 0))
        wt("sync", SPB, st.pop("prep_spb", 0))
        wt("sync", VE, st.get("wsl_ve", 0))
        st["wsl_ve"] = 0
        st["wsh_sp"] = 0
        st.setdefault("wshp_mm", [0, 0])

        def loadg(g):
            par = g % 2
            ws = WSA if par == 0 else WSB
            gw = len(groups[g]) * 128
            c0 = 2 * g * 128
            wt("sync", MM, st["wshp_mm"][par])
            wt("sync", MM, st["wsh_mm"])
            cl = 0
            for dc in range(NCH):
                cl = dma("sync", sb["wsh"][:, par * 2048 + dc * 256:par * 2048 + dc * 256 + gw],
                         wh_fn(dc, c0, gw), ws)
                cl = dma("sync", sb["wsl"][:, par * 2048 + dc * 256:par * 2048 + dc * 256 + gw],
                         wl_fn(dc, c0, gw), ws)
            return cl

        cls = [None] * len(groups)
        cls[0] = loadg(0)
        for g, ocs in enumerate(groups):
            par = g % 2
            if g + 1 < len(groups):
                cls[g + 1] = loadg(g + 1)
            wt("tensor", WSA if par == 0 else WSB, cls[g])
            for i, oc in enumerate(ocs):
                for tt in range(ntt):
                    bk = par * 2 + i * ntt + tt
                    wt_bank(bk)
                    pairs = []
                    for dc in range(NCH):
                        wof = par * 2048 + dc * 256 + i * 128
                        xno = rhsof(dc) + tt * 512
                        pairs.append((("wsh", wof, 128), (rh, xno)))
                        pairs.append((("wsh", wof, 128), (rl, xno)))
                        pairs.append((("wsl", wof, 128), (rh, xno)))
                    if lora2 is not None:
                        pairs.extend(lora2(i, oc, tt))
                    n = len(pairs)
                    for j, ((an, ao, aw), (bn, bo)) in enumerate(pairs):
                        em("tensor", lambda e, an=an, ao=ao, aw=aw, bn=bn, bo=bo, bk=bk,
                           f=(j == 0), l=(j == n - 1): e.matmul(
                            pb[bk][:], sb[an][:, ao:ao + aw], sb[bn][:, bo:bo + 512],
                            start=f, stop=l))
                    cm = minc()
                    st["wshp_mm"][par] = cm
                    evac(bk, oc, tt, cm)

    def evac_split(bk, cm, dsth, dstl, ho, bias_ap):
        wt("vector", MM, cm)
        if bias_ap is None:
            em("vector", lambda e, bk=bk, ho=ho, d=dsth: e.tensor_copy(
                out=sb[d][:, ho:ho + 512], in_=pb[bk][:]))
            vlock()
            em("vector", lambda e, bk=bk, ho=ho, d=dsth, dl=dstl: e.tensor_tensor(
                out=sb[dl][:, ho:ho + 512], in0=pb[bk][:], in1=sb[d][:, ho:ho + 512],
                op=ALU.subtract))
        else:
            em("vector", lambda e, bk=bk, ho=ho, d=dsth, ba=bias_ap: e.tensor_scalar(
                out=sb[d][:, ho:ho + 512], in0=pb[bk][:], scalar1=ba, scalar2=None, op0=ALU.add))
            vlock()
            em("vector", lambda e, bk=bk, ho=ho, d=dsth, dl=dstl, ba=bias_ap: e.scalar_tensor_tensor(
                out=sb[dl][:, ho:ho + 512], in0=pb[bk][:], scalar=ba,
                in1=sb[d][:, ho:ho + 512], op0=ALU.add, op1=ALU.subtract))
        return vinc()

    def stats_sq_mm(bk, hname, ho):
        em("vector", lambda e, h=hname, ho=ho: e.tensor_tensor(
            out=sb["sqb"][:], in0=sb[h][:, ho:ho + 512], in1=sb[h][:, ho:ho + 512], op=ALU.mult))
        cv = vinc()
        wt("tensor", VE, cv)
        em("tensor", lambda e, bk=bk: e.matmul(pb[bk][0:2, :], sb["c_onesh"][:], sb["sqb"][:],
                                               start=True, stop=True))
        return minc()

    def stats_q(bk, c, tt, hname, ho):
        cm2 = stats_sq_mm(bk, hname, ho)
        wt("vector", MM, cm2)
        wt("vector", CT, st.get("otmp_ld", 0))
        em("vector", lambda e, bk=bk: e.tensor_copy(out=sb["otmp"][0:2, :], in_=pb[bk][0:2, :]))
        cv = vinc()
        bank_free[bk] = ("VE", cv)
        wt("sync", VE, cv)
        wt("sync", SC, st["bvec_sc"])
        st["bvec_sc"] = 0
        cd = dma("sync", sb["nq2"][2 * c:2 * c + 2, tt * 512:tt * 512 + 512], sb["otmp"][0:2, :], CT)
        st["nq2_ld"] = max(st["nq2_ld"], cd)
        st["otmp_ld"] = cd

    def stats_k(bk, c, tt, hname, ho):
        cm2 = stats_sq_mm(bk, hname, ho)
        wt("vector", MM, cm2)
        wt("vector", CT, CT.v)
        em("vector", lambda e, bk=bk: e.reduce_max(out=sb["nktmp"][0:2, 0:1],
                                                   in_=pb[bk][0:2, :], axis=AX))
        cv2 = vinc()
        bank_free[bk] = ("VE", cv2)
        wt("sync", VE, cv2)
        cd = dma("sync", sb["nkst2"][2 * c:2 * c + 2, tt:tt + 1], sb["nktmp"][0:2, 0:1], CT)
        st["nkst_ld"] = cd

    def finish_kstats():
        wt("vector", CT, CT.v)
        em("vector", lambda e: e.reduce_max(out=sb["nktmp"][:], in_=sb["nkst2"][:], axis=AX))
        vlock()
        em("vector", lambda e: e.tensor_tensor(out=sb["nk16"][:], in0=sb["nk16"][:],
                                               in1=sb["nktmp"][:], op=ALU.max))
        em("vector", lambda e: e.memset(sb["nkst2"][:], 0.0))
        st["kst_ve"] = vinc()
        return st["kst_ve"]

    def build_bvec(Ttok, q0, use_alibi):
        if use_alibi:
            wt("sync", VE, st.get("kst_ve", 0))
            caq = dma("sync", sb["c_aq8"][0:16, 0:Ttok], din["c_aq8"][0:16, q0:q0 + Ttok], CT)
            wt("vector", CT, caq)
            wt("scalar", CT, caq)
        wt("scalar", CT, CT.v)
        em("scalar", lambda e, w=Ttok: e.activation(out=sb["bvec"][0:16, 0:w],
                                                    in_=sb["nq2"][0:16, 0:w], func=AF.Sqrt,
                                                    bias=0.0, scale=1.0))
        em("scalar", lambda e: e.activation(out=sb["nkmax"][:], in_=sb["nk16"][:],
                                            func=AF.Sqrt, bias=0.0, scale=1.0))
        cs = sinc()
        st["bvec_sc"] = cs
        wt("vector", SC, cs)
        em("vector", lambda e, w=Ttok: e.tensor_scalar(
            out=sb["bvec"][0:16, 0:w], in0=sb["bvec"][0:16, 0:w],
            scalar1=sb["nkmax"][0:16, 0:1], scalar2=1.05, op0=ALU.mult, op1=ALU.mult))
        vlock()
        if use_alibi:
            em("vector", lambda e, w=Ttok: e.tensor_tensor(
                out=sb["bvec"][0:16, 0:w], in0=sb["bvec"][0:16, 0:w],
                in1=sb["c_aq8"][0:16, 0:w], op=ALU.add))
            vlock()
        em("vector", lambda e, w=Ttok: e.tensor_scalar(
            out=sb["bvec"][0:16, 0:w], in0=sb["bvec"][0:16, 0:w],
            scalar1=8.0, scalar2=-1.0, op0=ALU.add, op1=ALU.mult))
        return vinc()

    def attention(L, hp, qts, bvec_ve, o_bias_col):
        wt("gpsimd", SP, st["kvspill_sp"])
        st["kvspill_sp"] = 0
        for bk in range(7):
            wt_bank(bk)
        for i, qd in enumerate(qts):
            for a in range(2):
                idx = i * 2 + a
                h = 2 * hp + a
                wt("sync", VE, bvec_ve)
                wt("sync", VE, st.get("o_ve", 0))
                wt("sync", VE, st["bvb_ve"])
                cd = dma("sync", sb["bvrow2"][idx * 32:idx * 32 + 1, 0:512],
                         sb["bvec"][h:h + 1, qd["bvoff"]:qd["bvoff"] + 512], CT)
                wt("vector", CT, cd)
                wt("vector", MM, st["bvb_mm"])
                em("vector", lambda e, idx=idx: e.tensor_copy(
                    out=sb["bvb"][idx * 32:idx * 32 + 1, 0:512],
                    in_=sb["bvrow2"][idx * 32:idx * 32 + 1, 0:512]))
        st["bvb_ve"] = vinc()
        PMAP = ((0, 1), (2, 3), (5, 7))

        def emit_av(un):
            wt("tensor", VE, un["cvL"])
            p, vo, obk, i = un["p"], un["vo"], un["obk"], un["i"]
            for pi, (vn, pn) in enumerate((("vbh", "ph"), ("vbh", "pl"), ("vbl", "ph"))):
                for a in range(2):
                    em("tensor", lambda e, vn=vn, pn=pn, a=a, vo=vo, p=p, obk=obk,
                       f=(un["fflag"] and pi == 0), l=(un["lflag"] and pi == 2): e.matmul(
                        pb[obk][a * 64:a * 64 + 64, :],
                        sb[vn][:, vo + a * 64:vo + a * 64 + 64],
                        sb[pn][:, p * 1024 + a * 512:p * 1024 + a * 512 + 512],
                        start=f, stop=l, tile_position=(0, a * 64),
                        skip_group_check=True))
            for pi, pn in enumerate(("ph", "pl")):
                for a in range(2):
                    dr = i * 64 + a * 32
                    em("tensor", lambda e, pn=pn, a=a, p=p, dr=dr,
                       f=(un["fflag"] and pi == 0), l=(un["lflag"] and pi == 1): e.matmul(
                        pb[6][dr:dr + 1, :], sb["c_ones_col"][:],
                        sb[pn][:, p * 1024 + a * 512:p * 1024 + a * 512 + 512],
                        start=f, stop=l, tile_position=(0, dr),
                        skip_group_check=True))
            cmA = minc()
            st["pf_mm"][p] = cmA
            st["kv_mm"][un["kp"]] = cmA

        wt("tensor", VE, st["bvb_ve"])
        ktmax = max(qd["kt_lim"] for qd in qts)
        u = 0
        ckv = 0
        for ktg in range((ktmax + 3) // 4):
            kp = ktg % 2
            KVp = KV0 if kp == 0 else KV1
            wt("gpsimd", MM, st["kv_mm"][kp])
            dma("gpsimd", sb["kbh"][:, kp * 512:kp * 512 + 512],
                ds[f"k{L}h"][hp][:, ktg * 512:ktg * 512 + 512], KVp)
            dma("gpsimd", sb["kbl"][:, kp * 512:kp * 512 + 512],
                ds[f"k{L}l"][hp][:, ktg * 512:ktg * 512 + 512], KVp)
            for j4 in range(4):
                dma("gpsimd", sb["vbh"][:, kp * 512 + j4 * 128:kp * 512 + j4 * 128 + 128],
                    ds[f"v{L}h"][hp][ktg * 512 + j4 * 128:ktg * 512 + j4 * 128 + 128, :], KVp)
                ckv = dma("gpsimd", sb["vbl"][:, kp * 512 + j4 * 128:kp * 512 + j4 * 128 + 128],
                          ds[f"v{L}l"][hp][ktg * 512 + j4 * 128:ktg * 512 + j4 * 128 + 128, :], KVp)
            ckvp = kp
            for j in range(4):
                kt = ktg * 4 + j
                if kt >= ktmax:
                    break
                for i, qd in enumerate(qts):
                    if kt >= qd["kt_lim"]:
                        continue
                    p = u % 3
                    wt("tensor", KV0 if ckvp == 0 else KV1, ckv)
                    wt("tensor", SC, st["pS_sc"][p])
                    ko = kp * 512 + j * 128
                    kvp_now = kp
                    for pi, (kn, qn) in enumerate((("kbh", qd["qh"]), ("kbh", qd["ql"]),
                                                   ("kbl", qd["qh"]))):
                        for a in range(2):
                            em("tensor", lambda e, kn=kn, qn=qn, a=a, ko=ko, qc=qd["qcol"],
                               p=p, f=(pi == 0): e.matmul(
                                pb[PMAP[p][a]][:], sb[kn][a * 64:a * 64 + 64, ko:ko + 128],
                                sb[qn][a * 64:a * 64 + 64, qc:qc + 512],
                                start=f, stop=False, tile_position=(a * 64, 0),
                                skip_group_check=True))
                    for a in range(2):
                        em("tensor", lambda e, a=a, p=p, idx=(i * 2 + a): e.matmul(
                            pb[PMAP[p][a]][:], sb["c_o1b"][idx * 32:idx * 32 + 1, :],
                            sb["bvb"][idx * 32:idx * 32 + 1, 0:512],
                            start=False, stop=True, tile_position=(idx * 32, 0),
                            skip_group_check=True))
                    cmS = minc()
                    st["bvb_mm"] = cmS
                    mk = qd["mask_fn"](kt)
                    if mk is not None:
                        wt("vector", MM, cmS)
                        for a in range(2):
                            em("vector", lambda e, a=a, p=p, mn=mk[0], mo=mk[1]: e.tensor_tensor(
                                out=pb[PMAP[p][a]][:], in0=pb[PMAP[p][a]][:],
                                in1=sb[mn][:, mo:mo + 512], op=ALU.add))
                        cvA = vinc()
                        st["mask_used_ve"] = cvA
                        wt("scalar", VE, cvA)
                    else:
                        wt("scalar", MM, cmS)
                    p2 = u % 2
                    wt("scalar", MM, st["pf_mm"][p])
                    wt("scalar", VE, st.setdefault("pf_ve", [0, 0])[p2])
                    for a in range(2):
                        if L == 0:
                            bcol = (2 * hp + a) * 32 + kt
                            bap = sb["c_alibi"][:, bcol:bcol + 1]
                        else:
                            bap = sb["c_zero"][:, 0:1]
                        pof = p2 * 1024 + a * 512
                        em("scalar", lambda e, p=p, a=a, bap=bap, pof=pof: e.activation(
                            out=sb["pf"][:, pof:pof + 512], in_=pb[PMAP[p][a]][:],
                            func=AF.Exp, bias=bap, scale=0.125))
                    for a in range(2):
                        pof = p2 * 1024 + a * 512
                        po = p * 1024 + a * 512
                        em("scalar", lambda e, po=po, pof=pof: e.copy(
                            out=sb["ph"][:, po:po + 512], in_=sb["pf"][:, pof:pof + 512]))
                    csE = sinc()
                    st["pS_sc"][p] = csE
                    wt("vector", SC, csE)
                    for a in range(2):
                        pof = p2 * 1024 + a * 512
                        po = p * 1024 + a * 512
                        em("vector", lambda e, po=po, pof=pof: e.tensor_tensor(
                            out=sb["pl"][:, po:po + 512], in0=sb["pf"][:, pof:pof + 512],
                            in1=sb["ph"][:, po:po + 512], op=ALU.subtract))
                    cvL = vinc()
                    st["pf_ve"][p2] = cvL
                    fflag = not qd["_started"]
                    qd["_started"] = True
                    lflag = (kt == qd["kt_lim"] - 1)
                    unit = dict(p=p, vo=kp * 512 + j * 128, obk=4 + i, i=i,
                                fflag=fflag, lflag=lflag, cvL=cvL, kp=kvp_now)
                    pq = st.setdefault("att_pend", [])
                    pq.append(unit)
                    if len(pq) > 2:
                        emit_av(pq.pop(0))
                    u += 1
        for un in st.get("att_pend", []):
            emit_av(un)
        st["att_pend"] = []
        for i, qd in enumerate(qts):
            obk = 4 + i
            wt("vector", MM, max(st["pf_mm"]))
            wt("vector", CT, st.get("otmp_ld", 0))
            em("vector", lambda e, i=i: e.tensor_copy(out=sb["otmp"][i * 64:i * 64 + 1, :],
                                                      in_=pb[6][i * 64:i * 64 + 1, :]))
            em("vector", lambda e, i=i: e.tensor_copy(out=sb["otmp"][i * 64 + 32:i * 64 + 33, :],
                                                      in_=pb[6][i * 64 + 32:i * 64 + 33, :]))
            cvd = vinc()
            wt("sync", VE, cvd)
            wt("sync", VE, st["drows_ve"])
            dma("sync", sb["drows"][0:1, :], sb["otmp"][i * 64:i * 64 + 1, :], CT)
            cd2 = dma("sync", sb["drows"][1:2, :], sb["otmp"][i * 64 + 32:i * 64 + 33, :], CT)
            st["otmp_ld"] = cd2
            wt("vector", CT, cd2)
            em("vector", lambda e: e.reciprocal(out=sb["drows"][:], in_=sb["drows"][:]))
            cv = vinc()
            st["drows_ve"] = cv
            wt("tensor", VE, cv)
            wt_bank(7)
            em("tensor", lambda e: e.matmul(pb[7][:], sb["c_sel2"][:], sb["drows"][:],
                                            start=True, stop=True))
            cm = minc()
            wt("scalar", MM, cm)
            em("scalar", lambda e: e.copy(out=sb["rtile"][:], in_=pb[7][:]))
            cs = sinc()
            bank_free[7] = ("SC", cs)
            wt("vector", SC, cs)
            wt("vector", MM, st["ar4_free_mm"])
            st["ar4_free_mm"] = 0
            em("vector", lambda e, obk=obk: e.tensor_tensor(
                out=sb["otmp"][:], in0=pb[obk][:], in1=sb["rtile"][:], op=ALU.mult))
            vlock()
            oc = qd["ocol"]
            if o_bias_col is not None:
                em("vector", lambda e, oc=oc, bc=o_bias_col: e.tensor_scalar(
                    out=sb["AR4h"][:, oc:oc + 512], in0=sb["otmp"][:],
                    scalar1=sb["battn"][:, bc:bc + 1], scalar2=None, op0=ALU.add))
                vlock()
                em("vector", lambda e, oc=oc, bc=o_bias_col: e.scalar_tensor_tensor(
                    out=sb["AR4l"][:, oc:oc + 512], in0=sb["otmp"][:],
                    scalar=sb["battn"][:, bc:bc + 1], in1=sb["AR4h"][:, oc:oc + 512],
                    op0=ALU.add, op1=ALU.subtract))
            else:
                em("vector", lambda e, oc=oc: e.tensor_copy(
                    out=sb["AR4h"][:, oc:oc + 512], in_=sb["otmp"][:]))
                vlock()
                em("vector", lambda e, oc=oc: e.tensor_tensor(
                    out=sb["AR4l"][:, oc:oc + 512], in0=sb["otmp"][:],
                    in1=sb["AR4h"][:, oc:oc + 512], op=ALU.subtract))
            cv = vinc()
            bank_free[obk] = ("VE", cv)
            bank_free[6] = ("VE", cv)
            st["o_ve"] = cv
            qd["_started"] = False

    def lora_u(Ttok, la_h, la_l, rhs, rhsof):
        rh, rl = rhs
        wt("tensor", VE, st["b12_ve"])
        for tt in range(Ttok // 512):
            bk = 7
            wt_bank(bk)
            pairs = []
            for dc in range(NCH):
                pairs.append(((la_h, dc * 32), (rh, rhsof(dc) + tt * 512)))
                pairs.append(((la_h, dc * 32), (rl, rhsof(dc) + tt * 512)))
                pairs.append(((la_l, dc * 32), (rh, rhsof(dc) + tt * 512)))
            n = len(pairs)
            for j, ((an, ao), (bn, bo)) in enumerate(pairs):
                em("tensor", lambda e, an=an, ao=ao, bn=bn, bo=bo, bk=bk,
                   f=(j == 0), l=(j == n - 1): e.matmul(
                    pb[bk][0:32, :], sb[an][:, ao:ao + 32], sb[bn][:, bo:bo + 512],
                    start=f, stop=l))
            cm = minc()
            wt("scalar", MM, cm)
            wt("scalar", MM, st["u_free_mm"])
            st["u_free_mm"] = 0
            em("scalar", lambda e, tt=tt, bk=bk: e.mul(sb["urow"][0:32, tt * 512:tt * 512 + 512],
                                                       pb[bk][0:32, :], 0.5))
            cs = sinc()
            bank_free[bk] = ("SC", cs)
            wt("vector", SC, cs)
            em("vector", lambda e, tt=tt: e.tensor_copy(
                out=sb["uh"][0:32, tt * 512:tt * 512 + 512],
                in_=sb["urow"][0:32, tt * 512:tt * 512 + 512]))
            vlock()
            em("vector", lambda e, tt=tt: e.tensor_tensor(
                out=sb["ul"][0:32, tt * 512:tt * 512 + 512],
                in0=sb["urow"][0:32, tt * 512:tt * 512 + 512],
                in1=sb["uh"][0:32, tt * 512:tt * 512 + 512], op=ALU.subtract))
        return vinc()

    for blk in range(8):
        t0 = blk * 512
        wt("sync", SP, st["x1spill_sp"])
        st["x1spill_sp"] = 0
        wt("sync", VE, st["b12_ve"])
        if blk == 0:
            wt("sync", VE, st["prep_lastv"][0])
        cx = 0
        for c in range(NCH):
            cx = dma("sync", sb["AR1"][:, c * 512:c * 512 + 512],
                     din["x_fm"][c][:, t0:t0 + 512], LD)
        wt("vector", LD, cx)
        ln(512, lambda c: c * 512, lambda c: c * 512)

        def ev_q(bk, oc, tt, cm):
            cv = evac_split(bk, cm, "AR3h", "AR3l", oc * 512, None)
            bank_free[bk] = ("VE", cv)
            stats_q(bk, oc, 0, "AR3h", oc * 512)

        wt("vector", MM, st["ar3_free_mm"])
        st["ar3_free_mm"] = 0
        wt("tensor", VE, st["b12_ve"])
        fm_proj(512, lambda dc, c0, gw: ds["w0h"][0][dc][:, c0:c0 + gw],
                lambda dc, c0, gw: ds["w0l"][0][dc][:, c0:c0 + gw],
                list(range(8)), ev_q, rhsof=lambda dc: dc * 512)

        def ev_k(bk, oc, tt, cm):
            wt("vector", SP, st["ev_sp"])
            cv = evac_split(bk, cm, "evh", "evl", 0, None)
            wt("gpsimd", VE, cv)
            dma("gpsimd", ds["k0h"][oc][:, t0:t0 + 512], sb["evh"][:], SP)
            cs = dma("gpsimd", ds["k0l"][oc][:, t0:t0 + 512], sb["evl"][:], SP)
            st["ev_sp"] = cs
            st["kvspill_sp"] = cs
            bank_free[bk] = ("VE", cv)
            stats_k(bk, oc, 0, "evh", 0)

        fm_proj(512, lambda dc, c0, gw: ds["w0h"][1][dc][:, c0:c0 + gw],
                lambda dc, c0, gw: ds["w0l"][1][dc][:, c0:c0 + gw],
                list(range(8)), ev_k, rhsof=lambda dc: dc * 512)
        finish_kstats()

        for half in range(2):
            cl = wload([ds["w0h"][2][dc][:, half * 512:half * 512 + 512] for dc in range(NCH)],
                       [ds["w0l"][2][dc][:, half * 512:half * 512 + 512] for dc in range(NCH)], 512)
            wt("tensor", WS, cl)
            for tk in range(4):
                bk = tk % 2
                wt_bank(bk)
                pairs = []
                for dc in range(NCH):
                    xno = dc * 512 + tk * 128
                    pairs.append((("AR2h", xno), ("wsh", dc * 512)))
                    pairs.append((("AR2l", xno), ("wsh", dc * 512)))
                    pairs.append((("AR2h", xno), ("wsl", dc * 512)))
                n = len(pairs)
                for j, ((an, ao), (bn, bo)) in enumerate(pairs):
                    em("tensor", lambda e, an=an, ao=ao, bn=bn, bo=bo, bk=bk,
                       f=(j == 0), l=(j == n - 1): e.matmul(
                        pb[bk][:], sb[an][:, ao:ao + 128], sb[bn][:, bo:bo + 512],
                        start=f, stop=l))
                cm = minc()
                st["wsh_mm"] = cm
                wt("vector", MM, cm)
                wt("vector", SP, st["ev_sp"])
                em("vector", lambda e, bk=bk: e.tensor_copy(out=sb["evh"][:], in_=pb[bk][:]))
                vlock()
                em("vector", lambda e, bk=bk: e.tensor_tensor(
                    out=sb["evl"][:], in0=pb[bk][:], in1=sb["evh"][:], op=ALU.subtract))
                cv = vinc()
                bank_free[bk] = ("VE", cv)
                wt("gpsimd", VE, cv)
                cs = 0
                for c4 in range(4):
                    cpr = half * 4 + c4
                    dma("gpsimd", ds["v0h"][cpr][t0 + tk * 128:t0 + tk * 128 + 128, :],
                        sb["evh"][:, c4 * 128:c4 * 128 + 128], SP)
                    cs = dma("gpsimd", ds["v0l"][cpr][t0 + tk * 128:t0 + tk * 128 + 128, :],
                             sb["evl"][:, c4 * 128:c4 * 128 + 128], SP)
                st["ev_sp"] = cs
                st["kvspill_sp"] = cs

        for tk in range(4):
            bk = 3
            wt_bank(bk)
            pairs = []
            for dc in range(NCH):
                xno = dc * 512 + tk * 128
                pairs.append((("AR2h", xno), ("rah", dc * 136)))
                pairs.append((("AR2l", xno), ("rah", dc * 136)))
                pairs.append((("AR2h", xno), ("ral", dc * 136)))
            n = len(pairs)
            for j, ((an, ao), (bn, bo)) in enumerate(pairs):
                em("tensor", lambda e, an=an, ao=ao, bn=bn, bo=bo, bk=bk,
                   f=(j == 0), l=(j == n - 1): e.matmul(
                    pb[bk][:, 0:136], sb[an][:, ao:ao + 128], sb[bn][:, bo:bo + 136],
                    start=f, stop=l))
            cm = minc()
            wt("vector", MM, cm)
            em("vector", lambda e, bk=bk: e.reduce_max(out=sb["negm"][:], in_=pb[bk][:, 0:8], axis=AX))
            vlock()
            em("vector", lambda e: e.tensor_scalar(out=sb["negm"][:], in0=sb["negm"][:],
                                                   scalar1=-1.0, scalar2=None, op0=ALU.mult))
            cv = vinc()
            wt("scalar", VE, cv)
            em("scalar", lambda e, bk=bk: e.activation(out=sb["gb0"][:], in_=pb[bk][:, 0:8],
                                                       func=AF.Exp, bias=sb["negm"][:, 0:1],
                                                       scale=1.0, accum_out=sb["r8"][:]))
            cs = sinc()
            wt("vector", SC, cs)
            em("vector", lambda e: e.reciprocal(out=sb["r8"][:], in_=sb["r8"][:]))
            cv = vinc()
            wt("vector", VE, cv)
            em("vector", lambda e: e.tensor_scalar(out=sb["gb1"][:], in0=sb["gb0"][:],
                                                   scalar1=sb["r8"][:, 0:1], scalar2=None,
                                                   op0=ALU.mult))
            vlock()
            em("vector", lambda e: e.max(out=sb["gb2"][:], in_=sb["gb1"][:]))
            cv = vinc()
            wt("vector", VE, cv)
            em("vector", lambda e: e.tensor_scalar(out=sb["gb3"][:], in0=sb["gb1"][:],
                                                   scalar1=sb["gb2"][:, 1:2], scalar2=None,
                                                   op0=ALU.is_ge))
            vlock()
            em("vector", lambda e: e.tensor_tensor(out=sb["gb1"][:], in0=sb["gb1"][:],
                                                   in1=sb["gb3"][:], op=ALU.mult))
            vlock()
            em("vector", lambda e: e.reduce_sum(out=sb["r8"][:], in_=sb["gb1"][:], axis=AX))
            vlock()
            em("vector", lambda e: e.tensor_scalar(out=sb["r8"][:], in0=sb["r8"][:],
                                                   scalar1=1e-6, scalar2=None, op0=ALU.add))
            vlock()
            em("vector", lambda e: e.reciprocal(out=sb["r8"][:], in_=sb["r8"][:]))
            cv = vinc()
            wt("vector", VE, cv)
            em("vector", lambda e: e.tensor_scalar(out=sb["gb1"][:], in0=sb["gb1"][:],
                                                   scalar1=sb["r8"][:, 0:1], scalar2=None,
                                                   op0=ALU.mult))
            cv = vinc()
            wt("vector", VE, cv)
            for ee in range(8):
                em("vector", lambda e, bk=bk, ee=ee: e.tensor_scalar(
                    out=sb["ghf"][:, ee * 16:ee * 16 + 16],
                    in0=pb[bk][:, 8 + ee * 16:8 + ee * 16 + 16],
                    scalar1=sb["gb1"][:, ee:ee + 1], scalar2=None, op0=ALU.mult))
            cv = vinc()
            bank_free[bk] = ("VE", cv)
            wt("tensor", VE, cv)
            wt("tensor", MM, st["gt_free_mm"])
            st["gt_free_mm"] = 0
            wt_bank(7)
            em("tensor", lambda e: e.transpose(pb[7][:, 0:128], sb["ghf"][:], sb["c_identf"][:]))
            cm = minc()
            wt("vector", MM, cm)
            em("vector", lambda e, tk=tk: e.tensor_copy(
                out=sb["gth"][:, tk * 128:tk * 128 + 128], in_=pb[7][:, 0:128]))
            vlock()
            em("vector", lambda e, tk=tk: e.tensor_tensor(
                out=sb["gtl"][:, tk * 128:tk * 128 + 128], in0=pb[7][:, 0:128],
                in1=sb["gth"][:, tk * 128:tk * 128 + 128], op=ALU.subtract))
            cv = vinc()
            bank_free[7] = ("VE", cv)
            st["gt_ve"] = cv

        def lora0(i, oc, tt):
            return [(("bsh", oc * 128, 128), ("gth", 0)),
                    (("bsh", oc * 128, 128), ("gtl", 0)),
                    (("bsl", oc * 128, 128), ("gth", 0))]

        def ev_ffn0(bk, oc, tt, cm):
            wt("scalar", MM, cm)
            wt("scalar", VE, st["gA_free_ve"])
            st["gA_free_ve"] = 0
            em("scalar", lambda e, bk=bk, oc=oc: e.activation(
                out=sb["gA"][:, oc * 512:oc * 512 + 512],
                in_=pb[bk][:], func=AF.Silu, bias=sb["bW"][:, oc:oc + 1], scale=1.0))
            cs = sinc()
            bank_free[bk] = ("SC", cs)
            st["gt_free_mm"] = cm

        wt("tensor", VE, st.get("gt_ve", 0))
        fm_proj(512, lambda dc, c0, gw: ds["w0h"][4][dc][:, c0:c0 + gw],
                lambda dc, c0, gw: ds["w0l"][4][dc][:, c0:c0 + gw],
                list(range(8)), ev_ffn0, rhsof=lambda dc: dc * 512, lora2=lora0)

        cbv = build_bvec(512, t0, True)
        st["ar2_free_mm"] = MM.v
        for hp in range(NCH):
            def mk0(kt, g=blk):
                base = 4 * g
                if kt >= base:
                    return ("c_masksb", (kt - base) * 512)
                return None
            qts = [dict(qcol=hp * 512, ocol=hp * 512, bvoff=0, kt_lim=4 * blk + 4,
                        mask_fn=mk0, qh="AR3h", ql="AR3l", _started=False)]
            attention(0, hp, qts, cbv, None)
        st["ar3_free_mm"] = MM.v

        def ev_o(bk, oc, tt, cm):
            wt("vector", MM, cm)
            wt("vector", VE, st.get("res_ve", 0))
            em("vector", lambda e, bk=bk, oc=oc: e.scalar_tensor_tensor(
                out=sb["hrow"][:], in0=pb[bk][:], scalar=0.0,
                in1=sb["gA"][:, oc * 512:oc * 512 + 512],
                op0=ALU.add, op1=ALU.mult))
            vlock()
            em("vector", lambda e, oc=oc: e.tensor_tensor(
                out=sb["AR1"][:, oc * 512:oc * 512 + 512],
                in0=sb["AR1"][:, oc * 512:oc * 512 + 512],
                in1=sb["hrow"][:], op=ALU.add))
            cv = vinc()
            bank_free[bk] = ("VE", cv)
            st["gA_free_ve"] = cv
            st["res_ve"] = cv
            st["b12_ve"] = max(st["b12_ve"], cv)
            wt("gpsimd", VE, cv)
            cs = dma("gpsimd", ds["x1s"][blk][:, oc * 512:oc * 512 + 512],
                     sb["AR1"][:, oc * 512:oc * 512 + 512], SP)
            st["x1spill_sp"] = cs

        wt("tensor", VE, st.get("o_ve", 0))
        fm_proj(512, lambda dc, c0, gw: ds["w0h"][3][dc][:, c0:c0 + gw],
                lambda dc, c0, gw: ds["w0l"][3][dc][:, c0:c0 + gw],
                list(range(8)), ev_o, rhs=("AR4h", "AR4l"), rhsof=lambda dc: dc * 512)
        st["ar4_free_mm"] = MM.v

    em("vector", lambda e: e.memset(sb["nk16"][:], 0.0))
    em("vector", lambda e: e.memset(sb["nkst2"][:], 0.0))
    vinc()
    for blk in range(8):
        t0 = blk * 512
        wt("sync", SP, st["x1spill_sp"])
        wt("sync", VE, st["b12_ve"])
        cx = dma("sync", sb["AR1"][:, 0:4096], ds["x1s"][blk], LD)
        wt("vector", LD, cx)
        ln(512, lambda c: c * 512, lambda c: c * 512)
        cu = lora_u(512, "lath", "latl", ("AR2h", "AR2l"), lambda dc: dc * 512)
        wt("tensor", VE, cu)

        def mk_lora_fm(colbase):
            def f(i, oc, tt):
                col = colbase + oc * 128
                return [(("lbth", col, 128), ("uh", tt * 512)),
                        (("lbth", col, 128), ("ul", tt * 512)),
                        (("lbtl", col, 128), ("uh", tt * 512))]
            return f

        def ev_k1(bk, oc, tt, cm):
            wt("vector", SP, st["ev_sp"])
            cv = evac_split(bk, cm, "evh", "evl", 0, sb["battn"][:, 8 + oc:8 + oc + 1])
            wt("gpsimd", VE, cv)
            dma("gpsimd", ds["k1h"][oc][:, t0:t0 + 512], sb["evh"][:], SP)
            cs = dma("gpsimd", ds["k1l"][oc][:, t0:t0 + 512], sb["evl"][:], SP)
            st["ev_sp"] = cs
            st["kvspill_sp"] = cs
            bank_free[bk] = ("VE", cv)
            stats_k(bk, oc, 0, "evh", 0)

        fm_proj(512, lambda dc, c0, gw: ds["w1h"][dc][:, 1024 + c0:1024 + c0 + gw],
                lambda dc, c0, gw: ds["w1l"][dc][:, 1024 + c0:1024 + c0 + gw],
                list(range(8)), ev_k1, rhsof=lambda dc: dc * 512,
                lora2=mk_lora_fm(1024))
        st["u_free_mm"] = MM.v

        for half in range(2):
            cl = wload([ds["w1h"][dc][:, 2048 + half * 512:2048 + half * 512 + 512]
                        for dc in range(NCH)],
                       [ds["w1l"][dc][:, 2048 + half * 512:2048 + half * 512 + 512]
                        for dc in range(NCH)], 512)
            wt("tensor", WS, cl)
            for tk in range(4):
                bk = tk % 2
                wt_bank(bk)
                pairs = []
                for dc in range(NCH):
                    xno = dc * 512 + tk * 128
                    pairs.append((("AR2h", xno, 128), ("wsh", dc * 512)))
                    pairs.append((("AR2l", xno, 128), ("wsl", dc * 512)))
                    pairs.append((("AR2h", xno, 128), ("wsl", dc * 512)))
                pairs = []
                for dc in range(NCH):
                    xno = dc * 512 + tk * 128
                    pairs.append((("AR2h", xno, 128), ("wsh", dc * 512)))
                    pairs.append((("AR2l", xno, 128), ("wsh", dc * 512)))
                    pairs.append((("AR2h", xno, 128), ("wsl", dc * 512)))
                pairs.append((("uh", tk * 128, 128), ("lbthv", half * 512)))
                pairs.append((("ul", tk * 128, 128), ("lbthv", half * 512)))
                pairs.append((("uh", tk * 128, 128), ("lbtlv", half * 512)))
                n = len(pairs)
                for j, (aa, bb) in enumerate(pairs):
                    an, ao = aa[0], aa[1]
                    bn, bo = bb
                    if bn == "lbthv":
                        bn, bo = "lbth", 2048 + bo
                    if bn == "lbtlv":
                        bn, bo = "lbtl", 2048 + bo
                    em("tensor", lambda e, an=an, ao=ao, bn=bn, bo=bo, bk=bk,
                       f=(j == 0), l=(j == n - 1): e.matmul(
                        pb[bk][:], sb[an][:, ao:ao + 128], sb[bn][:, bo:bo + 512],
                        start=f, stop=l))
                cm = minc()
                st["wsh_mm"] = cm
                st["u_free_mm"] = cm
                wt("vector", MM, cm)
                wt("vector", SP, st["ev_sp"])
                em("vector", lambda e, bk=bk: e.tensor_copy(out=sb["evh"][:], in_=pb[bk][:]))
                vlock()
                em("vector", lambda e, bk=bk: e.tensor_tensor(
                    out=sb["evl"][:], in0=pb[bk][:], in1=sb["evh"][:], op=ALU.subtract))
                cv = vinc()
                bank_free[bk] = ("VE", cv)
                wt("gpsimd", VE, cv)
                cs = 0
                for c4 in range(4):
                    cpr = half * 4 + c4
                    dma("gpsimd", ds["v1h"][cpr][t0 + tk * 128:t0 + tk * 128 + 128, :],
                        sb["evh"][:, c4 * 128:c4 * 128 + 128], SP)
                    cs = dma("gpsimd", ds["v1l"][cpr][t0 + tk * 128:t0 + tk * 128 + 128, :],
                             sb["evl"][:, c4 * 128:c4 * 128 + 128], SP)
                st["ev_sp"] = cs
                st["kvspill_sp"] = cs
        st["ar2_free_mm"] = MM.v
        finish_kstats()

    for s in range(4):
        wt("sync", SP, st["x1spill_sp"])
        st["x1spill_sp"] = 0
        wt("sync", VE, st["b12_ve"])
        cx = dma_ri("sync", sb["AR1"][:, 0:4096],
                    (lambda idx: dsh["x1s"][idx]), G0[s], G1[s], LD)
        wt("vector", LD, cx)
        ln(512, lambda c: c * 512, lambda c: c * 512)
        cu = lora_u(512, "lath", "latl", ("AR2h", "AR2l"), lambda dc: dc * 512)
        wt("tensor", VE, cu)

        def ev_q1(bk, oc, tt, cm):
            cv = evac_split(bk, cm, "AR3h", "AR3l", oc * 512, sb["battn"][:, oc:oc + 1])
            bank_free[bk] = ("VE", cv)
            stats_q(bk, oc, 0, "AR3h", oc * 512)

        wt("vector", MM, st["ar3_free_mm"])
        st["ar3_free_mm"] = 0

        def l1q_lora(i, oc, tt):
            return [(("lbth", oc * 128, 128), ("uh", 0)),
                    (("lbth", oc * 128, 128), ("ul", 0)),
                    (("lbtl", oc * 128, 128), ("uh", 0))]

        fm_proj(512, lambda dc, c0, gw: ds["w1h"][dc][:, c0:c0 + gw],
                lambda dc, c0, gw: ds["w1l"][dc][:, c0:c0 + gw],
                list(range(8)), ev_q1, rhsof=lambda dc: dc * 512, lora2=l1q_lora)
        st["u_free_mm"] = MM.v

        for hcp in range(12):
            cl = wload([ds["wgu"][hcp][dc] for dc in range(NCH)], None, 512)
            wt("tensor", WS, cl)
            for jj in range(2):
                hc = 2 * hcp + jj
                bkg = jj * 2
                bku = jj * 2 + 1
                wt_bank(bkg)
                wt_bank(bku)
                mms = []
                for tgt, bk, co in ((0, bkg, jj * 128), (1, bku, 256 + jj * 128)):
                    pairs = []
                    for dc in range(NCH):
                        pairs.append((("wsh", dc * 512 + co), ("AR2h", dc * 512)))
                        pairs.append((("wsh", dc * 512 + co), ("AR2l", dc * 512)))
                    n = len(pairs)
                    for j2, ((an, ao), (bn, bo)) in enumerate(pairs):
                        em("tensor", lambda e, an=an, ao=ao, bn=bn, bo=bo, bk=bk,
                           f=(j2 == 0), l=(j2 == n - 1): e.matmul(
                            pb[bk][:], sb[an][:, ao:ao + 128], sb[bn][:, bo:bo + 512],
                            start=f, stop=l))
                    mms.append(minc())
                st["wsh_mm"] = mms[-1]
                wt("scalar", MM, mms[0])
                wt("scalar", VE, max(st.get("hrow_ve", 0), st.get("res_ve", 0)))
                wt("scalar", OT, st.get("out_ot", 0))
                em("scalar", lambda e, bkg=bkg: e.activation(
                    out=sb["hrow"][:], in_=pb[bkg][:], func=AF.Silu, bias=0.0, scale=S127))
                cs = sinc()
                bank_free[bkg] = ("SC", cs)
                wt("vector", SC, cs)
                wt("vector", MM, mms[1])
                wt("vector", SP, st["ev_sp"])
                em("vector", lambda e, bku=bku: e.tensor_tensor(
                    out=sb["hrow"][:], in0=sb["hrow"][:], in1=pb[bku][:], op=ALU.mult))
                vlock()
                em("vector", lambda e: e.tensor_copy(out=sb["evh"][:], in_=sb["hrow"][:]))
                vlock()
                em("vector", lambda e: e.tensor_tensor(
                    out=sb["evl"][:], in0=sb["hrow"][:], in1=sb["evh"][:], op=ALU.subtract))
                cv = vinc()
                bank_free[bku] = ("VE", cv)
                st["hrow_ve"] = cv
                wt("gpsimd", VE, cv)
                dma("gpsimd", ds["hsh"][hc], sb["evh"][:], SP)
                cs2 = dma("gpsimd", ds["hsl"][hc], sb["evl"][:], SP)
                st["ev_sp"] = cs2
                st["hspill_sp"] = cs2

        for ocg in range(2):
            for hg in range(3):
                cl = wload([ds["wd"][hg * 8 + ii][:, ocg * 512:ocg * 512 + 512] for ii in range(NCH)],
                           None, 512)
                wt("tensor", WS, cl)
                for ii in range(8):
                    hc = hg * 8 + ii
                    hb = hc % 2
                    KVp = KV0 if hb == 0 else KV1
                    wt("sync", MM, st["kv_mm"][hb])
                    if hc == 0 and ocg == 0:
                        wt("sync", SP, st.get("hspill_sp", 0))
                    dma("sync", sb["kbh"][:, hb * 512:hb * 512 + 512], ds["hsh"][hc], KVp)
                    chl = dma("sync", sb["kbl"][:, hb * 512:hb * 512 + 512], ds["hsl"][hc], KVp)
                    wt("tensor", KVp, chl)
                    for ocf in range(4):
                        bk = 4 + ocf
                        if hc == 0:
                            wt_bank(bk)
                        for pi, hn in enumerate(("kbh", "kbl")):
                            em("tensor", lambda e, hn=hn, hb=hb, ii=ii, ocf=ocf, bk=bk,
                               f=(hc == 0 and pi == 0), l=(hc == 23 and pi == 1): e.matmul(
                                pb[bk][:], sb["wsh"][:, ii * 512 + ocf * 128:ii * 512 + ocf * 128 + 128],
                                sb[hn][:, hb * 512:hb * 512 + 512], start=f, stop=l))
                    cmu = minc()
                    st["kv_mm"][hb] = cmu
                cm = cmu
                st["wsh_mm"] = cm
            for ocf in range(4):
                oc = ocg * 4 + ocf
                bk = 4 + ocf
                wt("scalar", MM, cm)
                wt("scalar", VE, st["gA_free_ve"])
                st["gA_free_ve"] = 0
                em("scalar", lambda e, bk=bk, oc=oc: e.activation(
                    out=sb["gA"][:, oc * 512:oc * 512 + 512], in_=pb[bk][:],
                    func=AF.Silu, bias=0.0, scale=S127 * S127))
                cs = sinc()
                bank_free[bk] = ("SC", cs)

        wt("sync", MM, st["wsh_mm"])
        st["wsh_mm"] = 0
        cm0 = dma_ri("sync", sb["maskb"][:, 0:4096],
                     (lambda idx: dinh["l1masks"][idx]), s, 4 + s, MK)
        st["maskb_ld"] = cm0
        cbv = build_bvec(512, 0, False)
        st["ar2_free_mm"] = MM.v
        wt("vector", MK, st["maskb_ld"])

        for hp in range(NCH):
            def mk1(kt, s=s):
                w0 = BND[s] - 8
                if kt >= w0:
                    return ("maskb", (kt - w0) * 512)
                return None
            qts = [dict(qcol=hp * 512, ocol=hp * 512, bvoff=0, kt_lim=BND[s], mask_fn=mk1,
                        qh="AR3h", ql="AR3l", _started=False)]
            attention(1, hp, qts, cbv, 16 + hp)
        st["ar3_free_mm"] = MM.v
        st["wsl_ve"] = st["mask_used_ve"]

        wt("tensor", VE, st.get("o_ve", 0))
        cu = lora_u(512, "laph", "lapl", ("AR4h", "AR4l"), lambda dc: dc * 512)
        wt("tensor", VE, cu)

        def l1p_lora(i, oc, tt):
            return [(("lbph", oc * 128, 128), ("uh", 0)),
                    (("lbph", oc * 128, 128), ("ul", 0)),
                    (("lbpl", oc * 128, 128), ("uh", 0))]

        def ev_p1(bk, oc, tt, cm):
            wt("vector", MM, cm)
            wt("vector", VE, st.get("res_ve", 0))
            em("vector", lambda e, bk=bk, oc=oc: e.scalar_tensor_tensor(
                out=sb["hrow"][:], in0=pb[bk][:], scalar=sb["bproj"][:, oc:oc + 1],
                in1=sb["gA"][:, oc * 512:oc * 512 + 512], op0=ALU.add, op1=ALU.mult))
            vlock()
            em("vector", lambda e, oc=oc: e.tensor_tensor(
                out=sb["hrow"][:], in0=sb["AR1"][:, oc * 512:oc * 512 + 512],
                in1=sb["hrow"][:], op=ALU.add))
            cv = vinc()
            bank_free[bk] = ("VE", cv)
            st["gA_free_ve"] = cv
            st["res_ve"] = cv
            st["b12_ve"] = max(st["b12_ve"], cv)
            wt("gpsimd", VE, cv)
            dma("gpsimd", out[oc][:, s * 512:s * 512 + 512], sb["hrow"][:], OT)

        def ev_p1_guarded(bk, oc, tt, cm):
            wt("vector", OT, st.get("out_ot", 0))
            ev_p1(bk, oc, tt, cm)
            st["out_ot"] = OT.v

        fm_proj(512, lambda dc, c0, gw: ds["w1h"][dc][:, 3072 + c0:3072 + c0 + gw],
                lambda dc, c0, gw: ds["w1l"][dc][:, 3072 + c0:3072 + c0 + gw],
                list(range(8)), ev_p1_guarded, rhs=("AR4h", "AR4l"),
                rhsof=lambda dc: dc * 512, lora2=l1p_lora)
        st["ar4_free_mm"] = MM.v
        st["u_free_mm"] = MM.v

    wt("gpsimd", OT, OT.v)

    with nc.Block() as block:
        @block.sync
        def _(eng):
            for f in P["sync"]:
                f(eng)

        @block.gpsimd
        def _(eng):
            for f in P["gpsimd"]:
                f(eng)

        @block.tensor
        def _(eng):
            for f in P["tensor"]:
                f(eng)

        @block.vector
        def _(eng):
            for f in P["vector"]:
                f(eng)

        @block.scalar
        def _(eng):
            for f in P["scalar"]:
                f(eng)

    return nc


_NC_CACHE = []


def kernel(**inp):
    x = np.ascontiguousarray(np.asarray(inp["L0_attn_x"], dtype=np.float32))
    f32 = lambda a: np.ascontiguousarray(np.asarray(a, dtype=np.float32))
    i32 = lambda a: np.ascontiguousarray(np.asarray(a, dtype=np.int32))

    A = f32(inp["L0_ffn_A"])
    Bm = f32(inp["L0_ffn_B"])
    R = f32(inp["L0_ffn_R"])
    RA = np.concatenate([R.T, A.transpose(2, 0, 1).reshape(DM, 128)], axis=1)
    RA = np.ascontiguousarray(RA.reshape(NCH, 128, 136).transpose(1, 0, 2).reshape(128, 1088))

    base = dict(
        Wq=f32(inp["L0_attn_Wq"]).reshape(NCH, 128, DM),
        Wk=f32(inp["L0_attn_Wk"]).reshape(NCH, 128, DM),
        Wv=f32(inp["L0_attn_Wv"]).reshape(NCH, 128, DM),
        Wo=f32(inp["L0_attn_Wo"]).reshape(NCH, 128, DM),
        W_T=np.ascontiguousarray(f32(inp["L0_ffn_W"]).T).reshape(NCH, 128, DM),
        bW=np.ascontiguousarray(f32(inp["L0_ffn_b"]).reshape(NCH, 128).T),
        RA=RA,
        Bstk=np.ascontiguousarray(Bm.transpose(0, 2, 1).reshape(128, DM)),
        w_attnT=np.ascontiguousarray(f32(inp["L1_attn_w_attn"]).T).reshape(NCH, 128, 3072),
        battn=np.ascontiguousarray(f32(inp["L1_attn_b_attn"]).reshape(24, 128).T),
        la_attnT=np.ascontiguousarray(
            f32(inp["L1_attn_la_attn"]).T.reshape(NCH, 128, 32).transpose(1, 0, 2).reshape(128, 256)),
        lb_attnT=np.ascontiguousarray(f32(inp["L1_attn_lb_attn"]).T),
        w_projT=np.ascontiguousarray(f32(inp["L1_attn_w_proj"]).T).reshape(NCH, 128, DM),
        bproj=np.ascontiguousarray(f32(inp["L1_attn_b_proj"]).reshape(NCH, 128).T),
        la_projT=np.ascontiguousarray(
            f32(inp["L1_attn_la_proj"]).T.reshape(NCH, 128, 32).transpose(1, 0, 2).reshape(128, 256)),
        lb_projT=np.ascontiguousarray(f32(inp["L1_attn_lb_proj"]).T),
        wgT=np.ascontiguousarray(i32(inp["L1_ffn_w_gate"]).T).reshape(NCH, 128, 3072),
        wuT=np.ascontiguousarray(i32(inp["L1_ffn_w_up"]).T).reshape(NCH, 128, 3072),
        wdT=np.ascontiguousarray(i32(inp["L1_ffn_w_down"]).T).reshape(24, 128, DM),
    )
    cst = host_constants()
    l1m = np.zeros((8, 128, 4096), dtype=ml_dtypes.bfloat16)
    for role in range(2):
        for s in range(4):
            for j in range(8):
                l1m[role * 4 + s][:, j * 512:(j + 1) * 512] = cst["c_masks"][l1_mask_idx(role, s, j)]
    base["l1masks"] = l1m
    base.update({k: np.ascontiguousarray(v) for k, v in cst.items()})

    in_maps = []
    for core in range(8):
        seq = core // 2
        m = dict(base)
        m["x_fm"] = np.ascontiguousarray(x[seq].T).reshape(NCH, 128, T)
        in_maps.append(m)

    nc = build_program()
    res = run_bass_kernel_spmd(nc, in_maps, core_ids=list(range(8)))

    outp = np.empty((4, T, DM), dtype=np.float32)
    for core in range(8):
        seq, role = core // 2, core % 2
        G = G0 if role == 0 else G1
        fm = np.asarray(res.results[core]["out"]).reshape(DM, 2048)
        for s, g in enumerate(G):
            outp[seq, g * 512:(g + 1) * 512, :] = fm[:, s * 512:(s + 1) * 512].T
    return outp


# version 10
# speedup vs baseline: 1.3344x; 1.0044x over previous
import math
import numpy as np
import ml_dtypes
import concourse.bass as bass
import concourse.mybir as mybir
from concourse.bass_utils import run_bass_kernel_spmd

F32 = mybir.dt.float32
BF16 = mybir.dt.bfloat16
I32 = mybir.dt.int32
AF = mybir.ActivationFunctionType
ALU = mybir.AluOpType
AX = mybir.AxisListType.X

T = 4096
DM = 1024
NCH = 8
NEG = -1.0e30
S127 = float(np.float32(1.0 / 127.0))

G0 = [0, 3, 4, 7]
G1 = [1, 2, 5, 6]
BND = [8, 16, 24, 32]
ENGS = ("sync", "gpsimd", "tensor", "vector", "scalar")


def _slopes16():
    start = 2.0 ** (-2.0 ** (-(math.log2(16) - 3)))
    return [start * (start ** i) for i in range(16)]


def host_constants():
    sl = np.array(_slopes16(), dtype=np.float64)
    k = np.arange(T, dtype=np.float64)
    alibi = sl[:, None] * (k[None, :] - 2047.0)
    a = alibi.reshape(16, 32, 128).transpose(2, 0, 1).reshape(128, 512)
    a_hi = a.astype(np.float32)
    a_lo = (a - a_hi.astype(np.float64)).astype(np.float32)
    q = np.arange(T, dtype=np.float64)
    aq8 = (8.0 * sl[:, None] * (q[None, :] - 2047.0)).astype(np.float32)
    masks = np.zeros((6, 128, 512), dtype=np.float64)
    kk = np.arange(128)[:, None]
    qq = np.arange(512)[None, :]
    for m in range(4):
        masks[1 + m] = np.where(qq >= 128 * m + kk, 0.0, NEG)
    masks[5] = NEG
    sel2 = np.zeros((2, 128), dtype=np.float32)
    sel2[0, 0:64] = 1.0
    sel2[1, 64:128] = 1.0
    onesh = np.zeros((128, 2), dtype=np.float32)
    onesh[0:64, 0] = 1.0
    onesh[64:128, 1] = 1.0
    return dict(c_alibi_hi=a_hi, c_alibi_lo=a_lo, c_aq8=aq8,
                c_masks=masks.astype(ml_dtypes.bfloat16),
                c_sel2=sel2, c_ident=np.eye(128, dtype=ml_dtypes.bfloat16),
                c_identf=np.eye(128, dtype=np.float32), c_onesh=onesh)


def l1_mask_idx(role, s, j):
    g = (G0 if role == 0 else G1)[s]
    prefix = 4 * (g + 1)
    kt = BND[s] - 8 + j
    if kt < prefix - 4:
        return 0
    if kt < prefix:
        return 1 + (kt - (prefix - 4))
    return 5


class Ctr:
    def __init__(self, sem):
        self.sem = sem
        self.v = 0


def build_program():
    nc = bass.Bass("TRN2", target_bir_lowering=False, debug=False)
    P = {e: [] for e in ENGS}

    def em(eng, fn):
        P[eng].append(fn)

    din = {}

    dinh = {}

    def DI(name, shape, dt=F32):
        h = nc.dram_tensor(name, shape, dt, kind="ExternalInput")
        dinh[name] = h
        din[name] = h.ap()

    DI("x_fm", [NCH, 128, T])
    DI("Wq", [NCH, 128, DM]); DI("Wk", [NCH, 128, DM]); DI("Wv", [NCH, 128, DM])
    DI("Wo", [NCH, 128, DM]); DI("W_T", [NCH, 128, DM])
    DI("bW", [128, 8]); DI("RA", [128, 1088]); DI("Bstk", [128, DM])
    DI("w_attnT", [NCH, 128, 3072]); DI("battn", [128, 24])
    DI("la_attnT", [128, 256]); DI("lb_attnT", [32, 3072])
    DI("w_projT", [NCH, 128, DM]); DI("bproj", [128, 8])
    DI("la_projT", [128, 256]); DI("lb_projT", [32, DM])
    DI("wgT", [NCH, 128, 3072], I32); DI("wuT", [NCH, 128, 3072], I32)
    DI("wdT", [24, 128, DM], I32)
    DI("c_alibi_hi", [128, 512]); DI("c_alibi_lo", [128, 512])
    DI("c_aq8", [16, T]); DI("c_masks", [6, 128, 512], BF16)
    DI("c_sel2", [2, 128]); DI("c_ident", [128, 128], BF16)
    DI("c_identf", [128, 128])
    DI("c_onesh", [128, 2])
    DI("l1masks", [8, 128, 4096], BF16)

    out = nc.dram_tensor("out", [NCH, 128, 2048], F32, kind="ExternalOutput").ap()

    ds = {}

    dsh = {}

    def DS(name, shape, dt):
        h = nc.dram_tensor(name, shape, dt)
        dsh[name] = h
        ds[name] = h.ap()

    DS("w0h", [5, NCH, 128, DM], BF16); DS("w0l", [5, NCH, 128, DM], BF16)
    DS("w1h", [NCH, 128, 4096], BF16); DS("w1l", [NCH, 128, 4096], BF16)
    DS("wgu", [12, NCH, 128, 512], BF16)
    DS("wd", [24, 128, DM], BF16)
    for L in (0, 1):
        DS(f"k{L}h", [NCH, 128, T], BF16); DS(f"k{L}l", [NCH, 128, T], BF16)
        DS(f"v{L}h", [NCH, T, 128], BF16); DS(f"v{L}l", [NCH, T, 128], BF16)
    DS("x1s", [8, 128, 4096], F32)
    DS("hsh", [24, 128, 512], BF16); DS("hsl", [24, 128, 512], BF16)

    sb = {}

    def SB(name, shape, dt):
        sb[name] = nc.alloc_sbuf_tensor("sb_" + name, shape, dt)

    SB("AR1", [128, 4096], F32)
    SB("AR2h", [128, 4096], BF16); SB("AR2l", [128, 4096], BF16)
    SB("AR3h", [128, 4096], BF16); SB("AR3l", [128, 4096], BF16)
    SB("AR4h", [128, 4096], BF16); SB("AR4l", [128, 4096], BF16)
    SB("gA", [128, 4096], F32)
    SB("wsh", [128, 4096], BF16); SB("wsl", [128, 4096], BF16)
    SB("pf", [128, 2048], F32)
    SB("ph", [128, 3072], BF16); SB("pl", [128, 3072], BF16)
    SB("kbh", [128, 1024], BF16); SB("kbl", [128, 1024], BF16)
    SB("vbh", [128, 1024], BF16); SB("vbl", [128, 1024], BF16)
    SB("evh", [128, 512], BF16); SB("evl", [128, 512], BF16)
    SB("btile", [128, 1024], F32)
    SB("rtile", [128, 512], F32)
    SB("otmp", [128, 512], F32)
    SB("tsq", [128, 512], F32)
    SB("sqb", [128, 512], BF16)
    SB("ttile", [128, 512], F32)
    SB("hrow", [128, 512], F32)
    SB("c_alibi", [128, 512], F32)
    SB("c_masksb", [128, 2048], BF16)
    SB("c_onesh", [128, 2], BF16)
    SB("c_ones_col", [128, 1], BF16)
    SB("c_ones_f32", [128, 1], F32)
    SB("c_zero", [128, 1], F32)
    SB("bvb", [64, 512], BF16)
    SB("c_o1b", [64, 128], BF16)
    SB("c_identf", [128, 128], F32)
    SB("rah", [128, 1088], BF16); SB("ral", [128, 1088], BF16)
    SB("bsh", [128, DM], BF16); SB("bsl", [128, DM], BF16)
    SB("lath", [128, 256], BF16); SB("latl", [128, 256], BF16)
    SB("laph", [128, 256], BF16); SB("lapl", [128, 256], BF16)
    SB("bW", [128, 8], F32); SB("battn", [128, 24], F32); SB("bproj", [128, 8], F32)
    SB("ghf", [128, 128], F32)
    SB("ghh", [128, 128], BF16); SB("ghl", [128, 128], BF16)
    SB("gth", [128, 512], BF16); SB("gtl", [128, 512], BF16)
    SB("gb0", [128, 8], F32); SB("gb1", [128, 8], F32); SB("gb2", [128, 8], F32)
    SB("gb3", [128, 8], F32)
    SB("r8", [128, 1], F32); SB("negm", [128, 1], F32)
    SB("STA", [128, 6464], F32)
    SB("LORA", [128, 10240], BF16)
    class View:
        def __init__(self, t, ro, co, nr, ncl):
            self.t, self.ro, self.co, self.nr, self.ncl = t, ro, co, nr, ncl

        def __getitem__(self, idx):
            if not isinstance(idx, tuple):
                idx = (idx, slice(None))
            rs, cs = idx
            r0 = self.ro + (rs.start or 0)
            r1 = self.ro + (self.nr if rs.stop is None else rs.stop)
            c0 = self.co + (cs.start or 0)
            c1 = self.co + (self.ncl if cs.stop is None else cs.stop)
            return self.t[r0:r1, c0:c1]

    def VW(name, base, ro, co, nr, ncl):
        sb[name] = View(sb[base], ro, co, nr, ncl)

    VW("c_aq8", "STA", 0, 0, 16, 512)
    VW("nq2", "STA", 0, 512, 16, 1024)
    VW("bvec", "STA", 0, 1536, 16, 1024)
    VW("nkst2", "STA", 0, 2560, 16, 2)
    VW("nk16", "STA", 0, 2564, 16, 1)
    VW("nkmax", "STA", 0, 2568, 16, 1)
    VW("nktmp", "STA", 0, 2572, 16, 1)
    VW("lnrow", "STA", 0, 2576, 1, 2048)
    VW("bvrow", "STA", 0, 4624, 1, 1024)
    VW("bvrow2", "STA", 0, 4624, 33, 512)
    VW("drows", "STA", 0, 5648, 2, 512)
    VW("c_sel2", "STA", 0, 6160, 2, 128)
    VW("c_ones_row", "STA", 0, 6288, 1, 128)
    VW("lbth", "LORA", 0, 0, 32, 3072)
    VW("lbtl", "LORA", 0, 3072, 32, 3072)
    VW("uh", "LORA", 0, 6144, 32, 1024)
    VW("ul", "LORA", 0, 7168, 32, 1024)
    VW("lbph", "LORA", 0, 8192, 32, 1024)
    VW("lbpl", "LORA", 0, 9216, 32, 1024)
    VW("wbuf", "pf", 0, 0, 128, 2048)
    VW("maskb", "wsl", 0, 0, 128, 4096)
    VW("b1sb", "rtile", 0, 0, 128, 512)
    VW("b2sb", "otmp", 0, 0, 128, 512)
    VW("cw", "gA", 0, 0, 128, 3072)
    VW("urow", "btile", 0, 0, 32, 1024)

    pb = [nc.alloc_psum_tensor(f"pb{i}", [128, 512], F32) for i in range(8)]

    import contextlib
    es = contextlib.ExitStack()
    C = {}
    for name in ("LD", "SP", "MM", "VE", "SC", "OT", "WS", "KV0", "KV1", "CT", "MK",
                 "WSA", "WSB", "SPA", "SPB", "GP"):
        C[name] = Ctr(es.enter_context(nc.semaphore(name)))
    nc._keep = es
    LD, SP, MM, VE, SC, OT = (C[k] for k in ("LD", "SP", "MM", "VE", "SC", "OT"))
    WS, KV0, KV1, CT, MK = (C[k] for k in ("WS", "KV0", "KV1", "CT", "MK"))
    WSA, WSB = C["WSA"], C["WSB"]
    SPA, SPB = C["SPA"], C["SPB"]

    def dma(q, dst, src, ctr, role=None):
        ctr.v += 16
        if role is None:
            em(q, lambda e, d=dst, s=src, sm=ctr.sem: e.dma_start(out=d, in_=s).then_inc(sm, 16))
        else:
            em(q, lambda e, d=dst, s=src, sm=ctr.sem, r=role: e.dma_start(
                out=d, in_=s, cond=(e.partition_id() % 2 == r)).then_inc(sm, 16))
        return ctr.v

    _role_sv = {}

    def dma_ri(q, dst, src_fn, i0, i1, ctr):
        ctr.v += 16
        def go(e, d=dst, f=src_fn, a=i0, b=i1, sm=ctr.sem):
            key = id(e)
            if key not in _role_sv:
                _role_sv[key] = e.partition_id() % 2
            ik = (key, a, b)
            if ik not in _role_sv:
                _role_sv[ik] = a + _role_sv[key] * (b - a)
            e.dma_start(out=d, in_=f(_role_sv[ik])).then_inc(sm, 16)
        em(q, go)
        return ctr.v

    def wt(eng, ctr, val):
        if val > 0:
            em(eng, lambda e, sm=ctr.sem, v=val: e.wait_ge(sm, v))

    def _attach_inc(eng, ctr):
        ctr.v += 1
        last = P[eng].pop()
        P[eng].append(lambda e, f=last, sm=ctr.sem: f(e).then_inc(sm, 1))
        return ctr.v

    def vinc():
        return _attach_inc("vector", VE)

    def vlock():
        cv = _attach_inc("vector", VE)
        wt("vector", VE, cv)
        return cv

    def slock():
        cs = _attach_inc("scalar", SC)
        wt("scalar", SC, cs)
        return cs

    def sinc():
        return _attach_inc("scalar", SC)

    def minc():
        return _attach_inc("tensor", MM)

    def ginc():
        return _attach_inc("gpsimd", C["GP"])

    st = {"wsh_mm": 0, "wsh_sp": 0, "kv_mm": [0, 0], "pf_mm": [0, 0, 0], "pS_sc": [0, 0, 0],
          "kst_ve": 0, "prep_spa": 0, "prep_spb": 0,
          "b12_ve": 0, "prep_par": 0, "prep_lastv": [0, 0], "prep_lastw": [0, 0],
          "mask_used_ve": 0, "bvb_mm": 0, "bvb_ve": 0,
          "drows_ve": 0, "nq2_ld": 0, "nkst_ld": 0, "bvec_sc": 0,
          "kvspill_sp": 0, "x1spill_sp": 0, "gA_free_ve": 0, "ar2_free_mm": 0,
          "ar3_free_mm": 0, "ar4_free_mm": 0, "gt_free_mm": 0, "u_free_mm": 0,
          "ev_sp": 0, "maskb_ve": 0, "hrow_mm": 0}
    CTRS = {"LD": LD, "SP": SP, "MM": MM, "VE": VE, "SC": SC, "OT": OT}
    bank_free = [("VE", 0)] * 8

    def wt_bank(i):
        kind, v = bank_free[i]
        wt("tensor", CTRS[kind], v)

    dma("sync", sb["cw"][:, 0:512], din["c_alibi_lo"], LD)
    dma("sync", sb["c_alibi"][:], din["c_alibi_hi"], LD)
    for mi in range(4):
        dma("sync", sb["c_masksb"][:, mi * 512:mi * 512 + 512], din["c_masks"][mi + 1], LD)
    dma("sync", sb["c_sel2"][:], din["c_sel2"], LD)
    dma("sync", sb["c_identf"][:], din["c_identf"], LD)
    dma("sync", sb["bW"][:], din["bW"], LD)
    dma("sync", sb["battn"][:], din["battn"], LD)
    c0 = dma("sync", sb["bproj"][:], din["bproj"], LD)

    wt("vector", LD, c0)
    for nm, v in (("c_ones_row", 1.0), ("c_ones_col", 1.0), ("c_ones_f32", 1.0),
                  ("c_o1b", 1.0), ("c_zero", 0.0), ("nkst2", 0.0), ("nk16", 0.0)):
        em("vector", lambda e, n=nm, vv=v: e.memset(sb[n][:], vv))
    em("vector", lambda e: e.tensor_tensor(out=sb["c_alibi"][:], in0=sb["c_alibi"][:],
                                           in1=sb["cw"][:, 0:512], op=ALU.add))
    st_cw = {"ve": vinc()}

    def split_const(hname, lname, src, npart, width):
        wt("sync", VE, st_cw["ve"])
        c = dma("sync", sb["cw"][0:npart, 0:width], src, LD)
        wt("vector", LD, c)
        em("vector", lambda e, h=hname, n=npart, w=width: e.tensor_copy(
            out=sb[h][0:n, 0:w], in_=sb["cw"][0:n, 0:w]))
        vlock()
        em("vector", lambda e, h=hname, l=lname, n=npart, w=width: e.tensor_tensor(
            out=sb[l][0:n, 0:w], in0=sb["cw"][0:n, 0:w], in1=sb[h][0:n, 0:w], op=ALU.subtract))
        st_cw["ve"] = vinc()
        return st_cw["ve"]

    split_const("rah", "ral", din["RA"], 128, 1088)
    split_const("bsh", "bsl", din["Bstk"], 128, DM)
    split_const("lath", "latl", din["la_attnT"], 128, 256)
    split_const("lbth", "lbtl", din["lb_attnT"], 32, 3072)
    split_const("laph", "lapl", din["la_projT"], 128, 256)
    split_const("lbph", "lbpl", din["lb_projT"], 32, DM)
    wt("sync", VE, st_cw["ve"])
    c = dma("sync", sb["cw"][:, 0:2], din["c_onesh"], LD)
    wt("vector", LD, c)
    em("vector", lambda e: e.tensor_copy(out=sb["c_onesh"][:], in_=sb["cw"][:, 0:2]))
    st_cw["ve"] = vinc()
    st["gA_free_ve"] = st_cw["ve"]
    st["prep_lastv"][0] = st_cw["ve"]
    st["prep_lastv"][1] = st_cw["ve"]

    def prep_split(dsth, dstl, src, ncols, is_int=False, dst_writes=None):
        for i in range((ncols + 1023) // 1024):
            w0 = i * 1024
            wid = min(1024, ncols - w0)
            par = st["prep_par"]
            st["prep_par"] = 1 - par
            KVp = KV0 if par == 0 else KV1
            wt("sync", VE, st["prep_lastv"][par])
            dstap = sb["wbuf"][:, par * 1024:par * 1024 + wid]
            if is_int:
                dstap = dstap.bitcast(I32)
            cl = dma("sync", dstap, src[:, w0:w0 + wid], KVp)
            wt("vector", KVp, cl)
            wt("vector", SPA if par == 0 else SPB, st["prep_lastw"][par])
            if is_int:
                em("vector", lambda e, p=par, w=wid: e.tensor_copy(
                    out=sb["wsh"][:, p * 1024:p * 1024 + w],
                    in_=sb["wbuf"][:, p * 1024:p * 1024 + w].bitcast(I32)))
            else:
                em("vector", lambda e, p=par, w=wid: e.tensor_copy(
                    out=sb["wsh"][:, p * 1024:p * 1024 + w],
                    in_=sb["wbuf"][:, p * 1024:p * 1024 + w]))
                vlock()
                em("vector", lambda e, p=par, w=wid: e.tensor_tensor(
                    out=sb["wsl"][:, p * 1024:p * 1024 + w],
                    in0=sb["wbuf"][:, p * 1024:p * 1024 + w],
                    in1=sb["wsh"][:, p * 1024:p * 1024 + w], op=ALU.subtract))
            cv = vinc()
            st["prep_lastv"][par] = cv
            wt("gpsimd", VE, cv)
            SPp = SPA if par == 0 else SPB
            cs = 0
            if dst_writes is None:
                dma("gpsimd", dsth[:, w0:w0 + wid], sb["wsh"][:, par * 1024:par * 1024 + wid], SPp)
                cs = dma("gpsimd", dstl[:, w0:w0 + wid], sb["wsl"][:, par * 1024:par * 1024 + wid], SPp)
            else:
                for (dap, c0_, cw_) in dst_writes:
                    lo = max(c0_, w0)
                    hi = min(c0_ + cw_, w0 + wid)
                    if lo < hi:
                        cs = dma("gpsimd", dap[:, lo - c0_:hi - c0_],
                                 sb["wsh"][:, par * 1024 + (lo - w0):par * 1024 + (hi - w0)], SPp)
            st["prep_lastw"][par] = cs

    for j, wn in enumerate(("Wq", "Wk", "Wv", "Wo", "W_T")):
        for c in range(NCH):
            prep_split(ds["w0h"][j][c], ds["w0l"][j][c], din[wn][c], DM)
    for c in range(NCH):
        prep_split(ds["w1h"][c][:, 0:3072], ds["w1l"][c][:, 0:3072], din["w_attnT"][c], 3072)
        prep_split(ds["w1h"][c][:, 3072:4096], ds["w1l"][c][:, 3072:4096], din["w_projT"][c], DM)
    for c in range(NCH):
        prep_split(None, None, din["wgT"][c], 3072, is_int=True,
                   dst_writes=[(ds["wgu"][hcp][c][:, 0:256], hcp * 256, 256) for hcp in range(12)])
        prep_split(None, None, din["wuT"][c], 3072, is_int=True,
                   dst_writes=[(ds["wgu"][hcp][c][:, 256:512], hcp * 256, 256) for hcp in range(12)])
    for hc in range(24):
        prep_split(None, None, din["wdT"][hc], DM, is_int=True,
                   dst_writes=[(ds["wd"][hc], 0, DM)])
    st["wsh_sp"] = SP.v
    st["prep_spa"] = SPA.v
    st["prep_spb"] = SPB.v

    def ln(Ttok, xof, xnof):
        for tt in range(Ttok // 512):
            lastmm = 0
            for c in range(NCH):
                xo = xof(c) + tt * 512
                wt("vector", MM, lastmm)
                em("vector", lambda e, xo=xo: e.tensor_tensor(
                    out=sb["tsq"][:], in0=sb["AR1"][:, xo:xo + 512],
                    in1=sb["AR1"][:, xo:xo + 512], op=ALU.mult))
                cv = vinc()
                wt("tensor", VE, cv)
                if c == 0:
                    wt_bank(0)
                    wt_bank(1)
                em("tensor", lambda e, xo=xo, f=(c == 0), l=(c == NCH - 1): e.matmul(
                    pb[0][0:1, :], sb["c_ones_f32"][:], sb["AR1"][:, xo:xo + 512],
                    start=f, stop=l))
                em("tensor", lambda e, f=(c == 0), l=(c == NCH - 1): e.matmul(
                    pb[1][0:1, :], sb["c_ones_f32"][:], sb["tsq"][:],
                    start=f, stop=l))
                lastmm = minc()
            wt("vector", MM, lastmm)
            em("vector", lambda e: e.tensor_scalar(out=sb["lnrow"][0:1, 0:512], in0=pb[0][0:1, :],
                                                   scalar1=1.0 / 1024.0, scalar2=None, op0=ALU.mult))
            em("vector", lambda e: e.tensor_scalar(out=sb["lnrow"][0:1, 512:1024],
                                                   in0=pb[1][0:1, :],
                                                   scalar1=1.0 / 1024.0, scalar2=None, op0=ALU.mult))
            vlock()
            em("vector", lambda e: e.tensor_tensor(out=sb["lnrow"][0:1, 1536:2048],
                                                   in0=sb["lnrow"][0:1, 0:512],
                                                   in1=sb["lnrow"][0:1, 0:512], op=ALU.mult))
            vlock()
            em("vector", lambda e: e.tensor_tensor(out=sb["lnrow"][0:1, 512:1024],
                                                   in0=sb["lnrow"][0:1, 512:1024],
                                                   in1=sb["lnrow"][0:1, 1536:2048], op=ALU.subtract))
            vlock()
            em("vector", lambda e: e.tensor_scalar(out=sb["lnrow"][0:1, 512:1024],
                                                   in0=sb["lnrow"][0:1, 512:1024],
                                                   scalar1=1e-5, scalar2=None, op0=ALU.add))
            cv = vinc()
            bank_free[0] = ("VE", cv)
            bank_free[1] = ("VE", cv)
            wt("scalar", VE, cv)
            em("scalar", lambda e: e.activation(out=sb["lnrow"][0:1, 1024:1536],
                                                in_=sb["lnrow"][0:1, 512:1024],
                                                func=AF.Sqrt, bias=0.0, scale=1.0))
            cs = sinc()
            wt("vector", SC, cs)
            em("vector", lambda e: e.reciprocal(out=sb["lnrow"][0:1, 1024:1536],
                                                in_=sb["lnrow"][0:1, 1024:1536]))
            vlock()
            em("vector", lambda e: e.tensor_tensor(out=sb["lnrow"][0:1, 1536:2048],
                                                   in0=sb["lnrow"][0:1, 0:512],
                                                   in1=sb["lnrow"][0:1, 1024:1536], op=ALU.mult))
            cv = vinc()
            wt("tensor", VE, cv)
            wt("tensor", VE, st["b12_ve"])
            wt_bank(2)
            wt_bank(3)
            em("tensor", lambda e: e.matmul(pb[2][:], sb["c_ones_row"][:],
                                            sb["lnrow"][0:1, 1024:1536], start=True, stop=True))
            em("tensor", lambda e: e.matmul(pb[3][:], sb["c_ones_row"][:],
                                            sb["lnrow"][0:1, 1536:2048], start=True, stop=True))
            cm = minc()
            wt("scalar", MM, cm)
            em("scalar", lambda e: e.copy(out=sb["b1sb"][:], in_=pb[2][:]))
            em("scalar", lambda e: e.copy(out=sb["b2sb"][:], in_=pb[3][:]))
            cs = sinc()
            bank_free[2] = ("SC", cs)
            bank_free[3] = ("SC", cs)
            wt("vector", SC, cs)
            wt("vector", MM, MM.v)
            for cp in range(NCH // 2):
                for half, tn in ((0, "tsq"), (1, "ttile")):
                    c = 2 * cp + half
                    xo = xof(c) + tt * 512
                    em("vector", lambda e, xo=xo, tn=tn: e.tensor_tensor(
                        out=sb[tn][:], in0=sb["AR1"][:, xo:xo + 512], in1=sb["b1sb"][:],
                        op=ALU.mult))
                vlock()
                for half, tn in ((0, "tsq"), (1, "ttile")):
                    em("vector", lambda e, tn=tn: e.tensor_tensor(
                        out=sb[tn][:], in0=sb[tn][:], in1=sb["b2sb"][:], op=ALU.subtract))
                vlock()
                for half, tn in ((0, "tsq"), (1, "ttile")):
                    c = 2 * cp + half
                    no = xnof(c) + tt * 512
                    em("vector", lambda e, no=no, tn=tn: e.tensor_copy(
                        out=sb["AR2h"][:, no:no + 512], in_=sb[tn][:]))
                vlock()
                for half, tn in ((0, "tsq"), (1, "ttile")):
                    c = 2 * cp + half
                    no = xnof(c) + tt * 512
                    em("vector", lambda e, no=no, tn=tn: e.tensor_tensor(
                        out=sb["AR2l"][:, no:no + 512], in0=sb[tn][:],
                        in1=sb["AR2h"][:, no:no + 512], op=ALU.subtract))
                vlock()
            st["b12_ve"] = vinc()

    def wload(srcs_h, srcs_l, gw):
        wt("sync", MM, st["wsh_mm"])
        wt("sync", MM, st.get("wshp_mm", [0, 0])[0])
        wt("sync", MM, st.get("wshp_mm", [0, 0])[1])
        wt("sync", SP, st["wsh_sp"])
        wt("sync", SPA, st.pop("prep_spa", 0))
        wt("sync", SPB, st.pop("prep_spb", 0))
        wt("sync", VE, st.get("wsl_ve", 0))
        st["wsl_ve"] = 0
        st["wsh_sp"] = 0
        cl = 0
        for dc in range(NCH):
            cl = dma("sync", sb["wsh"][:, dc * 512:dc * 512 + gw], srcs_h[dc], WS)
            if srcs_l is not None:
                cl = dma("sync", sb["wsl"][:, dc * 512:dc * 512 + gw], srcs_l[dc], WS)
        return cl

    def fm_proj(Ttok, wh_fn, wl_fn, oc_list, evac, rhs=("AR2h", "AR2l"),
                rhsof=None, lora2=None):
        ntt = Ttok // 512
        rh, rl = rhs
        groups = [oc_list[k:k + 2] for k in range(0, len(oc_list), 2)]
        wt("sync", SP, st["wsh_sp"])
        wt("sync", SPA, st.pop("prep_spa", 0))
        wt("sync", SPB, st.pop("prep_spb", 0))
        wt("sync", VE, st.get("wsl_ve", 0))
        st["wsl_ve"] = 0
        st["wsh_sp"] = 0
        st.setdefault("wshp_mm", [0, 0])

        def loadg(g):
            par = g % 2
            ws = WSA if par == 0 else WSB
            gw = len(groups[g]) * 128
            c0 = 2 * g * 128
            wt("sync", MM, st["wshp_mm"][par])
            wt("sync", MM, st["wsh_mm"])
            cl = 0
            for dc in range(NCH):
                cl = dma("sync", sb["wsh"][:, par * 2048 + dc * 256:par * 2048 + dc * 256 + gw],
                         wh_fn(dc, c0, gw), ws)
                cl = dma("sync", sb["wsl"][:, par * 2048 + dc * 256:par * 2048 + dc * 256 + gw],
                         wl_fn(dc, c0, gw), ws)
            return cl

        cls = [None] * len(groups)
        cls[0] = loadg(0)
        for g, ocs in enumerate(groups):
            par = g % 2
            if g + 1 < len(groups):
                cls[g + 1] = loadg(g + 1)
            wt("tensor", WSA if par == 0 else WSB, cls[g])
            for i, oc in enumerate(ocs):
                for tt in range(ntt):
                    bk = par * 2 + i * ntt + tt
                    wt_bank(bk)
                    pairs = []
                    for dc in range(NCH):
                        wof = par * 2048 + dc * 256 + i * 128
                        xno = rhsof(dc) + tt * 512
                        pairs.append((("wsh", wof, 128), (rh, xno)))
                        pairs.append((("wsh", wof, 128), (rl, xno)))
                        pairs.append((("wsl", wof, 128), (rh, xno)))
                    if lora2 is not None:
                        pairs.extend(lora2(i, oc, tt))
                    n = len(pairs)
                    for j, ((an, ao, aw), (bn, bo)) in enumerate(pairs):
                        em("tensor", lambda e, an=an, ao=ao, aw=aw, bn=bn, bo=bo, bk=bk,
                           f=(j == 0), l=(j == n - 1): e.matmul(
                            pb[bk][:], sb[an][:, ao:ao + aw], sb[bn][:, bo:bo + 512],
                            start=f, stop=l))
                    cm = minc()
                    st["wshp_mm"][par] = cm
                    evac(bk, oc, tt, cm)

    def evac_split(bk, cm, dsth, dstl, ho, bias_ap):
        wt("vector", MM, cm)
        if bias_ap is None:
            em("vector", lambda e, bk=bk, ho=ho, d=dsth: e.tensor_copy(
                out=sb[d][:, ho:ho + 512], in_=pb[bk][:]))
            vlock()
            em("vector", lambda e, bk=bk, ho=ho, d=dsth, dl=dstl: e.tensor_tensor(
                out=sb[dl][:, ho:ho + 512], in0=pb[bk][:], in1=sb[d][:, ho:ho + 512],
                op=ALU.subtract))
        else:
            em("vector", lambda e, bk=bk, ho=ho, d=dsth, ba=bias_ap: e.tensor_scalar(
                out=sb[d][:, ho:ho + 512], in0=pb[bk][:], scalar1=ba, scalar2=None, op0=ALU.add))
            vlock()
            em("vector", lambda e, bk=bk, ho=ho, d=dsth, dl=dstl, ba=bias_ap: e.scalar_tensor_tensor(
                out=sb[dl][:, ho:ho + 512], in0=pb[bk][:], scalar=ba,
                in1=sb[d][:, ho:ho + 512], op0=ALU.add, op1=ALU.subtract))
        return vinc()

    def stats_sq_mm(bk, hname, ho):
        em("vector", lambda e, h=hname, ho=ho: e.tensor_tensor(
            out=sb["sqb"][:], in0=sb[h][:, ho:ho + 512], in1=sb[h][:, ho:ho + 512], op=ALU.mult))
        cv = vinc()
        wt("tensor", VE, cv)
        em("tensor", lambda e, bk=bk: e.matmul(pb[bk][0:2, :], sb["c_onesh"][:], sb["sqb"][:],
                                               start=True, stop=True))
        return minc()

    def stats_q(bk, c, tt, hname, ho):
        cm2 = stats_sq_mm(bk, hname, ho)
        wt("vector", MM, cm2)
        wt("vector", CT, st.get("otmp_ld", 0))
        em("vector", lambda e, bk=bk: e.tensor_copy(out=sb["otmp"][0:2, :], in_=pb[bk][0:2, :]))
        cv = vinc()
        bank_free[bk] = ("VE", cv)
        wt("sync", VE, cv)
        wt("sync", SC, st["bvec_sc"])
        st["bvec_sc"] = 0
        cd = dma("sync", sb["nq2"][2 * c:2 * c + 2, tt * 512:tt * 512 + 512], sb["otmp"][0:2, :], CT)
        st["nq2_ld"] = max(st["nq2_ld"], cd)
        st["otmp_ld"] = cd

    def stats_k(bk, c, tt, hname, ho):
        cm2 = stats_sq_mm(bk, hname, ho)
        wt("vector", MM, cm2)
        wt("vector", CT, CT.v)
        em("vector", lambda e, bk=bk: e.reduce_max(out=sb["nktmp"][0:2, 0:1],
                                                   in_=pb[bk][0:2, :], axis=AX))
        cv2 = vinc()
        bank_free[bk] = ("VE", cv2)
        wt("sync", VE, cv2)
        cd = dma("sync", sb["nkst2"][2 * c:2 * c + 2, tt:tt + 1], sb["nktmp"][0:2, 0:1], CT)
        st["nkst_ld"] = cd

    def finish_kstats():
        wt("vector", CT, CT.v)
        em("vector", lambda e: e.reduce_max(out=sb["nktmp"][:], in_=sb["nkst2"][:], axis=AX))
        vlock()
        em("vector", lambda e: e.tensor_tensor(out=sb["nk16"][:], in0=sb["nk16"][:],
                                               in1=sb["nktmp"][:], op=ALU.max))
        em("vector", lambda e: e.memset(sb["nkst2"][:], 0.0))
        st["kst_ve"] = vinc()
        return st["kst_ve"]

    def build_bvec(Ttok, q0, use_alibi):
        if use_alibi:
            wt("sync", VE, st.get("kst_ve", 0))
            caq = dma("sync", sb["c_aq8"][0:16, 0:Ttok], din["c_aq8"][0:16, q0:q0 + Ttok], CT)
            wt("vector", CT, caq)
            wt("scalar", CT, caq)
        wt("scalar", CT, CT.v)
        em("scalar", lambda e, w=Ttok: e.activation(out=sb["bvec"][0:16, 0:w],
                                                    in_=sb["nq2"][0:16, 0:w], func=AF.Sqrt,
                                                    bias=0.0, scale=1.0))
        em("scalar", lambda e: e.activation(out=sb["nkmax"][:], in_=sb["nk16"][:],
                                            func=AF.Sqrt, bias=0.0, scale=1.0))
        cs = sinc()
        st["bvec_sc"] = cs
        wt("vector", SC, cs)
        em("vector", lambda e, w=Ttok: e.tensor_scalar(
            out=sb["bvec"][0:16, 0:w], in0=sb["bvec"][0:16, 0:w],
            scalar1=sb["nkmax"][0:16, 0:1], scalar2=1.05, op0=ALU.mult, op1=ALU.mult))
        vlock()
        if use_alibi:
            em("vector", lambda e, w=Ttok: e.tensor_tensor(
                out=sb["bvec"][0:16, 0:w], in0=sb["bvec"][0:16, 0:w],
                in1=sb["c_aq8"][0:16, 0:w], op=ALU.add))
            vlock()
        em("vector", lambda e, w=Ttok: e.tensor_scalar(
            out=sb["bvec"][0:16, 0:w], in0=sb["bvec"][0:16, 0:w],
            scalar1=8.0, scalar2=-1.0, op0=ALU.add, op1=ALU.mult))
        return vinc()

    def attention(L, hp, qts, bvec_ve, o_bias_col):
        wt("gpsimd", SP, st["kvspill_sp"])
        st["kvspill_sp"] = 0
        for bk in range(7):
            wt_bank(bk)
        for i, qd in enumerate(qts):
            for a in range(2):
                idx = i * 2 + a
                h = 2 * hp + a
                wt("sync", VE, bvec_ve)
                wt("sync", VE, st.get("o_ve", 0))
                wt("sync", VE, st["bvb_ve"])
                cd = dma("sync", sb["bvrow2"][idx * 32:idx * 32 + 1, 0:512],
                         sb["bvec"][h:h + 1, qd["bvoff"]:qd["bvoff"] + 512], CT)
                wt("vector", CT, cd)
                wt("vector", MM, st["bvb_mm"])
                em("vector", lambda e, idx=idx: e.tensor_copy(
                    out=sb["bvb"][idx * 32:idx * 32 + 1, 0:512],
                    in_=sb["bvrow2"][idx * 32:idx * 32 + 1, 0:512]))
        st["bvb_ve"] = vinc()
        PMAP = ((0, 1), (2, 3), (5, 7))

        def emit_av(un):
            wt("tensor", VE, un["cvL"])
            p, vo, obk, i = un["p"], un["vo"], un["obk"], un["i"]
            for pi, (vn, pn) in enumerate((("vbh", "ph"), ("vbh", "pl"), ("vbl", "ph"))):
                for a in range(2):
                    em("tensor", lambda e, vn=vn, pn=pn, a=a, vo=vo, p=p, obk=obk,
                       f=(un["fflag"] and pi == 0), l=(un["lflag"] and pi == 2): e.matmul(
                        pb[obk][a * 64:a * 64 + 64, :],
                        sb[vn][:, vo + a * 64:vo + a * 64 + 64],
                        sb[pn][:, p * 1024 + a * 512:p * 1024 + a * 512 + 512],
                        start=f, stop=l, tile_position=(0, a * 64),
                        skip_group_check=True))
            for pi, pn in enumerate(("ph", "pl")):
                for a in range(2):
                    dr = i * 64 + a * 32
                    em("tensor", lambda e, pn=pn, a=a, p=p, dr=dr,
                       f=(un["fflag"] and pi == 0), l=(un["lflag"] and pi == 1): e.matmul(
                        pb[6][dr:dr + 1, :], sb["c_ones_col"][:],
                        sb[pn][:, p * 1024 + a * 512:p * 1024 + a * 512 + 512],
                        start=f, stop=l, tile_position=(0, dr),
                        skip_group_check=True))
            cmA = minc()
            st["pf_mm"][p] = cmA
            st["kv_mm"][un["kp"]] = cmA

        wt("tensor", VE, st["bvb_ve"])
        ktmax = max(qd["kt_lim"] for qd in qts)
        u = 0
        nktg = (ktmax + 3) // 4

        def issue_kv(g):
            kp = g % 2
            KVp = KV0 if kp == 0 else KV1
            wt("gpsimd", MM, st["kv_mm"][kp])
            dma("gpsimd", sb["kbh"][:, kp * 512:kp * 512 + 512],
                ds[f"k{L}h"][hp][:, g * 512:g * 512 + 512], KVp)
            dma("gpsimd", sb["kbl"][:, kp * 512:kp * 512 + 512],
                ds[f"k{L}l"][hp][:, g * 512:g * 512 + 512], KVp)
            ck = 0
            for j4 in range(4):
                dma("gpsimd", sb["vbh"][:, kp * 512 + j4 * 128:kp * 512 + j4 * 128 + 128],
                    ds[f"v{L}h"][hp][g * 512 + j4 * 128:g * 512 + j4 * 128 + 128, :], KVp)
                ck = dma("gpsimd", sb["vbl"][:, kp * 512 + j4 * 128:kp * 512 + j4 * 128 + 128],
                         ds[f"v{L}l"][hp][g * 512 + j4 * 128:g * 512 + j4 * 128 + 128, :], KVp)
            return ck

        ckv_of = {0: issue_kv(0)}
        for ktg in range(nktg):
            kp = ktg % 2
            ckv = ckv_of[ktg]
            ug = 0
            for j in range(4):
                kt = ktg * 4 + j
                if kt >= ktmax:
                    break
                for i, qd in enumerate(qts):
                    if kt >= qd["kt_lim"]:
                        continue
                    p = u % 3
                    wt("tensor", KV0 if kp == 0 else KV1, ckv)
                    wt("tensor", SC, st["pS_sc"][p])
                    ko = kp * 512 + j * 128
                    kvp_now = kp
                    for pi, (kn, qn) in enumerate((("kbh", qd["qh"]), ("kbh", qd["ql"]),
                                                   ("kbl", qd["qh"]))):
                        for a in range(2):
                            em("tensor", lambda e, kn=kn, qn=qn, a=a, ko=ko, qc=qd["qcol"],
                               p=p, f=(pi == 0): e.matmul(
                                pb[PMAP[p][a]][:], sb[kn][a * 64:a * 64 + 64, ko:ko + 128],
                                sb[qn][a * 64:a * 64 + 64, qc:qc + 512],
                                start=f, stop=False, tile_position=(a * 64, 0),
                                skip_group_check=True))
                    for a in range(2):
                        em("tensor", lambda e, a=a, p=p, idx=(i * 2 + a): e.matmul(
                            pb[PMAP[p][a]][:], sb["c_o1b"][idx * 32:idx * 32 + 1, :],
                            sb["bvb"][idx * 32:idx * 32 + 1, 0:512],
                            start=False, stop=True, tile_position=(idx * 32, 0),
                            skip_group_check=True))
                    cmS = minc()
                    st["bvb_mm"] = cmS
                    mk = qd["mask_fn"](kt)
                    if mk is not None:
                        wt("vector", MM, cmS)
                        for a in range(2):
                            em("vector", lambda e, a=a, p=p, mn=mk[0], mo=mk[1]: e.tensor_tensor(
                                out=pb[PMAP[p][a]][:], in0=pb[PMAP[p][a]][:],
                                in1=sb[mn][:, mo:mo + 512], op=ALU.add))
                        cvA = vinc()
                        st["mask_used_ve"] = cvA
                        wt("scalar", VE, cvA)
                    else:
                        wt("scalar", MM, cmS)
                    p2 = u % 2
                    wt("scalar", MM, st["pf_mm"][p])
                    wt("scalar", VE, st.setdefault("pf_ve", [0, 0])[p2])
                    for a in range(2):
                        if L == 0:
                            bcol = (2 * hp + a) * 32 + kt
                            bap = sb["c_alibi"][:, bcol:bcol + 1]
                        else:
                            bap = sb["c_zero"][:, 0:1]
                        pof = p2 * 1024 + a * 512
                        em("scalar", lambda e, p=p, a=a, bap=bap, pof=pof: e.activation(
                            out=sb["pf"][:, pof:pof + 512], in_=pb[PMAP[p][a]][:],
                            func=AF.Exp, bias=bap, scale=0.125))
                    for a in range(2):
                        pof = p2 * 1024 + a * 512
                        po = p * 1024 + a * 512
                        em("scalar", lambda e, po=po, pof=pof: e.copy(
                            out=sb["ph"][:, po:po + 512], in_=sb["pf"][:, pof:pof + 512]))
                    csE = sinc()
                    st["pS_sc"][p] = csE
                    wt("vector", SC, csE)
                    for a in range(2):
                        pof = p2 * 1024 + a * 512
                        po = p * 1024 + a * 512
                        em("vector", lambda e, po=po, pof=pof: e.tensor_tensor(
                            out=sb["pl"][:, po:po + 512], in0=sb["pf"][:, pof:pof + 512],
                            in1=sb["ph"][:, po:po + 512], op=ALU.subtract))
                    cvL = vinc()
                    st["pf_ve"][p2] = cvL
                    fflag = not qd["_started"]
                    qd["_started"] = True
                    lflag = (kt == qd["kt_lim"] - 1)
                    unit = dict(p=p, vo=kp * 512 + j * 128, obk=4 + i, i=i,
                                fflag=fflag, lflag=lflag, cvL=cvL, kp=kvp_now)
                    pq = st.setdefault("att_pend", [])
                    pq.append(unit)
                    if len(pq) > 2:
                        emit_av(pq.pop(0))
                    u += 1
                    ug += 1
                    if ug == 2 and ktg + 1 < nktg and (ktg + 1) not in ckv_of:
                        ckv_of[ktg + 1] = issue_kv(ktg + 1)
            if ktg + 1 < nktg and (ktg + 1) not in ckv_of:
                ckv_of[ktg + 1] = issue_kv(ktg + 1)
        for un in st.get("att_pend", []):
            emit_av(un)
        st["att_pend"] = []
        for i, qd in enumerate(qts):
            obk = 4 + i
            wt("vector", MM, max(st["pf_mm"]))
            wt("vector", CT, st.get("otmp_ld", 0))
            em("vector", lambda e, i=i: e.tensor_copy(out=sb["otmp"][i * 64:i * 64 + 1, :],
                                                      in_=pb[6][i * 64:i * 64 + 1, :]))
            em("vector", lambda e, i=i: e.tensor_copy(out=sb["otmp"][i * 64 + 32:i * 64 + 33, :],
                                                      in_=pb[6][i * 64 + 32:i * 64 + 33, :]))
            cvd = vinc()
            wt("sync", VE, cvd)
            wt("sync", VE, st["drows_ve"])
            dma("sync", sb["drows"][0:1, :], sb["otmp"][i * 64:i * 64 + 1, :], CT)
            cd2 = dma("sync", sb["drows"][1:2, :], sb["otmp"][i * 64 + 32:i * 64 + 33, :], CT)
            st["otmp_ld"] = cd2
            wt("vector", CT, cd2)
            em("vector", lambda e: e.reciprocal(out=sb["drows"][:], in_=sb["drows"][:]))
            cv = vinc()
            st["drows_ve"] = cv
            wt("tensor", VE, cv)
            wt_bank(7)
            em("tensor", lambda e: e.matmul(pb[7][:], sb["c_sel2"][:], sb["drows"][:],
                                            start=True, stop=True))
            cm = minc()
            wt("scalar", MM, cm)
            em("scalar", lambda e: e.copy(out=sb["rtile"][:], in_=pb[7][:]))
            cs = sinc()
            bank_free[7] = ("SC", cs)
            wt("vector", SC, cs)
            wt("vector", MM, st["ar4_free_mm"])
            st["ar4_free_mm"] = 0
            em("vector", lambda e, obk=obk: e.tensor_tensor(
                out=sb["otmp"][:], in0=pb[obk][:], in1=sb["rtile"][:], op=ALU.mult))
            vlock()
            oc = qd["ocol"]
            if o_bias_col is not None:
                em("vector", lambda e, oc=oc, bc=o_bias_col: e.tensor_scalar(
                    out=sb["AR4h"][:, oc:oc + 512], in0=sb["otmp"][:],
                    scalar1=sb["battn"][:, bc:bc + 1], scalar2=None, op0=ALU.add))
                vlock()
                em("vector", lambda e, oc=oc, bc=o_bias_col: e.scalar_tensor_tensor(
                    out=sb["AR4l"][:, oc:oc + 512], in0=sb["otmp"][:],
                    scalar=sb["battn"][:, bc:bc + 1], in1=sb["AR4h"][:, oc:oc + 512],
                    op0=ALU.add, op1=ALU.subtract))
            else:
                em("vector", lambda e, oc=oc: e.tensor_copy(
                    out=sb["AR4h"][:, oc:oc + 512], in_=sb["otmp"][:]))
                vlock()
                em("vector", lambda e, oc=oc: e.tensor_tensor(
                    out=sb["AR4l"][:, oc:oc + 512], in0=sb["otmp"][:],
                    in1=sb["AR4h"][:, oc:oc + 512], op=ALU.subtract))
            cv = vinc()
            bank_free[obk] = ("VE", cv)
            bank_free[6] = ("VE", cv)
            st["o_ve"] = cv
            qd["_started"] = False

    def lora_u(Ttok, la_h, la_l, rhs, rhsof):
        rh, rl = rhs
        wt("tensor", VE, st["b12_ve"])
        for tt in range(Ttok // 512):
            bk = 7
            wt_bank(bk)
            pairs = []
            for dc in range(NCH):
                pairs.append(((la_h, dc * 32), (rh, rhsof(dc) + tt * 512)))
                pairs.append(((la_h, dc * 32), (rl, rhsof(dc) + tt * 512)))
                pairs.append(((la_l, dc * 32), (rh, rhsof(dc) + tt * 512)))
            n = len(pairs)
            for j, ((an, ao), (bn, bo)) in enumerate(pairs):
                em("tensor", lambda e, an=an, ao=ao, bn=bn, bo=bo, bk=bk,
                   f=(j == 0), l=(j == n - 1): e.matmul(
                    pb[bk][0:32, :], sb[an][:, ao:ao + 32], sb[bn][:, bo:bo + 512],
                    start=f, stop=l))
            cm = minc()
            wt("scalar", MM, cm)
            wt("scalar", MM, st["u_free_mm"])
            st["u_free_mm"] = 0
            em("scalar", lambda e, tt=tt, bk=bk: e.mul(sb["urow"][0:32, tt * 512:tt * 512 + 512],
                                                       pb[bk][0:32, :], 0.5))
            cs = sinc()
            bank_free[bk] = ("SC", cs)
            wt("vector", SC, cs)
            em("vector", lambda e, tt=tt: e.tensor_copy(
                out=sb["uh"][0:32, tt * 512:tt * 512 + 512],
                in_=sb["urow"][0:32, tt * 512:tt * 512 + 512]))
            vlock()
            em("vector", lambda e, tt=tt: e.tensor_tensor(
                out=sb["ul"][0:32, tt * 512:tt * 512 + 512],
                in0=sb["urow"][0:32, tt * 512:tt * 512 + 512],
                in1=sb["uh"][0:32, tt * 512:tt * 512 + 512], op=ALU.subtract))
        return vinc()

    for blk in range(8):
        t0 = blk * 512
        wt("sync", SP, st["x1spill_sp"])
        st["x1spill_sp"] = 0
        wt("sync", VE, st["b12_ve"])
        if blk == 0:
            wt("sync", VE, st["prep_lastv"][0])
        cx = 0
        for c in range(NCH):
            cx = dma("sync", sb["AR1"][:, c * 512:c * 512 + 512],
                     din["x_fm"][c][:, t0:t0 + 512], LD)
        wt("vector", LD, cx)
        ln(512, lambda c: c * 512, lambda c: c * 512)

        def ev_q(bk, oc, tt, cm):
            cv = evac_split(bk, cm, "AR3h", "AR3l", oc * 512, None)
            bank_free[bk] = ("VE", cv)
            stats_q(bk, oc, 0, "AR3h", oc * 512)

        wt("vector", MM, st["ar3_free_mm"])
        st["ar3_free_mm"] = 0
        wt("tensor", VE, st["b12_ve"])
        fm_proj(512, lambda dc, c0, gw: ds["w0h"][0][dc][:, c0:c0 + gw],
                lambda dc, c0, gw: ds["w0l"][0][dc][:, c0:c0 + gw],
                list(range(8)), ev_q, rhsof=lambda dc: dc * 512)

        def ev_k(bk, oc, tt, cm):
            wt("vector", SP, st["ev_sp"])
            cv = evac_split(bk, cm, "evh", "evl", 0, None)
            wt("gpsimd", VE, cv)
            dma("gpsimd", ds["k0h"][oc][:, t0:t0 + 512], sb["evh"][:], SP)
            cs = dma("gpsimd", ds["k0l"][oc][:, t0:t0 + 512], sb["evl"][:], SP)
            st["ev_sp"] = cs
            st["kvspill_sp"] = cs
            bank_free[bk] = ("VE", cv)
            stats_k(bk, oc, 0, "evh", 0)

        fm_proj(512, lambda dc, c0, gw: ds["w0h"][1][dc][:, c0:c0 + gw],
                lambda dc, c0, gw: ds["w0l"][1][dc][:, c0:c0 + gw],
                list(range(8)), ev_k, rhsof=lambda dc: dc * 512)
        finish_kstats()

        for half in range(2):
            cl = wload([ds["w0h"][2][dc][:, half * 512:half * 512 + 512] for dc in range(NCH)],
                       [ds["w0l"][2][dc][:, half * 512:half * 512 + 512] for dc in range(NCH)], 512)
            wt("tensor", WS, cl)
            for tk in range(4):
                bk = tk % 2
                wt_bank(bk)
                pairs = []
                for dc in range(NCH):
                    xno = dc * 512 + tk * 128
                    pairs.append((("AR2h", xno), ("wsh", dc * 512)))
                    pairs.append((("AR2l", xno), ("wsh", dc * 512)))
                    pairs.append((("AR2h", xno), ("wsl", dc * 512)))
                n = len(pairs)
                for j, ((an, ao), (bn, bo)) in enumerate(pairs):
                    em("tensor", lambda e, an=an, ao=ao, bn=bn, bo=bo, bk=bk,
                       f=(j == 0), l=(j == n - 1): e.matmul(
                        pb[bk][:], sb[an][:, ao:ao + 128], sb[bn][:, bo:bo + 512],
                        start=f, stop=l))
                cm = minc()
                st["wsh_mm"] = cm
                wt("vector", MM, cm)
                wt("vector", SP, st["ev_sp"])
                em("vector", lambda e, bk=bk: e.tensor_copy(out=sb["evh"][:], in_=pb[bk][:]))
                vlock()
                em("vector", lambda e, bk=bk: e.tensor_tensor(
                    out=sb["evl"][:], in0=pb[bk][:], in1=sb["evh"][:], op=ALU.subtract))
                cv = vinc()
                bank_free[bk] = ("VE", cv)
                wt("gpsimd", VE, cv)
                cs = 0
                for c4 in range(4):
                    cpr = half * 4 + c4
                    dma("gpsimd", ds["v0h"][cpr][t0 + tk * 128:t0 + tk * 128 + 128, :],
                        sb["evh"][:, c4 * 128:c4 * 128 + 128], SP)
                    cs = dma("gpsimd", ds["v0l"][cpr][t0 + tk * 128:t0 + tk * 128 + 128, :],
                             sb["evl"][:, c4 * 128:c4 * 128 + 128], SP)
                st["ev_sp"] = cs
                st["kvspill_sp"] = cs

        for tk in range(4):
            bk = 3
            wt_bank(bk)
            pairs = []
            for dc in range(NCH):
                xno = dc * 512 + tk * 128
                pairs.append((("AR2h", xno), ("rah", dc * 136)))
                pairs.append((("AR2l", xno), ("rah", dc * 136)))
                pairs.append((("AR2h", xno), ("ral", dc * 136)))
            n = len(pairs)
            for j, ((an, ao), (bn, bo)) in enumerate(pairs):
                em("tensor", lambda e, an=an, ao=ao, bn=bn, bo=bo, bk=bk,
                   f=(j == 0), l=(j == n - 1): e.matmul(
                    pb[bk][:, 0:136], sb[an][:, ao:ao + 128], sb[bn][:, bo:bo + 136],
                    start=f, stop=l))
            cm = minc()
            wt("vector", MM, cm)
            em("vector", lambda e, bk=bk: e.reduce_max(out=sb["negm"][:], in_=pb[bk][:, 0:8], axis=AX))
            vlock()
            em("vector", lambda e: e.tensor_scalar(out=sb["negm"][:], in0=sb["negm"][:],
                                                   scalar1=-1.0, scalar2=None, op0=ALU.mult))
            cv = vinc()
            wt("scalar", VE, cv)
            em("scalar", lambda e, bk=bk: e.activation(out=sb["gb0"][:], in_=pb[bk][:, 0:8],
                                                       func=AF.Exp, bias=sb["negm"][:, 0:1],
                                                       scale=1.0, accum_out=sb["r8"][:]))
            cs = sinc()
            wt("vector", SC, cs)
            em("vector", lambda e: e.reciprocal(out=sb["r8"][:], in_=sb["r8"][:]))
            cv = vinc()
            wt("vector", VE, cv)
            em("vector", lambda e: e.tensor_scalar(out=sb["gb1"][:], in0=sb["gb0"][:],
                                                   scalar1=sb["r8"][:, 0:1], scalar2=None,
                                                   op0=ALU.mult))
            vlock()
            em("vector", lambda e: e.max(out=sb["gb2"][:], in_=sb["gb1"][:]))
            cv = vinc()
            wt("vector", VE, cv)
            em("vector", lambda e: e.tensor_scalar(out=sb["gb3"][:], in0=sb["gb1"][:],
                                                   scalar1=sb["gb2"][:, 1:2], scalar2=None,
                                                   op0=ALU.is_ge))
            vlock()
            em("vector", lambda e: e.tensor_tensor(out=sb["gb1"][:], in0=sb["gb1"][:],
                                                   in1=sb["gb3"][:], op=ALU.mult))
            vlock()
            em("vector", lambda e: e.reduce_sum(out=sb["r8"][:], in_=sb["gb1"][:], axis=AX))
            vlock()
            em("vector", lambda e: e.tensor_scalar(out=sb["r8"][:], in0=sb["r8"][:],
                                                   scalar1=1e-6, scalar2=None, op0=ALU.add))
            vlock()
            em("vector", lambda e: e.reciprocal(out=sb["r8"][:], in_=sb["r8"][:]))
            cv = vinc()
            wt("vector", VE, cv)
            em("vector", lambda e: e.tensor_scalar(out=sb["gb1"][:], in0=sb["gb1"][:],
                                                   scalar1=sb["r8"][:, 0:1], scalar2=None,
                                                   op0=ALU.mult))
            cv = vinc()
            wt("vector", VE, cv)
            for ee in range(8):
                em("vector", lambda e, bk=bk, ee=ee: e.tensor_scalar(
                    out=sb["ghf"][:, ee * 16:ee * 16 + 16],
                    in0=pb[bk][:, 8 + ee * 16:8 + ee * 16 + 16],
                    scalar1=sb["gb1"][:, ee:ee + 1], scalar2=None, op0=ALU.mult))
            cv = vinc()
            bank_free[bk] = ("VE", cv)
            wt("tensor", VE, cv)
            wt("tensor", MM, st["gt_free_mm"])
            st["gt_free_mm"] = 0
            wt_bank(7)
            em("tensor", lambda e: e.transpose(pb[7][:, 0:128], sb["ghf"][:], sb["c_identf"][:]))
            cm = minc()
            wt("vector", MM, cm)
            em("vector", lambda e, tk=tk: e.tensor_copy(
                out=sb["gth"][:, tk * 128:tk * 128 + 128], in_=pb[7][:, 0:128]))
            vlock()
            em("vector", lambda e, tk=tk: e.tensor_tensor(
                out=sb["gtl"][:, tk * 128:tk * 128 + 128], in0=pb[7][:, 0:128],
                in1=sb["gth"][:, tk * 128:tk * 128 + 128], op=ALU.subtract))
            cv = vinc()
            bank_free[7] = ("VE", cv)
            st["gt_ve"] = cv

        def lora0(i, oc, tt):
            return [(("bsh", oc * 128, 128), ("gth", 0)),
                    (("bsh", oc * 128, 128), ("gtl", 0)),
                    (("bsl", oc * 128, 128), ("gth", 0))]

        def ev_ffn0(bk, oc, tt, cm):
            wt("scalar", MM, cm)
            wt("scalar", VE, st["gA_free_ve"])
            st["gA_free_ve"] = 0
            em("scalar", lambda e, bk=bk, oc=oc: e.activation(
                out=sb["gA"][:, oc * 512:oc * 512 + 512],
                in_=pb[bk][:], func=AF.Silu, bias=sb["bW"][:, oc:oc + 1], scale=1.0))
            cs = sinc()
            bank_free[bk] = ("SC", cs)
            st["gt_free_mm"] = cm

        wt("tensor", VE, st.get("gt_ve", 0))
        fm_proj(512, lambda dc, c0, gw: ds["w0h"][4][dc][:, c0:c0 + gw],
                lambda dc, c0, gw: ds["w0l"][4][dc][:, c0:c0 + gw],
                list(range(8)), ev_ffn0, rhsof=lambda dc: dc * 512, lora2=lora0)

        cbv = build_bvec(512, t0, True)
        st["ar2_free_mm"] = MM.v
        for hp in range(NCH):
            def mk0(kt, g=blk):
                base = 4 * g
                if kt >= base:
                    return ("c_masksb", (kt - base) * 512)
                return None
            qts = [dict(qcol=hp * 512, ocol=hp * 512, bvoff=0, kt_lim=4 * blk + 4,
                        mask_fn=mk0, qh="AR3h", ql="AR3l", _started=False)]
            attention(0, hp, qts, cbv, None)
        st["ar3_free_mm"] = MM.v

        def ev_o(bk, oc, tt, cm):
            wt("vector", MM, cm)
            wt("vector", VE, st.get("res_ve", 0))
            em("vector", lambda e, bk=bk, oc=oc: e.scalar_tensor_tensor(
                out=sb["hrow"][:], in0=pb[bk][:], scalar=0.0,
                in1=sb["gA"][:, oc * 512:oc * 512 + 512],
                op0=ALU.add, op1=ALU.mult))
            vlock()
            em("vector", lambda e, oc=oc: e.tensor_tensor(
                out=sb["AR1"][:, oc * 512:oc * 512 + 512],
                in0=sb["AR1"][:, oc * 512:oc * 512 + 512],
                in1=sb["hrow"][:], op=ALU.add))
            cv = vinc()
            bank_free[bk] = ("VE", cv)
            st["gA_free_ve"] = cv
            st["res_ve"] = cv
            st["b12_ve"] = max(st["b12_ve"], cv)
            wt("gpsimd", VE, cv)
            cs = dma("gpsimd", ds["x1s"][blk][:, oc * 512:oc * 512 + 512],
                     sb["AR1"][:, oc * 512:oc * 512 + 512], SP)
            st["x1spill_sp"] = cs

        wt("tensor", VE, st.get("o_ve", 0))
        fm_proj(512, lambda dc, c0, gw: ds["w0h"][3][dc][:, c0:c0 + gw],
                lambda dc, c0, gw: ds["w0l"][3][dc][:, c0:c0 + gw],
                list(range(8)), ev_o, rhs=("AR4h", "AR4l"), rhsof=lambda dc: dc * 512)
        st["ar4_free_mm"] = MM.v

    em("vector", lambda e: e.memset(sb["nk16"][:], 0.0))
    em("vector", lambda e: e.memset(sb["nkst2"][:], 0.0))
    vinc()
    for blk in range(8):
        t0 = blk * 512
        wt("sync", SP, st["x1spill_sp"])
        wt("sync", VE, st["b12_ve"])
        cx = dma("sync", sb["AR1"][:, 0:4096], ds["x1s"][blk], LD)
        wt("vector", LD, cx)
        ln(512, lambda c: c * 512, lambda c: c * 512)
        cu = lora_u(512, "lath", "latl", ("AR2h", "AR2l"), lambda dc: dc * 512)
        wt("tensor", VE, cu)

        def mk_lora_fm(colbase):
            def f(i, oc, tt):
                col = colbase + oc * 128
                return [(("lbth", col, 128), ("uh", tt * 512)),
                        (("lbth", col, 128), ("ul", tt * 512)),
                        (("lbtl", col, 128), ("uh", tt * 512))]
            return f

        def ev_k1(bk, oc, tt, cm):
            wt("vector", SP, st["ev_sp"])
            cv = evac_split(bk, cm, "evh", "evl", 0, sb["battn"][:, 8 + oc:8 + oc + 1])
            wt("gpsimd", VE, cv)
            dma("gpsimd", ds["k1h"][oc][:, t0:t0 + 512], sb["evh"][:], SP)
            cs = dma("gpsimd", ds["k1l"][oc][:, t0:t0 + 512], sb["evl"][:], SP)
            st["ev_sp"] = cs
            st["kvspill_sp"] = cs
            bank_free[bk] = ("VE", cv)
            stats_k(bk, oc, 0, "evh", 0)

        fm_proj(512, lambda dc, c0, gw: ds["w1h"][dc][:, 1024 + c0:1024 + c0 + gw],
                lambda dc, c0, gw: ds["w1l"][dc][:, 1024 + c0:1024 + c0 + gw],
                list(range(8)), ev_k1, rhsof=lambda dc: dc * 512,
                lora2=mk_lora_fm(1024))
        st["u_free_mm"] = MM.v

        for half in range(2):
            cl = wload([ds["w1h"][dc][:, 2048 + half * 512:2048 + half * 512 + 512]
                        for dc in range(NCH)],
                       [ds["w1l"][dc][:, 2048 + half * 512:2048 + half * 512 + 512]
                        for dc in range(NCH)], 512)
            wt("tensor", WS, cl)
            for tk in range(4):
                bk = tk % 2
                wt_bank(bk)
                pairs = []
                for dc in range(NCH):
                    xno = dc * 512 + tk * 128
                    pairs.append((("AR2h", xno, 128), ("wsh", dc * 512)))
                    pairs.append((("AR2l", xno, 128), ("wsl", dc * 512)))
                    pairs.append((("AR2h", xno, 128), ("wsl", dc * 512)))
                pairs = []
                for dc in range(NCH):
                    xno = dc * 512 + tk * 128
                    pairs.append((("AR2h", xno, 128), ("wsh", dc * 512)))
                    pairs.append((("AR2l", xno, 128), ("wsh", dc * 512)))
                    pairs.append((("AR2h", xno, 128), ("wsl", dc * 512)))
                pairs.append((("uh", tk * 128, 128), ("lbthv", half * 512)))
                pairs.append((("ul", tk * 128, 128), ("lbthv", half * 512)))
                pairs.append((("uh", tk * 128, 128), ("lbtlv", half * 512)))
                n = len(pairs)
                for j, (aa, bb) in enumerate(pairs):
                    an, ao = aa[0], aa[1]
                    bn, bo = bb
                    if bn == "lbthv":
                        bn, bo = "lbth", 2048 + bo
                    if bn == "lbtlv":
                        bn, bo = "lbtl", 2048 + bo
                    em("tensor", lambda e, an=an, ao=ao, bn=bn, bo=bo, bk=bk,
                       f=(j == 0), l=(j == n - 1): e.matmul(
                        pb[bk][:], sb[an][:, ao:ao + 128], sb[bn][:, bo:bo + 512],
                        start=f, stop=l))
                cm = minc()
                st["wsh_mm"] = cm
                st["u_free_mm"] = cm
                wt("vector", MM, cm)
                wt("vector", SP, st["ev_sp"])
                em("vector", lambda e, bk=bk: e.tensor_copy(out=sb["evh"][:], in_=pb[bk][:]))
                vlock()
                em("vector", lambda e, bk=bk: e.tensor_tensor(
                    out=sb["evl"][:], in0=pb[bk][:], in1=sb["evh"][:], op=ALU.subtract))
                cv = vinc()
                bank_free[bk] = ("VE", cv)
                wt("gpsimd", VE, cv)
                cs = 0
                for c4 in range(4):
                    cpr = half * 4 + c4
                    dma("gpsimd", ds["v1h"][cpr][t0 + tk * 128:t0 + tk * 128 + 128, :],
                        sb["evh"][:, c4 * 128:c4 * 128 + 128], SP)
                    cs = dma("gpsimd", ds["v1l"][cpr][t0 + tk * 128:t0 + tk * 128 + 128, :],
                             sb["evl"][:, c4 * 128:c4 * 128 + 128], SP)
                st["ev_sp"] = cs
                st["kvspill_sp"] = cs
        st["ar2_free_mm"] = MM.v
        finish_kstats()

    for s in range(4):
        wt("sync", SP, st["x1spill_sp"])
        st["x1spill_sp"] = 0
        wt("sync", VE, st["b12_ve"])
        cx = dma_ri("sync", sb["AR1"][:, 0:4096],
                    (lambda idx: dsh["x1s"][idx]), G0[s], G1[s], LD)
        wt("vector", LD, cx)
        ln(512, lambda c: c * 512, lambda c: c * 512)
        cu = lora_u(512, "lath", "latl", ("AR2h", "AR2l"), lambda dc: dc * 512)
        wt("tensor", VE, cu)

        def ev_q1(bk, oc, tt, cm):
            cv = evac_split(bk, cm, "AR3h", "AR3l", oc * 512, sb["battn"][:, oc:oc + 1])
            bank_free[bk] = ("VE", cv)
            stats_q(bk, oc, 0, "AR3h", oc * 512)

        wt("vector", MM, st["ar3_free_mm"])
        st["ar3_free_mm"] = 0

        def l1q_lora(i, oc, tt):
            return [(("lbth", oc * 128, 128), ("uh", 0)),
                    (("lbth", oc * 128, 128), ("ul", 0)),
                    (("lbtl", oc * 128, 128), ("uh", 0))]

        fm_proj(512, lambda dc, c0, gw: ds["w1h"][dc][:, c0:c0 + gw],
                lambda dc, c0, gw: ds["w1l"][dc][:, c0:c0 + gw],
                list(range(8)), ev_q1, rhsof=lambda dc: dc * 512, lora2=l1q_lora)
        st["u_free_mm"] = MM.v

        for hcp in range(12):
            cl = wload([ds["wgu"][hcp][dc] for dc in range(NCH)], None, 512)
            wt("tensor", WS, cl)
            for jj in range(2):
                hc = 2 * hcp + jj
                bkg = jj * 2
                bku = jj * 2 + 1
                wt_bank(bkg)
                wt_bank(bku)
                mms = []
                for tgt, bk, co in ((0, bkg, jj * 128), (1, bku, 256 + jj * 128)):
                    pairs = []
                    for dc in range(NCH):
                        pairs.append((("wsh", dc * 512 + co), ("AR2h", dc * 512)))
                        pairs.append((("wsh", dc * 512 + co), ("AR2l", dc * 512)))
                    n = len(pairs)
                    for j2, ((an, ao), (bn, bo)) in enumerate(pairs):
                        em("tensor", lambda e, an=an, ao=ao, bn=bn, bo=bo, bk=bk,
                           f=(j2 == 0), l=(j2 == n - 1): e.matmul(
                            pb[bk][:], sb[an][:, ao:ao + 128], sb[bn][:, bo:bo + 512],
                            start=f, stop=l))
                    mms.append(minc())
                st["wsh_mm"] = mms[-1]
                wt("scalar", MM, mms[0])
                wt("scalar", VE, max(st.get("hrow_ve", 0), st.get("res_ve", 0)))
                wt("scalar", OT, st.get("out_ot", 0))
                em("scalar", lambda e, bkg=bkg: e.activation(
                    out=sb["hrow"][:], in_=pb[bkg][:], func=AF.Silu, bias=0.0, scale=S127))
                cs = sinc()
                bank_free[bkg] = ("SC", cs)
                wt("vector", SC, cs)
                wt("vector", MM, mms[1])
                wt("vector", SP, st["ev_sp"])
                em("vector", lambda e, bku=bku: e.tensor_tensor(
                    out=sb["hrow"][:], in0=sb["hrow"][:], in1=pb[bku][:], op=ALU.mult))
                vlock()
                em("vector", lambda e: e.tensor_copy(out=sb["evh"][:], in_=sb["hrow"][:]))
                vlock()
                em("vector", lambda e: e.tensor_tensor(
                    out=sb["evl"][:], in0=sb["hrow"][:], in1=sb["evh"][:], op=ALU.subtract))
                cv = vinc()
                bank_free[bku] = ("VE", cv)
                st["hrow_ve"] = cv
                wt("gpsimd", VE, cv)
                dma("gpsimd", ds["hsh"][hc], sb["evh"][:], SP)
                cs2 = dma("gpsimd", ds["hsl"][hc], sb["evl"][:], SP)
                st["ev_sp"] = cs2
                st["hspill_sp"] = cs2

        for ocg in range(2):
            for hg in range(3):
                cl = wload([ds["wd"][hg * 8 + ii][:, ocg * 512:ocg * 512 + 512] for ii in range(NCH)],
                           None, 512)
                wt("tensor", WS, cl)
                for ii in range(8):
                    hc = hg * 8 + ii
                    hb = hc % 2
                    KVp = KV0 if hb == 0 else KV1
                    wt("sync", MM, st["kv_mm"][hb])
                    if hc == 0 and ocg == 0:
                        wt("sync", SP, st.get("hspill_sp", 0))
                    dma("sync", sb["kbh"][:, hb * 512:hb * 512 + 512], ds["hsh"][hc], KVp)
                    chl = dma("sync", sb["kbl"][:, hb * 512:hb * 512 + 512], ds["hsl"][hc], KVp)
                    wt("tensor", KVp, chl)
                    for ocf in range(4):
                        bk = 4 + ocf
                        if hc == 0:
                            wt_bank(bk)
                        for pi, hn in enumerate(("kbh", "kbl")):
                            em("tensor", lambda e, hn=hn, hb=hb, ii=ii, ocf=ocf, bk=bk,
                               f=(hc == 0 and pi == 0), l=(hc == 23 and pi == 1): e.matmul(
                                pb[bk][:], sb["wsh"][:, ii * 512 + ocf * 128:ii * 512 + ocf * 128 + 128],
                                sb[hn][:, hb * 512:hb * 512 + 512], start=f, stop=l))
                    cmu = minc()
                    st["kv_mm"][hb] = cmu
                cm = cmu
                st["wsh_mm"] = cm
            for ocf in range(4):
                oc = ocg * 4 + ocf
                bk = 4 + ocf
                wt("scalar", MM, cm)
                wt("scalar", VE, st["gA_free_ve"])
                st["gA_free_ve"] = 0
                em("scalar", lambda e, bk=bk, oc=oc: e.activation(
                    out=sb["gA"][:, oc * 512:oc * 512 + 512], in_=pb[bk][:],
                    func=AF.Silu, bias=0.0, scale=S127 * S127))
                cs = sinc()
                bank_free[bk] = ("SC", cs)

        wt("sync", MM, st["wsh_mm"])
        st["wsh_mm"] = 0
        cm0 = dma_ri("sync", sb["maskb"][:, 0:4096],
                     (lambda idx: dinh["l1masks"][idx]), s, 4 + s, MK)
        st["maskb_ld"] = cm0
        cbv = build_bvec(512, 0, False)
        st["ar2_free_mm"] = MM.v
        wt("vector", MK, st["maskb_ld"])

        for hp in range(NCH):
            def mk1(kt, s=s):
                w0 = BND[s] - 8
                if kt >= w0:
                    return ("maskb", (kt - w0) * 512)
                return None
            qts = [dict(qcol=hp * 512, ocol=hp * 512, bvoff=0, kt_lim=BND[s], mask_fn=mk1,
                        qh="AR3h", ql="AR3l", _started=False)]
            attention(1, hp, qts, cbv, 16 + hp)
        st["ar3_free_mm"] = MM.v
        st["wsl_ve"] = st["mask_used_ve"]

        wt("tensor", VE, st.get("o_ve", 0))
        cu = lora_u(512, "laph", "lapl", ("AR4h", "AR4l"), lambda dc: dc * 512)
        wt("tensor", VE, cu)

        def l1p_lora(i, oc, tt):
            return [(("lbph", oc * 128, 128), ("uh", 0)),
                    (("lbph", oc * 128, 128), ("ul", 0)),
                    (("lbpl", oc * 128, 128), ("uh", 0))]

        def ev_p1(bk, oc, tt, cm):
            wt("vector", MM, cm)
            wt("vector", VE, st.get("res_ve", 0))
            em("vector", lambda e, bk=bk, oc=oc: e.scalar_tensor_tensor(
                out=sb["hrow"][:], in0=pb[bk][:], scalar=sb["bproj"][:, oc:oc + 1],
                in1=sb["gA"][:, oc * 512:oc * 512 + 512], op0=ALU.add, op1=ALU.mult))
            vlock()
            em("vector", lambda e, oc=oc: e.tensor_tensor(
                out=sb["hrow"][:], in0=sb["AR1"][:, oc * 512:oc * 512 + 512],
                in1=sb["hrow"][:], op=ALU.add))
            cv = vinc()
            bank_free[bk] = ("VE", cv)
            st["gA_free_ve"] = cv
            st["res_ve"] = cv
            st["b12_ve"] = max(st["b12_ve"], cv)
            wt("gpsimd", VE, cv)
            dma("gpsimd", out[oc][:, s * 512:s * 512 + 512], sb["hrow"][:], OT)

        def ev_p1_guarded(bk, oc, tt, cm):
            wt("vector", OT, st.get("out_ot", 0))
            ev_p1(bk, oc, tt, cm)
            st["out_ot"] = OT.v

        fm_proj(512, lambda dc, c0, gw: ds["w1h"][dc][:, 3072 + c0:3072 + c0 + gw],
                lambda dc, c0, gw: ds["w1l"][dc][:, 3072 + c0:3072 + c0 + gw],
                list(range(8)), ev_p1_guarded, rhs=("AR4h", "AR4l"),
                rhsof=lambda dc: dc * 512, lora2=l1p_lora)
        st["ar4_free_mm"] = MM.v
        st["u_free_mm"] = MM.v

    wt("gpsimd", OT, OT.v)

    with nc.Block() as block:
        @block.sync
        def _(eng):
            for f in P["sync"]:
                f(eng)

        @block.gpsimd
        def _(eng):
            for f in P["gpsimd"]:
                f(eng)

        @block.tensor
        def _(eng):
            for f in P["tensor"]:
                f(eng)

        @block.vector
        def _(eng):
            for f in P["vector"]:
                f(eng)

        @block.scalar
        def _(eng):
            for f in P["scalar"]:
                f(eng)

    return nc


_NC_CACHE = []


def kernel(**inp):
    x = np.ascontiguousarray(np.asarray(inp["L0_attn_x"], dtype=np.float32))
    f32 = lambda a: np.ascontiguousarray(np.asarray(a, dtype=np.float32))
    i32 = lambda a: np.ascontiguousarray(np.asarray(a, dtype=np.int32))

    A = f32(inp["L0_ffn_A"])
    Bm = f32(inp["L0_ffn_B"])
    R = f32(inp["L0_ffn_R"])
    RA = np.concatenate([R.T, A.transpose(2, 0, 1).reshape(DM, 128)], axis=1)
    RA = np.ascontiguousarray(RA.reshape(NCH, 128, 136).transpose(1, 0, 2).reshape(128, 1088))

    base = dict(
        Wq=f32(inp["L0_attn_Wq"]).reshape(NCH, 128, DM),
        Wk=f32(inp["L0_attn_Wk"]).reshape(NCH, 128, DM),
        Wv=f32(inp["L0_attn_Wv"]).reshape(NCH, 128, DM),
        Wo=f32(inp["L0_attn_Wo"]).reshape(NCH, 128, DM),
        W_T=np.ascontiguousarray(f32(inp["L0_ffn_W"]).T).reshape(NCH, 128, DM),
        bW=np.ascontiguousarray(f32(inp["L0_ffn_b"]).reshape(NCH, 128).T),
        RA=RA,
        Bstk=np.ascontiguousarray(Bm.transpose(0, 2, 1).reshape(128, DM)),
        w_attnT=np.ascontiguousarray(f32(inp["L1_attn_w_attn"]).T).reshape(NCH, 128, 3072),
        battn=np.ascontiguousarray(f32(inp["L1_attn_b_attn"]).reshape(24, 128).T),
        la_attnT=np.ascontiguousarray(
            f32(inp["L1_attn_la_attn"]).T.reshape(NCH, 128, 32).transpose(1, 0, 2).reshape(128, 256)),
        lb_attnT=np.ascontiguousarray(f32(inp["L1_attn_lb_attn"]).T),
        w_projT=np.ascontiguousarray(f32(inp["L1_attn_w_proj"]).T).reshape(NCH, 128, DM),
        bproj=np.ascontiguousarray(f32(inp["L1_attn_b_proj"]).reshape(NCH, 128).T),
        la_projT=np.ascontiguousarray(
            f32(inp["L1_attn_la_proj"]).T.reshape(NCH, 128, 32).transpose(1, 0, 2).reshape(128, 256)),
        lb_projT=np.ascontiguousarray(f32(inp["L1_attn_lb_proj"]).T),
        wgT=np.ascontiguousarray(i32(inp["L1_ffn_w_gate"]).T).reshape(NCH, 128, 3072),
        wuT=np.ascontiguousarray(i32(inp["L1_ffn_w_up"]).T).reshape(NCH, 128, 3072),
        wdT=np.ascontiguousarray(i32(inp["L1_ffn_w_down"]).T).reshape(24, 128, DM),
    )
    cst = host_constants()
    l1m = np.zeros((8, 128, 4096), dtype=ml_dtypes.bfloat16)
    for role in range(2):
        for s in range(4):
            for j in range(8):
                l1m[role * 4 + s][:, j * 512:(j + 1) * 512] = cst["c_masks"][l1_mask_idx(role, s, j)]
    base["l1masks"] = l1m
    base.update({k: np.ascontiguousarray(v) for k, v in cst.items()})

    in_maps = []
    for core in range(8):
        seq = core // 2
        m = dict(base)
        m["x_fm"] = np.ascontiguousarray(x[seq].T).reshape(NCH, 128, T)
        in_maps.append(m)

    nc = build_program()
    res = run_bass_kernel_spmd(nc, in_maps, core_ids=list(range(8)))

    outp = np.empty((4, T, DM), dtype=np.float32)
    for core in range(8):
        seq, role = core // 2, core % 2
        G = G0 if role == 0 else G1
        fm = np.asarray(res.results[core]["out"]).reshape(DM, 2048)
        for s, g in enumerate(G):
            outp[seq, g * 512:(g + 1) * 512, :] = fm[:, s * 512:(s + 1) * 512].T
    return outp
